# version 06
# baseline (speedup 1.0000x reference)
import math
import numpy as np
import ml_dtypes
from contextlib import ExitStack

import concourse.bass as bass
import concourse.mybir as mybir
from concourse.ap import AP
from concourse.bass_utils import run_bass_kernel_spmd

F32 = mybir.dt.float32
BF16 = mybir.dt.bfloat16
AF = mybir.ActivationFunctionType
ALU = mybir.AluOpType
AX = mybir.AxisListType
BF = ml_dtypes.bfloat16

N_CORES = 8
BATCH = 16
SEQ = 2048
D = 1024
NT = SEQ // 128
NQB = SEQ // 512
SEQ_PER_CORE = BATCH // N_CORES
NEG = -1.0e38

ENGINES = ("tensor", "vector", "scalar", "gpsimd", "sync")
DMA_QUEUES = ("sync", "gpsimd", "scalar")



class _Sem:
    __slots__ = ("handle", "count", "name")

    def __init__(self, name):
        self.handle = None
        self.count = 0
        self.name = name


class Tok:
    __slots__ = ("sem", "val", "engine")

    def __init__(self, sem, val, engine):
        self.sem = sem
        self.val = val
        self.engine = engine


class Region:
    __slots__ = ("name", "writer", "readers")

    def __init__(self, name):
        self.name = name
        self.writer = None
        self.readers = {}


class Prog:
    SEM_CAP = 55000

    def __init__(self):
        self.streams = {e: [] for e in ENGINES}
        self.sems = []
        self.cur = {}
        self.dma_sem = {}
        self.clock = {}
        self.n_inst = 0
        for e in ENGINES:
            self.cur[e] = self._new_sem("c_" + e)
        for q in DMA_QUEUES:
            self.dma_sem[q] = self._new_sem("d_" + q)

    def _new_sem(self, name):
        s = _Sem(f"{name}_{len(self.sems)}")
        self.sems.append(s)
        return s

    def _wait(self, engine, tok):
        if tok is None:
            return
        key = (engine, id(tok.sem))
        if self.clock.get(key, 0) >= tok.val:
            return
        self.clock[key] = tok.val
        sem, val = tok.sem, tok.val
        self.streams[engine].append(lambda eng, s=sem, v=val: eng.wait_ge(s.handle, v))

    def _deps(self, engine, reads, writes):
        for r in reads:
            self._wait(engine, r.writer)
        for w in writes:
            self._wait(engine, w.writer)
            for t in w.readers.values():
                self._wait(engine, t)

    def _finish(self, engine, tok, reads, writes):
        for w in writes:
            w.writer = tok
            w.readers = {}
        for r in reads:
            r.readers[(tok.engine, id(tok.sem))] = tok

    def op(self, engine, fns, reads=(), writes=()):
        if not isinstance(fns, (list, tuple)):
            fns = [fns]
        self._deps(engine, reads, writes)
        sem = self.cur[engine]
        if sem.count + 1 > self.SEM_CAP:
            sem = self.cur[engine] = self._new_sem("c_" + engine)
        sem.count += 1
        val = sem.count
        self.n_inst += len(fns)
        for f in fns[:-1]:
            self.streams[engine].append(f)
        last = fns[-1]
        self.streams[engine].append(
            lambda eng, f=last, s=sem, v=val: f(eng).then_inc(s.handle, 1))
        tok = Tok(sem, val, engine)
        self._finish(engine, tok, reads, writes)
        return tok

    def dma(self, queue, out_ap, in_ap, reads=(), writes=(), **kw):
        self._deps(queue, reads, writes)
        sem = self.dma_sem[queue]
        if sem.count + 16 > self.SEM_CAP:
            sem = self.dma_sem[queue] = self._new_sem("d_" + queue)
        sem.count += 16
        val = sem.count
        self.n_inst += 1
        self.streams[queue].append(
            lambda eng, o=out_ap, i=in_ap, s=sem, v=val, kw=kw:
                eng.dma_start(out=o, in_=i, **kw).then_inc(s.handle, 16))
        tok = Tok(sem, val, queue)
        self._finish(queue, tok, reads, writes)
        return tok

    def barrier_token(self, engine):
        sem = self.cur[engine]
        return Tok(sem, sem.count, engine) if sem.count else None

    def finalize(self, nc, es):
        for s in self.sems:
            if s.name.startswith("d_") and s.count:
                self.streams["sync"].append(
                    lambda eng, s=s, v=s.count: eng.wait_ge(s.handle, v))
        for s in self.sems:
            s.handle = es.enter_context(nc.semaphore(s.name))
        block = es.enter_context(nc.Block())
        for ename in ENGINES:
            items = self.streams[ename]
            getattr(block, ename)(lambda eng, items=items: [f(eng) for f in items])



def _bucket_causal(d):
    NUM_BUCKETS, MAX_DISTANCE = 32, 128
    n = max(d, 0)
    max_exact = NUM_BUCKETS // 2
    if n < max_exact:
        return n
    val = max_exact + int(
        math.log(max(n, 1) / max_exact) / math.log(MAX_DISTANCE / max_exact)
        * (NUM_BUCKETS - max_exact))
    return min(val, NUM_BUCKETS - 1)


DMIN = -511
NGARR = 2688
TBL_W = 2560


def _host_constants():
    c = {}
    c["c_ident"] = np.eye(128).astype(BF)
    c["c_ident32"] = np.eye(128, dtype=np.float32)
    c["c_J"] = np.eye(128)[::-1].copy().astype(BF)
    c["c_ones"] = np.ones((1, 512), dtype=np.float32).astype(BF)
    j = np.arange(NGARR)
    dval = j + DMIN - 127
    c["c_gcaus"] = np.where(dval >= 0, 0.0, NEG).astype(np.float32).astype(BF)
    oh = np.zeros((33, NGARR), dtype=np.float32)
    for jj in range(NGARR):
        dd = int(dval[jj])
        if dd >= 0:
            oh[_bucket_causal(dd), jj] = 0.125
        else:
            oh[32, jj] = NEG
    c["c_oh2"] = oh
    c["c_ones16"] = np.ones((1, 16), dtype=np.float32)
    HD = 32
    inv_freq = 1.0 / (10000.0 ** (np.arange(0, HD, 2, dtype=np.float64) / HD))
    ang = np.arange(SEQ, dtype=np.float64)[:, None] * inv_freq[None, :]
    c["c_cos"] = np.cos(ang).astype(np.float32)
    c["c_sin"] = np.sin(ang).astype(np.float32)
    b64 = np.zeros((128, 128), dtype=np.float32)
    for dc in range(8):
        for p in range(128):
            b64[p, dc * 16 + (128 * dc + p) // 64] = 1.0
    c["c_b64"] = b64
    hh = np.arange(16)[:, None] % 4
    tt = np.arange(SEQ)[None, :] % 4
    c["c_m16"] = (hh == tt).astype(np.float32).astype(BF)
    c["c_e64"] = (np.arange(1024)[None, :] // 64 == np.arange(16)[:, None]
                  ).astype(np.float32).astype(BF)
    return c


def _perm_latent():
    p = []
    for h in range(16):
        p.extend(32 * h + 2 * i for i in range(16))
    for h in range(16):
        p.extend(32 * h + 2 * i + 1 for i in range(16))
    return np.array(p)


def _prep_weights(inp):
    w = {}
    f32 = np.float32

    def T(a):
        return np.ascontiguousarray(np.asarray(a, f32).T)

    def C(a):
        return np.ascontiguousarray(np.asarray(a, f32))

    def row(a):
        return np.asarray(a, f32).reshape(1, -1).copy()

    for k in ("q", "k", "v", "o"):
        w[f"l0a_w{k}"] = T(inp[f"L0_attn_a_W{k}"])
        w[f"l0a_b{k}"] = row(inp[f"L0_attn_a_b{k}"])
    w["l0a_rel"] = C(inp["L0_attn_a_rel_emb"])
    for k in ("q", "k", "v", "o"):
        w[f"l0b_w{k}"] = C(inp[f"L0_attn_b_W{k}"])
        w[f"l0b_b{k}"] = row(inp[f"L0_attn_b_b{k}"])
    w["l0f_g1"] = row(inp["L0_ffn_g1"])
    w["l0f_g2"] = row(inp["L0_ffn_g2"])
    w["l0f_wdown"] = C(inp["L0_ffn_w_down"])
    def pad_lat(a):
        a = np.asarray(a, f32)
        out = np.zeros((512, 1024), f32)
        for h in range(16):
            out[:, 64 * h: 64 * h + 16] = a[:, 32 * h + 2 * np.arange(16)]
            out[:, 64 * h + 32: 64 * h + 48] = a[:, 32 * h + 2 * np.arange(16) + 1]
        return out
    w["l0f_wq"] = pad_lat(inp["L0_ffn_wq"])
    w["l0f_wk"] = pad_lat(inp["L0_ffn_wk"])
    w["l0f_wv"] = C(inp["L0_ffn_wv"])
    w["l0f_wup"] = C(inp["L0_ffn_w_up"])
    w["l0f_router"] = C(inp["L0_ffn_router_w"])
    w["l0f_w1"] = C(inp["L0_ffn_w1"])
    w["l0f_w3"] = C(inp["L0_ffn_w3"])
    w["l0f_w2"] = C(inp["L0_ffn_w2"])
    w["l1a_gamma"] = row(inp["L1_attn_a_gamma"])
    w["l1a_beta"] = row(inp["L1_attn_a_beta"])
    w["l1a_wqkv"] = C(inp["L1_attn_a_w_qkv"])
    w["l1a_wout"] = C(inp["L1_attn_a_w_out"])
    w["l1a_bout"] = row(inp["L1_attn_a_b_out"])
    for k in ("q", "k", "v", "o"):
        w[f"l1b_w{k}"] = T(inp[f"L1_attn_b_w{k}_w"])
        w[f"l1b_b{k}"] = row(inp[f"L1_attn_b_w{k}_b"])
    w["l1f_router"] = C(inp["L1_ffn_router_w"])
    w["l1f_wg_r"] = C(inp["L1_ffn_wg_r"])
    w["l1f_wu_r"] = C(inp["L1_ffn_wu_r"])
    w["l1f_wd_r"] = C(inp["L1_ffn_wd_r"])
    w["l1f_wg_s"] = C(inp["L1_ffn_wg_s"])
    w["l1f_wu_s"] = C(inp["L1_ffn_wu_s"])
    w["l1f_wd_s"] = C(inp["L1_ffn_wd_s"])
    return w



BIG_WEIGHTS = {
    "l0a_wq": (1024, 1024), "l0a_wk": (1024, 1024), "l0a_wv": (1024, 1024),
    "l0a_wo": (1024, 1024),
    "l0b_wq": (1024, 1024), "l0b_wk": (1024, 1024), "l0b_wv": (1024, 1024),
    "l0b_wo": (1024, 1024),
    "l0f_wdown": (1024, 512), "l0f_wq": (512, 1024), "l0f_wk": (512, 1024),
    "l0f_wv": (512, 512), "l0f_wup": (512, 1024), "l0f_router": (1024, 8),
    "l0f_w1": (8, 1024, 2048), "l0f_w3": (8, 1024, 2048), "l0f_w2": (8, 2048, 1024),
    "l1a_wqkv": (1024, 3072), "l1a_wout": (1024, 1024),
    "l1b_wq": (1024, 1024), "l1b_wk": (1024, 1024), "l1b_wv": (1024, 1024),
    "l1b_wo": (1024, 1024),
    "l1f_router": (1024, 16),
    "l1f_wg_r": (16, 1024, 256), "l1f_wu_r": (16, 1024, 256),
    "l1f_wd_r": (16, 256, 1024),
    "l1f_wg_s": (2, 1024, 512), "l1f_wu_s": (2, 1024, 512),
    "l1f_wd_s": (2, 512, 1024),
}
ROW_WEIGHTS = {
    "l0a_bq": 1024, "l0a_bk": 1024, "l0a_bv": 1024, "l0a_bo": 1024,
    "l0b_bq": 1024, "l0b_bk": 1024, "l0b_bv": 1024, "l0b_bo": 1024,
    "l0f_g1": 1024, "l0f_g2": 1024,
    "l1a_gamma": 1024, "l1a_beta": 1024, "l1a_bout": 1024,
    "l1b_bq": 1024, "l1b_bk": 1024, "l1b_bv": 1024, "l1b_bo": 1024,
}
PART_WEIGHTS = {
    "l0a": ["l0a_wq", "l0a_wk", "l0a_wv", "l0a_wo", "l0a_bq", "l0a_bk",
            "l0a_bv", "l0a_bo", "l0a_rel"],
    "l0f": ["l0f_g1", "l0f_g2", "l0f_wdown", "l0f_wq", "l0f_wk", "l0f_wv",
            "l0f_wup", "l0f_router", "l0f_w1", "l0f_w3", "l0f_w2"],
    "l0b": ["l0b_wq", "l0b_wk", "l0b_wv", "l0b_wo", "l0b_bq", "l0b_bk",
            "l0b_bv", "l0b_bo"],
    "l1a": ["l1a_gamma", "l1a_beta", "l1a_wqkv", "l1a_wout", "l1a_bout"],
    "l1b": ["l1b_wq", "l1b_wk", "l1b_wv", "l1b_wo", "l1b_bq", "l1b_bk",
            "l1b_bv", "l1b_bo"],
    "l1f": ["l1f_router", "l1f_wg_r", "l1f_wu_r", "l1f_wd_r", "l1f_wg_s",
            "l1f_wu_s", "l1f_wd_s"],
}
CONST_NAMES = ["c_ident", "c_ident32", "c_J", "c_ones", "c_gcaus", "c_oh2",
               "c_ones16", "c_cos", "c_sin", "c_m16", "c_e64", "c_b64"]
CONST_DT = {"c_oh2": F32, "c_cos": F32, "c_sin": F32, "c_ones16": F32,
            "c_ident32": F32, "c_b64": F32}


class Builder:
    def __init__(self, parts):
        self.parts = list(parts)
        self.p = Prog()
        self.nc = bass.Bass("TRN2", target_bir_lowering=False)
        self.es = ExitStack()
        nc = self.nc
        self.x_in = nc.dram_tensor("x_in", [SEQ_PER_CORE, SEQ, D], F32,
                                   kind="ExternalInput").ap()
        self.x_out = nc.dram_tensor("x_out", [SEQ_PER_CORE, SEQ, D], F32,
                                    kind="ExternalOutput").ap()
        self.dram = {}
        need = set()
        for prt in self.parts:
            need.update(PART_WEIGHTS[prt])
        hc = _host_constants()
        self.need_consts = [n for n in CONST_NAMES]
        for n in self.need_consts:
            arr = hc[n]
            dt = CONST_DT.get(n, BF16)
            self.dram[n] = nc.dram_tensor(n, list(arr.shape), dt,
                                          kind="ExternalInput").ap()
        self.need_weights = sorted(need)
        for n in self.need_weights:
            if n in BIG_WEIGHTS:
                self.dram[n] = nc.dram_tensor(n, list(BIG_WEIGHTS[n]), F32,
                                              kind="ExternalInput").ap()
                self.dram[n + "_bf"] = nc.dram_tensor(
                    n + "_bf", list(BIG_WEIGHTS[n]), BF16, kind="Internal").ap()
            elif n in ROW_WEIGHTS:
                self.dram[n] = nc.dram_tensor(n, [1, ROW_WEIGHTS[n]], F32,
                                              kind="ExternalInput").ap()
            elif n == "l0a_rel":
                self.dram[n] = nc.dram_tensor(n, [32, 16], F32,
                                              kind="ExternalInput").ap()
        self.dram["vs"] = nc.dram_tensor("vs", [16, SEQ, 65], BF16,
                                         kind="Internal").ap()
        self.dram["garr"] = nc.dram_tensor("garr", [16, NGARR], BF16,
                                           kind="Internal").ap()
        es, s = self.es, self
        def sb(name, shape, dt):
            t = es.enter_context(nc.sbuf_tensor(name, shape, dt))
            return t, Region(name)
        s.X = sb("X", [128, NT * 1024], F32)
        s.TA = sb("TA", [128, NT * 1024], BF16)
        s.TB = sb("TB", [128, NT * 1024], BF16)
        s.TC = sb("TC", [128, NT * 1024], BF16)
        s.TD = sb("TD", [128, 8 * 1024], BF16)
        s.TDr = [Region(f"TD{i}") for i in range(16)]
        s.WS = [sb(f"WS{i}", [128, 1152], BF16) for i in range(4)]
        s.SCR32 = sb("SCR32", [128, 1024], F32)
        s.SCRB = sb("SCRB", [128, 2560], BF16)
        s.MROW = sb("MROW", [1, 2048], BF16)
        s.IDE = sb("IDE", [128, 128], BF16)
        s.JX = sb("JX", [128, 128], BF16)
        s.ONE = sb("ONE", [1, 512], BF16)
        s.SUMS = sb("SUMS", [128, 16], F32)
        s.MEAN = sb("MEAN", [128, 16], F32)
        s.RSTD = sb("RSTD", [128, 16], F32)
        s.MM = sb("MM", [128, 16], F32)
        s.MMB = sb("MMB", [128, 16], BF16)
        s.MSK = sb("MSK", [128, 16], F32)
        s.GTS = sb("GTS", [128, 16 * 16], F32)
        s.SMALL = sb("SMALL", [128, 64], F32)
        s.B64 = sb("B64", [128, 128], F32)
        s.E64 = sb("E64", [16, 1024], BF16)
        s.CONSTV = sb("CONSTV", [128, 16], F32)
        s._constv_map = {}
        s.BROW = sb("BROW", [1, 2048], BF16)
        s.PP = es.enter_context(nc.psum_tensor("PP", [128, 2048], F32))
        s.PQ = es.enter_context(nc.psum_tensor("PQ", [128, 2048], F32))
        s.pr = [Region(f"PP{i}") for i in range(4)] + \
               [Region(f"PQ{i}") for i in range(4)]
        self._psum_rr = 0

    def ps(self, idx):
        t = self.PP if idx < 4 else self.PQ
        off = (idx % 4) * 512
        return t[:, off:off + 512], self.pr[idx]

    def ps_rr(self, n=1, lo=0, hi=8):
        out = []
        for _ in range(n):
            out.append(lo + self._psum_rr % (hi - lo))
            self._psum_rr += 1
        return out if n > 1 else out[0]

    def td_regs(self, c0, c1):
        return self.TDr[c0 // 512: (c1 + 511) // 512]

    def wdma(self, slot_idx, out_ap, in_ap, reads=(), writes=()):
        q = "sync" if slot_idx % 2 == 0 else "gpsimd"
        return self.p.dma(q, out_ap, in_ap, reads=reads, writes=writes)

    def constv(self, val):
        if val in self._constv_map:
            return self.CONSTV[0][:, self._constv_map[val]:self._constv_map[val] + 1]
        idx = len(self._constv_map)
        assert idx < 16
        self._constv_map[val] = idx
        ap = self.CONSTV[0][:, idx:idx + 1]
        self.vop(lambda eng, o=ap, v=val: eng.memset(o, v),
                 reads=[self.CONSTV[1]], writes=[self.CONSTV[1]])
        return ap

    def mm_group(self, items, reads, writes):
        fns = [lambda eng, it=it: eng.matmul(it[0], it[1], it[2],
                                             start=it[3], stop=it[4])
               for it in items]
        return self.p.op("tensor", fns, reads=reads, writes=writes)

    def transpose_group(self, items, reads, writes):
        idet, idereg = self.IDE
        ide = idet[:, :]
        fns = [lambda eng, it=it: eng.transpose(it[0], it[1], ide)
               for it in items]
        return self.p.op("tensor", fns, reads=list(reads) + [idereg],
                         writes=writes)

    def act(self, out_ap, in_ap, func, reads, writes, bias=0.0, scale=1.0,
            accum_out=None):
        return self.p.op(
            "scalar",
            lambda eng: eng.activation(out_ap, in_ap, func, bias=bias,
                                       scale=scale, accum_out=accum_out),
            reads=reads, writes=writes)

    def vop(self, fn, reads, writes):
        return self.p.op("vector", fn, reads=reads, writes=writes)

    def convert_weights(self):
        p, nc = self.p, self.nc
        CH = 2048
        stage32 = self.X
        stagebf = self.TA
        nslots = 4
        for name in self.need_weights:
            if name not in BIG_WEIGHTS:
                continue
            shp = BIG_WEIGHTS[name]
            n = int(np.prod(shp))
            src = self.dram[name]
            dst = self.dram[name + "_bf"]
            flat_src = AP(src.tensor, 0, [[n // 128, 128], [1, n // 128]]) \
                if n % 128 == 0 else None
            assert flat_src is not None and (n // 128) % CH == 0 or True
            per_row = n // 128
            flat_dst = AP(dst.tensor, 0, [[per_row, 128], [1, per_row]])
            pos = 0
            while pos < per_row:
                w = min(CH, per_row - pos)
                slot = (pos // CH) % nslots
                s32 = stage32[0][:, slot * CH: slot * CH + w]
                sbf = stagebf[0][:, slot * CH: slot * CH + w]
                src_ap = AP(src.tensor, 0 * 1, [[per_row, 128], [1, per_row]])
                src_ap = AP(src.tensor, pos, [[per_row, 128], [1, w]])
                dst_ap = AP(dst.tensor, pos, [[per_row, 128], [1, w]])
                p.dma("sync", s32, src_ap, reads=[], writes=[self.X[1]])
                self.vop(lambda eng, o=sbf, i=s32: eng.tensor_copy(o, i),
                         reads=[self.X[1]], writes=[self.TA[1]])
                p.dma("gpsimd", dst_ap, sbf, reads=[self.TA[1]], writes=[])
                pos += w

    def load_x(self, s):
        for i in range(NT):
            self.p.dma("sync", self.X[0][:, i * 1024:(i + 1) * 1024],
                       self.x_in[s, i * 128:(i + 1) * 128, :],
                       reads=[], writes=[self.X[1]])

    def store_x(self, s):
        for i in range(NT):
            self.p.dma("gpsimd", self.x_out[s, i * 128:(i + 1) * 128, :],
                       self.X[0][:, i * 1024:(i + 1) * 1024],
                       reads=[self.X[1]], writes=[])

    def load_cm_pair(self, name_g, name_b):
        out = []
        for j, nm in enumerate((name_g, name_b)):
            dst = self.SMALL[0][:, 16 + j * 8: 16 + (j + 1) * 8]
            src = self.dram[nm].tensor
            for c in range(8):
                self.p.dma("sync", dst[:, c:c + 1],
                           AP(src, c * 128, [[1, 128], [1, 1]]),
                           reads=[self.SMALL[1]], writes=[self.SMALL[1]])
            out.append(dst)
        return out[0], out[1]

    def emit_ln(self, dst, dst_reg, want_msk=False, inner_cm=None):
        X, rX = self.X[0], self.X[1]
        if hasattr(self, "_dbg"):
            self._dbg("X0", X[:, :], rX)
        dump, rdump = self.SCR32[0], self.SCR32[1]
        xnb = self.SCRB[0][:, 0:1024]
        rxnb = self.SCRB[1]
        SM, rSM = self.SMALL[0], self.SMALL[1]
        for i in range(NT):
            xi = X[:, i * 1024:(i + 1) * 1024]
            self.vop(lambda eng, o=self.SUMS[0][:, i:i + 1], a=xi:
                     eng.reduce_sum(o, a, axis=AX.X),
                     reads=[rX], writes=[self.SUMS[1]])
            self.act(dump[:, 0:1024], xi, AF.Square,
                     reads=[rX], writes=[rdump, rSM],
                     accum_out=SM[:, 0:1])
            mean = self.MEAN[0][:, i:i + 1]
            rstd = self.RSTD[0][:, i:i + 1]
            t0, t1 = SM[:, 1:2], SM[:, 2:3]
            self.vop(lambda eng, o=mean, a=self.SUMS[0][:, i:i + 1]:
                     eng.tensor_scalar_mul(o, a, 1.0 / D),
                     reads=[self.SUMS[1]], writes=[self.MEAN[1]])
            self.vop(lambda eng, o=t0, a=SM[:, 0:1]:
                     eng.tensor_scalar_mul(o, a, 1.0 / D),
                     reads=[rSM], writes=[rSM])
            self.vop(lambda eng, o=t1, m=mean: eng.tensor_mul(o, m, m),
                     reads=[self.MEAN[1]], writes=[rSM])
            self.vop(lambda eng, o=t0, a=t0, b=t1: eng.tensor_sub(o, a, b),
                     reads=[rSM], writes=[rSM])
            self.act(t1, t0, AF.Sqrt, bias=self.constv(1e-5),
                     reads=[rSM, self.CONSTV[1]], writes=[rSM])
            self.vop(lambda eng, o=rstd, a=t1: eng.reciprocal(o, a),
                     reads=[rSM], writes=[self.RSTD[1]])
            if want_msk:
                z = SM[:, 3:4]
                self.vop(lambda eng, o=z, a=X[:, i * 1024:i * 1024 + 1],
                         m=mean, r=rstd:
                         eng.tensor_scalar(o, a, m, r, ALU.subtract, ALU.mult),
                         reads=[rX, self.MEAN[1], self.RSTD[1]], writes=[rSM])
                self.vop(lambda eng, o=self.MSK[0][:, i:i + 1], a=z:
                         eng.tensor_scalar(o, a, 0.0, NEG, ALU.is_equal,
                                           ALU.mult),
                         reads=[rSM], writes=[self.MSK[1]])
            if inner_cm is None:
                self.vop(lambda eng, o=xnb, a=xi, m=mean, r=rstd:
                         eng.tensor_scalar(o, a, m, r, ALU.subtract, ALU.mult),
                         reads=[rX, self.MEAN[1], self.RSTD[1]],
                         writes=[rxnb])
            else:
                xf = dump[:, 0:1024]
                self.vop(lambda eng, o=xf, a=xi, m=mean, r=rstd:
                         eng.tensor_scalar(o, a, m, r, ALU.subtract, ALU.mult),
                         reads=[rX, self.MEAN[1], self.RSTD[1]],
                         writes=[rdump])
                s2, q2 = SM[:, 4:5], SM[:, 5:6]
                m2, r2, t2 = SM[:, 6:7], SM[:, 7:8], SM[:, 8:9]
                self.vop(lambda eng, o=s2, a=xf:
                         eng.reduce_sum(o, a, axis=AX.X),
                         reads=[rdump], writes=[rSM])
                self.act(self.SCRB[0][:, 1024:2048], xf, AF.Square,
                         reads=[rdump], writes=[rxnb, rSM], accum_out=q2)
                self.vop(lambda eng, o=m2, a=s2:
                         eng.tensor_scalar_mul(o, a, 1.0 / D),
                         reads=[rSM], writes=[rSM])
                self.vop(lambda eng, o=q2, a=q2:
                         eng.tensor_scalar_mul(o, a, 1.0 / D),
                         reads=[rSM], writes=[rSM])
                self.vop(lambda eng, o=t2, m=m2: eng.tensor_mul(o, m, m),
                         reads=[rSM], writes=[rSM])
                self.vop(lambda eng, o=q2, a=q2, b=t2: eng.tensor_sub(o, a, b),
                         reads=[rSM], writes=[rSM])
                self.act(t2, q2, AF.Sqrt, bias=self.constv(1e-5),
                         reads=[rSM, self.CONSTV[1]], writes=[rSM])
                self.vop(lambda eng, o=r2, a=t2: eng.reciprocal(o, a),
                         reads=[rSM], writes=[rSM])
                self.vop(lambda eng, o=xnb, a=xf, m=m2, r=r2:
                         eng.tensor_scalar(o, a, m, r, ALU.subtract, ALU.mult),
                         reads=[rdump, rSM], writes=[rxnb])
            pap, preg = self.ps(self.ps_rr())
            pbf = pap.bitcast(BF16)
            items = [(pbf[:, c * 128:(c + 1) * 128],
                      xnb[:, c * 128:(c + 1) * 128]) for c in range(8)]
            self.transpose_group(items, reads=[rxnb], writes=[preg])
            if inner_cm is None:
                self.vop(lambda eng, o=_chunks_out(dst, i), a=pbf[:, 0:1024]:
                         eng.tensor_copy(o, a),
                         reads=[preg], writes=[dst_reg])
            else:
                gcm, bcm = inner_cm
                for c in range(8):
                    self.vop(lambda eng,
                             o=dst[:, c * 2048 + i * 128: c * 2048 + (i + 1) * 128],
                             a=pbf[:, c * 128:(c + 1) * 128],
                             g=gcm[:, c:c + 1], b=bcm[:, c:c + 1]:
                             eng.tensor_scalar(o, a, g, b, ALU.mult, ALU.add),
                             reads=[preg, rSM], writes=[dst_reg])


def _chunks_out(dst_handle, i):
    a = dst_handle[:, :]
    return AP(a.tensor, i * 128, [[a.ap[0][0], 128], [2048, 8], [1, 128]])




class AttnCfg:
    def __init__(self, name, wq, wk, wv, wo, wcols=1024, bq=None, bk=None,
                 bv=None, bo=None, scale=1.0, causal=False, tbl=None,
                 colmask=False, qcol0=0, kcol0=0, vcol0=0):
        self.name = name
        self.wq, self.wk, self.wv, self.wo = wq, wk, wv, wo
        self.wcols = wcols
        self.bq, self.bk, self.bv, self.bo = bq, bk, bv, bo
        self.scale = scale
        self.causal = causal
        self.tbl = tbl
        self.colmask = colmask
        self.qcol0, self.kcol0, self.vcol0 = qcol0, kcol0, vcol0


def _mk_group(items):
    out = []
    for i, (o, l, r) in enumerate(items):
        out.append((o, l, r, i == 0, i == len(items) - 1))
    return out


class _AttnMixin:
    def brow_reset(self):
        self._brow_pos = 0

    def load_row_bf(self, dram_name, n, slot=None):
        pos = self._brow_pos if slot is None else slot * 1024
        if slot is None:
            assert pos + n <= 2048, "BROW overflow"
            self._brow_pos = pos + n
        r32 = self.SCR32[0][0:1, 0:n]
        rb = self.BROW[0][0:1, pos:pos + n]
        self.p.dma("sync", r32, self.dram[dram_name][0:1, 0:n],
                   reads=[self.SCR32[1]], writes=[self.SCR32[1]])
        self.vop(lambda eng, o=rb, i=r32: eng.tensor_copy(o, i),
                 reads=[self.SCR32[1]], writes=[self.BROW[1]])
        return rb, self.BROW[1]

    def emit_qT_proj(self, dst, w_name, col0, bias, wcols, src=None,
                     ndout=8, nkc=8, dcol0=0):
        dstT, rdst = dst
        TA, rTA = src if src is not None else self.TA
        wbf = self.dram[w_name + "_bf"].tensor
        for dc in range(ndout):
            slot = self.WS[dc % 4]
            self.wdma(dc, slot[0][:, 0:nkc * 128],
                      AP(wbf, col0 + dc * 128,
                         [[wcols, 128], [128 * wcols, nkc], [1, 128]]),
                      reads=[slot[1]], writes=[slot[1]])
            for nt4 in range(4):
                pap, preg = self.ps(self.ps_rr())
                items = [(pap, slot[0][:, kc * 128:(kc + 1) * 128],
                          TA[:, kc * 2048 + nt4 * 512: kc * 2048 + nt4 * 512 + 512])
                         for kc in range(nkc)]
                reads = [slot[1], rTA]
                if bias is not None:
                    items.append((pap, bias[0][0:1, dc * 128:dc * 128 + 128],
                                  self.ONE[0][0:1, 0:512]))
                    reads += [bias[1], self.ONE[1]]
                self.mm_group(_mk_group(items), reads=reads, writes=[preg])
                self.act(dstT[:, dcol0 + dc * 2048 + nt4 * 512:
                              dcol0 + dc * 2048 + nt4 * 512 + 512],
                         pap, AF.Copy, reads=[preg], writes=[rdst])

    def emit_v_to_vs(self, cfg, src=None, hd=64, vtot=1024, nkc=8):
        TA, rTA = src if src is not None else self.TA
        wbf = self.dram[cfg.wv + "_bf"].tensor
        wcols = cfg.wcols
        bias = self._battn.get("v")
        nvs = vtot // 512
        hd1 = hd + 1
        nh_per_vs = 512 // hd
        for ntg in range(4):
            psl = [[self.ps_rr() for _ in range(4)] for _ in range(nvs)]
            for vs in range(nvs):
                for kc2 in range(nkc // 2):
                    slot = self.WS[kc2 % 4]
                    self.wdma(kc2, slot[0][:, 0:1024],
                              AP(wbf, cfg.vcol0 + vs * 512 + kc2 * 256 * wcols,
                                 [[wcols, 128], [128 * wcols, 2], [1, 512]]),
                              reads=[slot[1]], writes=[slot[1]])
                    for kh in range(2):
                        kc = kc2 * 2 + kh
                        for i in range(4):
                            nt = ntg * 4 + i
                            pap, preg = self.ps(psl[vs][i])
                            its = [(pap, TA[:, kc * 2048 + nt * 128:
                                            kc * 2048 + nt * 128 + 128],
                                    slot[0][:, kh * 512:kh * 512 + 512],
                                    kc == 0,
                                    kc == nkc - 1 and bias is None)]
                            self.mm_group(its, reads=[slot[1], rTA],
                                          writes=[preg])
                if bias is not None:
                    for i in range(4):
                        pap, preg = self.ps(psl[vs][i])
                        its = [(pap, self.ONE[0][0:1, 0:128],
                                bias[0][0:1, vs * 512:vs * 512 + 512],
                                False, True)]
                        self.mm_group(its, reads=[self.ONE[1], bias[1]],
                                      writes=[preg])
            stw = 16 * hd1
            for i in range(4):
                nt = ntg * 4 + i
                st = self.SCRB[0][:, (i % 2) * stw:(i % 2) * stw + stw]
                rst = self.SCRB[1]
                part = st.ap[0][0]
                ones_ap = AP(st.tensor, st.offset, [[part, 128], [hd1, 16], [1, 1]])
                self.vop(lambda eng, o=ones_ap: eng.memset(o, 1.0),
                         reads=[rst], writes=[rst])
                for vs in range(nvs):
                    pap, preg = self.ps(psl[vs][i])
                    out_ap = AP(st.tensor, st.offset + hd1 * nh_per_vs * vs + 1,
                                [[part, 128], [hd1, nh_per_vs], [1, hd]])
                    self.act(out_ap, pap, AF.Copy, reads=[preg], writes=[rst])
                vs_t = self.dram["vs"].tensor
                dst_ap = AP(vs_t, nt * 128 * hd1,
                            [[hd1, 128], [SEQ * hd1, 16], [1, hd1]])
                self.p.dma("gpsimd", dst_ap, st, reads=[rst],
                           writes=[self.rVS])

    def emit_full_attn(self, cfg, inner_ln=False):
        p = self.p
        self.rO = self.TA[1]
        self.brow_reset()
        self._battn = {}
        if cfg.bq is not None:
            self._battn["q"] = self.load_row_bf(cfg.bq, 1024, slot=0)
        if cfg.bk is not None:
            self._battn["k"] = self.load_row_bf(cfg.bk, 1024, slot=1)
        if inner_ln:
            gcm, bcm = self.load_cm_pair(cfg.inner_gamma, cfg.inner_beta)
            self.emit_ln(self.TA[0], self.TA[1], inner_cm=(gcm, bcm))
        else:
            self.emit_ln(self.TA[0], self.TA[1], want_msk=cfg.colmask)
        self._dbg("xnT", self.TA[0][:, :], self.TA[1])
        self._dbg("MEAN", self.MEAN[0][:, :], self.MEAN[1])
        self._dbg("RSTD", self.RSTD[0][:, :], self.RSTD[1])
        self.emit_qT_proj(self.TB, cfg.wq, cfg.qcol0, self._battn.get("q"),
                          cfg.wcols)
        self.emit_qT_proj(self.TC, cfg.wk, cfg.kcol0, self._battn.get("k"),
                          cfg.wcols)
        self._dbg("qT", self.TB[0][:, :], self.TB[1])
        self._dbg("kT", self.TC[0][:, :], self.TC[1])
        if cfg.bv is not None:
            self._battn["v"] = self.load_row_bf(cfg.bv, 1024, slot=0)
        self.emit_v_to_vs(cfg)
        self.emit_attn_core(cfg, (self.TB, self.TB[1]), (self.TC, self.TC[1]),
                            self.TA[0], self.rO, 0, 1024, 64)
        if cfg.bo is not None:
            self._battn["o"] = self.load_row_bf(cfg.bo, 1024, slot=1)
        self.emit_out_proj(cfg.wo, self._battn.get("o"), self.TA, self.rO, 1024)

    def emit_out_proj(self, wo_name, bias, src, rsrc, src_d, src_is_T=False):
        wbf = self.dram[wo_name + "_bf"].tensor
        ndc = src_d // 128
        srcT, _ = src
        TD = self.TD[0]
        for qtg in range(4):
            if not src_is_T:
                for i in range(4):
                    qt = qtg * 4 + i
                    pap, preg = self.ps(self.ps_rr())
                    pbf = pap.bitcast(BF16)
                    items = [(pbf[:, c * 128:(c + 1) * 128],
                              srcT[:, qt * src_d + c * 128: qt * src_d + (c + 1) * 128])
                             for c in range(ndc)]
                    self.transpose_group(items, reads=[rsrc], writes=[preg])
                    self.vop(lambda eng, o=TD[:, i * 2048:i * 2048 + ndc * 128],
                             a=pbf[:, 0:ndc * 128]:
                             eng.tensor_copy(o, a),
                             reads=[preg],
                             writes=self.td_regs(i * 2048,
                                                 i * 2048 + ndc * 128))

            def lhs_tile(i, dc):
                if src_is_T:
                    qt = qtg * 4 + i
                    return srcT[:, dc * 2048 + qt * 128: dc * 2048 + qt * 128 + 128]
                return TD[:, i * 2048 + dc * 128: i * 2048 + (dc + 1) * 128]
            pass
            for ds in range(2):
                for dc2 in range((ndc + 1) // 2):
                    slot = self.WS[dc2 % 4]
                    nld = min(2, ndc - dc2 * 2)
                    self.wdma(dc2, slot[0][:, 0:512 * nld],
                              AP(wbf, ds * 512 + dc2 * 256 * 1024,
                                 [[1024, 128], [128 * 1024, nld], [1, 512]]),
                              reads=[slot[1]], writes=[slot[1]])
                    for dh in range(nld):
                        dc = dc2 * 2 + dh
                        for i in range(4):
                            pap, preg = self.ps((4 + i) if ds == 0 else i)
                            lrg = [rsrc] if src_is_T else list(self.td_regs(
                                i * 2048 + dc * 128, i * 2048 + (dc + 1) * 128))
                            its = [(pap, lhs_tile(i, dc),
                                    slot[0][:, dh * 512:dh * 512 + 512],
                                    dc == 0,
                                    dc == ndc - 1 and bias is None)]
                            self.mm_group(its, reads=[slot[1]] + lrg,
                                          writes=[preg])
                for i in range(4):
                    pap, preg = self.ps((4 + i) if ds == 0 else i)
                    if bias is not None:
                        its = [(pap, self.ONE[0][0:1, 0:128],
                                bias[0][0:1, ds * 512:ds * 512 + 512],
                                False, True)]
                        self.mm_group(its, reads=[self.ONE[1], bias[1]],
                                      writes=[preg])
                    qt = qtg * 4 + i
                    xs = self.X[0][:, qt * 1024 + ds * 512: qt * 1024 + ds * 512 + 512]
                    self.vop(lambda eng, o=xs, a=pap, b=xs:
                             eng.tensor_add(o, a, b),
                             reads=[preg, self.X[1]], writes=[self.X[1]])

    def emit_attn_core(self, cfg, qsrc, ksrc, obuf, oreg, obase, ostride, ohd):
        TBt, rTB = qsrc[0][0], qsrc[1]
        TCt, rTC = ksrc[0][0], ksrc[1]
        TB, TC = TBt, TCt
        TD = self.TD[0]
        hd1 = ohd + 1
        gt = self.SCRB[0][:, 0:TBL_W]
        rgt = self.SCRB[1]
        vh = self.WS[0]
        for h in range(16):
            hb = 64 * (h % 2)
            hc = (h // 2) * 2048
            if cfg.tbl == "garr":
                self.p.dma("sync", gt,
                           AP(self.dram["garr"].tensor, h * NGARR,
                              [[1, 128], [1, TBL_W]]),
                           reads=[rgt], writes=[rgt])
            elif cfg.tbl == "gcaus" and h == 0:
                self.p.dma("sync", gt,
                           AP(self.dram["c_gcaus"].tensor, 0,
                              [[1, 128], [1, TBL_W]]),
                           reads=[rgt], writes=[rgt])
            self.p.dma("gpsimd", vh[0][:, 0:16 * hd1],
                       AP(self.dram["vs"].tensor, h * SEQ * hd1,
                          [[hd1, 128], [hd1 * 128, 16], [1, hd1]]),
                       reads=[vh[1], self.rVS], writes=[vh[1]])
            for qt in range(16):
                nks = (qt // 4 + 1) if cfg.causal else 4
                pt = self.PP if qt % 2 == 0 else self.PQ
                prs = self.pr[0:4] if qt % 2 == 0 else self.pr[4:8]
                items = []
                for ks in range(nks):
                    items.append((pt[:, ks * 512:(ks + 1) * 512],
                                  TB[hb:hb + 64, hc + qt * 128: hc + qt * 128 + 128],
                                  TC[hb:hb + 64, hc + ks * 512: hc + ks * 512 + 512],
                                  True, True))
                self.mm_group(items, reads=[rTB, rTC], writes=prs[:nks])
                self.vop(lambda eng, o=self.MM[0][:, qt:qt + 1],
                         a=pt[:, 0:512 * nks]:
                         eng.reduce_max(o, a, axis=AX.X, negate=True),
                         reads=prs[:nks], writes=[self.MM[1]])
            self.vop(lambda eng, o=self.MMB[0][:, :], a=self.MM[0][:, :]:
                     eng.tensor_copy(o, a),
                     reads=[self.MM[1]], writes=[self.MMB[1]])
            for qb in range(4):
                pap, preg = self.ps(self.ps_rr())
                pbf = pap.bitcast(BF16)
                items = [(pbf[0:1, i * 128:(i + 1) * 128],
                          self.MMB[0][:, qb * 4 + i: qb * 4 + i + 1])
                         for i in range(4)]
                self.transpose_group(items, reads=[self.MMB[1]], writes=[preg])
                self.vop(lambda eng, o=self.MROW[0][0:1, qb * 512:(qb + 1) * 512],
                         a=pbf[0:1, 0:512]:
                         eng.tensor_copy(o, a),
                         reads=[preg], writes=[self.MROW[1]])
            for qb in range(4):
                kts = [kt for kt in range(16)
                       if not cfg.causal or kt * 128 <= qb * 512 + 511]
                avp = [self.ps(i) for i in range(4)]
                nk = len(kts)

                def emit_pb(ki):
                    kt = kts[ki]
                    pap, preg = self.ps(self.ps_rr(lo=4, hi=8))
                    items = [(pap,
                              TC[hb:hb + 64, hc + kt * 128: hc + kt * 128 + 128],
                              TB[hb:hb + 64, hc + qb * 512: hc + qb * 512 + 512]),
                             (pap, self.ONE[0][0:1, 0:128],
                              self.MROW[0][0:1, qb * 512:(qb + 1) * 512])]
                    reads = [rTB, rTC, self.ONE[1], self.MROW[1]]
                    if cfg.tbl is not None:
                        c = qb * 512 - kt * 128 - DMIN
                        items.append((pap, self.JX[0][:, :], gt[:, c:c + 512]))
                        reads += [self.JX[1], rgt]
                    self.mm_group(_mk_group([(o, l, r) for (o, l, r) in items]),
                                  reads=reads, writes=[preg])
                    bias_ap = self.MSK[0][:, kt:kt + 1] if cfg.colmask else 0.0
                    rd = [preg, self.CONSTV[1]] + \
                        ([self.MSK[1]] if cfg.colmask else [])
                    sc = 1.0 if cfg.scale == 1.0 else self.constv(cfg.scale)
                    self.act(TD[:, kt * 512:(kt + 1) * 512], pap, AF.Exp,
                             bias=bias_ap, scale=sc,
                             reads=rd, writes=[self.TDr[kt]])

                def emit_av(ki):
                    kt = kts[ki]
                    for i in range(4):
                        self.mm_group(
                            [(avp[i][0][:, 0:hd1],
                              TD[:, kt * 512 + i * 128: kt * 512 + i * 128 + 128],
                              vh[0][:, kt * hd1:(kt + 1) * hd1],
                              ki == 0, ki == nk - 1)],
                            reads=[self.TDr[kt], vh[1]], writes=[avp[i][1]])
                LAG = 2
                for ki in range(nk + LAG):
                    if ki < nk:
                        emit_pb(ki)
                    if ki >= LAG:
                        emit_av(ki - LAG)
                for i in range(4):
                    qt = qb * 4 + i
                    pap, preg = avp[i]
                    rr = self.SMALL[0][:, 9:10]
                    self.vop(lambda eng, o=rr, a=pap[:, 0:1]:
                             eng.reciprocal(o, a),
                             reads=[preg], writes=[self.SMALL[1]])
                    self.vop(lambda eng,
                             o=obuf[:, obase + qt * ostride + ohd * h:
                                    obase + qt * ostride + ohd * h + ohd],
                             a=pap[:, 1:hd1], s=rr:
                             eng.tensor_scalar_mul(o, a, s),
                             reads=[preg, self.SMALL[1]], writes=[oreg])

    def emit_router_gates(self, src, rsrc, w_name, E, plus_eps):
        wslot = self.WS[0]
        wbf = self.dram[w_name + "_bf"].tensor
        self.p.dma("sync", wslot[0][:, 0:8 * E],
                   AP(wbf, 0, [[E, 128], [128 * E, 8], [1, E]]),
                   reads=[wslot[1]], writes=[wslot[1]])
        SM, rSM = self.SMALL[0], self.SMALL[1]
        probs = self.SCR32[0][:, 0:16]
        rprob = self.SCR32[1]
        for nt in range(NT):
            pap, preg = self.ps(self.ps_rr())
            items = [(pap[:, 0:E], src[:, kc * 2048 + nt * 128: kc * 2048 + nt * 128 + 128],
                      wslot[0][:, kc * E:(kc + 1) * E]) for kc in range(8)]
            self.mm_group(_mk_group(items), reads=[rsrc, wslot[1]],
                          writes=[preg])
            nm = SM[:, 10:11]
            self.vop(lambda eng, o=nm, a=pap[:, 0:E]:
                     eng.reduce_max(o, a, axis=AX.X, negate=True),
                     reads=[preg], writes=[rSM])
            se_ = SM[:, 11:12]
            self.act(probs[:, 0:E], pap[:, 0:E], AF.Exp, bias=nm,
                     reads=[preg, rSM], writes=[rprob, rSM], accum_out=se_)
            rs = SM[:, 12:13]
            self.vop(lambda eng, o=rs, a=se_: eng.reciprocal(o, a),
                     reads=[rSM], writes=[rSM])
            self.vop(lambda eng, o=probs[:, 0:E], a=probs[:, 0:E], s=rs:
                     eng.tensor_scalar_mul(o, a, s),
                     reads=[rprob, rSM], writes=[rprob])
            m1 = SM[:, 13:14]
            self.vop(lambda eng, o=m1, a=probs[:, 0:E]:
                     eng.reduce_max(o, a, axis=AX.X),
                     reads=[rprob], writes=[rSM])
            ind1 = self.SCR32[0][:, 16:32]
            self.vop(lambda eng, o=ind1[:, 0:E], a=probs[:, 0:E], s=m1:
                     eng.tensor_single_scalar(o, a, s, ALU.is_ge),
                     reads=[rprob, rSM], writes=[rprob])
            pm = self.SCR32[0][:, 32:48]
            self.vop(lambda eng, o=pm[:, 0:E], a=ind1[:, 0:E], b=probs[:, 0:E]:
                     eng.scalar_tensor_tensor(o, a, -1e9, b, ALU.mult, ALU.add),
                     reads=[rprob], writes=[rprob])
            m2 = SM[:, 14:15]
            self.vop(lambda eng, o=m2, a=pm[:, 0:E]:
                     eng.reduce_max(o, a, axis=AX.X),
                     reads=[rprob], writes=[rSM])
            den = SM[:, 15:16]
            self.vop(lambda eng, o=den, a=m1, b=m2: eng.tensor_add(o, a, b),
                     reads=[rSM], writes=[rSM])
            if plus_eps:
                self.vop(lambda eng, o=den, a=den:
                         eng.tensor_scalar_add(o, a, 1e-9),
                         reads=[rSM], writes=[rSM])
            rw = SM[:, 10:11]
            self.vop(lambda eng, o=rw, a=den: eng.reciprocal(o, a),
                     reads=[rSM], writes=[rSM])
            gcol = self.GTS[0][:, nt * 16: nt * 16 + E]
            self.vop(lambda eng, o=gcol, a=probs[:, 0:E], s=m2:
                     eng.tensor_single_scalar(o, a, s, ALU.is_ge),
                     reads=[rprob, rSM], writes=[self.GTS[1]])
            self.vop(lambda eng, o=gcol, a=gcol, b=probs[:, 0:E]:
                     eng.tensor_mul(o, a, b),
                     reads=[rprob, self.GTS[1]], writes=[self.GTS[1]])
            self.vop(lambda eng, o=gcol, a=gcol, s=rw:
                     eng.tensor_scalar_mul(o, a, s),
                     reads=[rSM, self.GTS[1]], writes=[self.GTS[1]])

    def emit_expert(self, src, rsrc, wg_t, wu_t, wd_t, base_gu, base_d,
                    Fdim, gate_col, silu_on_g=True):
        TC, rTC = self.TC
        nF = Fdim // 128
        win = 4 if nF <= 8 else 2
        for w0 in range(0, 4, win):
            for fc in range(nF):
                sg = self.WS[(2 * fc) % 4]
                su = self.WS[(2 * fc + 1) % 4]
                self.wdma(0, sg[0][:, 0:1024],
                          AP(wg_t, base_gu + fc * 128,
                             [[Fdim, 128], [128 * Fdim, 8], [1, 128]]),
                          reads=[sg[1]], writes=[sg[1]])
                self.wdma(1, su[0][:, 0:1024],
                          AP(wu_t, base_gu + fc * 128,
                             [[Fdim, 128], [128 * Fdim, 8], [1, 128]]),
                          reads=[su[1]], writes=[su[1]])
                for tw in range(win):
                    tt4 = w0 + tw
                    pg, prg = self.ps(self.ps_rr())
                    pu, pru = self.ps(self.ps_rr())
                    gi = [(pg, sg[0][:, kc * 128:(kc + 1) * 128],
                           src[:, kc * 2048 + tt4 * 512: kc * 2048 + tt4 * 512 + 512])
                          for kc in range(8)]
                    self.mm_group(_mk_group(gi), reads=[sg[1], rsrc],
                                  writes=[prg])
                    ui = [(pu, su[0][:, kc * 128:(kc + 1) * 128],
                           src[:, kc * 2048 + tt4 * 512: kc * 2048 + tt4 * 512 + 512])
                          for kc in range(8)]
                    self.mm_group(_mk_group(ui), reads=[su[1], rsrc],
                                  writes=[pru])
                    stile = self.SCRB[0][:, (tw % 2) * 512:(tw % 2) * 512 + 512]
                    silu_src = pg if silu_on_g else pu
                    silu_reg = prg if silu_on_g else pru
                    other = pu if silu_on_g else pg
                    other_reg = pru if silu_on_g else prg
                    self.act(stile, silu_src, AF.Silu,
                             reads=[silu_reg], writes=[self.SCRB[1]])
                    self.vop(lambda eng,
                             o=TC[:, fc * (win * 512) + tw * 512:
                                  fc * (win * 512) + tw * 512 + 512],
                             a=other, b=stile:
                             eng.tensor_mul(o, a, b),
                             reads=[other_reg, self.SCRB[1]], writes=[rTC])
            for tb in range(win):
                tt4 = w0 + tb
                for ds in range(2):
                    psl = [self.ps_rr() for _ in range(4)]
                    for fc2 in range((nF + 1) // 2):
                        slot = self.WS[fc2 % 4]
                        nld = min(2, nF - fc2 * 2)
                        self.wdma(fc2, slot[0][:, 0:512 * nld],
                                  AP(wd_t, base_d + ds * 512 + fc2 * 256 * 1024,
                                     [[1024, 128], [128 * 1024, nld], [1, 512]]),
                                  reads=[slot[1]], writes=[slot[1]])
                        for fh in range(nld):
                            fc = fc2 * 2 + fh
                            for i in range(4):
                                pap, preg = self.ps(psl[i])
                                its = [(pap,
                                        TC[:, fc * (win * 512) + tb * 512 + i * 128:
                                           fc * (win * 512) + tb * 512 + i * 128 + 128],
                                        slot[0][:, fh * 512:fh * 512 + 512],
                                        fc == 0, fc == nF - 1)]
                                self.mm_group(its, reads=[slot[1], rTC],
                                              writes=[preg])
                    for i in range(4):
                        nt = tt4 * 4 + i
                        pap, preg = self.ps(psl[i])
                        xs = self.X[0][:, nt * 1024 + ds * 512:
                                       nt * 1024 + ds * 512 + 512]
                        if gate_col is None:
                            self.vop(lambda eng, o=xs, a=pap, b=xs:
                                     eng.tensor_add(o, a, b),
                                     reads=[preg, self.X[1]],
                                     writes=[self.X[1]])
                        else:
                            gc = self.GTS[0][:, nt * 16 + gate_col:
                                             nt * 16 + gate_col + 1]
                            self.vop(lambda eng, o=xs, a=pap, s=gc, b=xs:
                                     eng.scalar_tensor_tensor(
                                         o, a, s, b, ALU.mult, ALU.add),
                                     reads=[preg, self.X[1], self.GTS[1]],
                                     writes=[self.X[1]])

    def emit_ln_rms(self, dst, dst_reg, g1cm):
        X, rX = self.X[0], self.X[1]
        dump, rdump = self.SCR32[0], self.SCR32[1]
        xnb = self.SCRB[0][:, 0:1024]
        hb = self.SCRB[0][:, 1024:2048]
        rxnb = self.SCRB[1]
        SM, rSM = self.SMALL[0], self.SMALL[1]
        for i in range(NT):
            xi = X[:, i * 1024:(i + 1) * 1024]
            self.act(dump[:, 0:1024], xi, AF.Copy,
                     reads=[rX], writes=[rdump, rSM], accum_out=SM[:, 0:1])
            self.act(dump[:, 0:1024], xi, AF.Square,
                     reads=[rX], writes=[rdump, rSM], accum_out=SM[:, 1:2])
            mean = self.MEAN[0][:, i:i + 1]
            rstd = self.RSTD[0][:, i:i + 1]
            t0, t1 = SM[:, 2:3], SM[:, 3:4]
            self.vop(lambda eng, o=mean, a=SM[:, 0:1]:
                     eng.tensor_scalar_mul(o, a, 1.0 / D),
                     reads=[rSM], writes=[self.MEAN[1]])
            self.vop(lambda eng, o=t0, a=SM[:, 1:2]:
                     eng.tensor_scalar_mul(o, a, 1.0 / D),
                     reads=[rSM], writes=[rSM])
            self.vop(lambda eng, o=t1, m=mean: eng.tensor_mul(o, m, m),
                     reads=[self.MEAN[1]], writes=[rSM])
            self.vop(lambda eng, o=t0, a=t0, b=t1: eng.tensor_sub(o, a, b),
                     reads=[rSM], writes=[rSM])
            self.act(t1, t0, AF.Sqrt, bias=self.constv(1e-5),
                     reads=[rSM, self.CONSTV[1]], writes=[rSM])
            self.vop(lambda eng, o=rstd, a=t1: eng.reciprocal(o, a),
                     reads=[rSM], writes=[self.RSTD[1]])
            self.vop(lambda eng, o=xnb, a=xi, m=mean, r=rstd:
                     eng.tensor_scalar(o, a, m, r, ALU.subtract, ALU.mult),
                     reads=[rX, self.MEAN[1], self.RSTD[1]],
                     writes=[rxnb])
            self.act(dump[:, 0:1024], xnb, AF.Square,
                     reads=[rxnb], writes=[rdump, rSM], accum_out=SM[:, 4:5])
            self.vop(lambda eng, o=SM[:, 4:5], a=SM[:, 4:5]:
                     eng.tensor_scalar_mul(o, a, 1.0 / D),
                     reads=[rSM], writes=[rSM])
            self.act(SM[:, 5:6], SM[:, 4:5], AF.Sqrt, bias=self.constv(1e-5),
                     reads=[rSM, self.CONSTV[1]], writes=[rSM])
            self.vop(lambda eng, o=SM[:, 4:5], a=SM[:, 5:6]:
                     eng.reciprocal(o, a),
                     reads=[rSM], writes=[rSM])
            self.vop(lambda eng, o=hb, a=xnb, s=SM[:, 4:5]:
                     eng.tensor_scalar_mul(o, a, s),
                     reads=[rxnb, rSM], writes=[rxnb])
            pap, preg = self.ps(self.ps_rr())
            pbf = pap.bitcast(BF16)
            items = [(pbf[:, c * 128:(c + 1) * 128],
                      hb[:, c * 128:(c + 1) * 128]) for c in range(8)]
            self.transpose_group(items, reads=[rxnb], writes=[preg])
            for c in range(8):
                self.vop(lambda eng,
                         o=dst[:, c * 2048 + i * 128: c * 2048 + (i + 1) * 128],
                         a=pbf[:, c * 128:(c + 1) * 128], g=g1cm[:, c:c + 1]:
                         eng.tensor_scalar_mul(o, a, g),
                         reads=[preg, rSM], writes=[dst_reg])

    def emit_latent_qk(self, dst, dst_reg, w_name):
        TC, rTC = self.TC
        wbf = self.dram[w_name + "_bf"].tensor
        cos_t = self.dram["c_cos"].tensor
        sin_t = self.dram["c_sin"].tensor
        stage = self.SCRB[0][:, 0:1024]
        rstage = self.SCRB[1]
        cosb = self.SCR32[0][:, 0:256]
        sinb = self.SCR32[0][:, 256:512]
        tmpa = self.SCR32[0][:, 512:768]
        tmpb = self.SCR32[0][:, 768:1024]
        r32 = self.SCR32[1]
        for nt in range(NT):
            self.p.dma("sync", cosb,
                       AP(cos_t, nt * 128 * 16, [[16, 128], [0, 16], [1, 16]]),
                       reads=[r32], writes=[r32])
            self.p.dma("sync", sinb,
                       AP(sin_t, nt * 128 * 16, [[16, 128], [0, 16], [1, 16]]),
                       reads=[r32], writes=[r32])
            for vs in range(2):
                base = ((nt * 2 + vs) % 2) * 2
                for half in range(2):
                    slot = self.WS[base + half]
                    self.wdma(half, slot[0][:, 0:1024],
                              AP(wbf, vs * 512 + half * 2 * 128 * 1024,
                                 [[1024, 128], [128 * 1024, 2], [1, 512]]),
                              reads=[slot[1]], writes=[slot[1]])
                pap, preg = self.ps(self.ps_rr())
                its = [(pap,
                        TC[:, kc * 2048 + nt * 128: kc * 2048 + nt * 128 + 128],
                        self.WS[base + kc // 2][0][:, (kc % 2) * 512:(kc % 2) * 512 + 512])
                       for kc in range(4)]
                self.mm_group(_mk_group(its),
                              reads=[self.WS[base][1], self.WS[base + 1][1], rTC],
                              writes=[preg])
                self.act(stage[:, vs * 512:(vs + 1) * 512], pap, AF.Copy,
                         reads=[preg], writes=[rstage])
            st = stage
            part = st.ap[0][0]
            def hap(base, off):
                return AP(st.tensor, st.offset + off, [[part, 128], [64, 16], [1, 16]])
            t1 = hap(st, 0)
            t2 = hap(st, 32)
            ca = AP(cosb.tensor, cosb.offset, [[cosb.ap[0][0], 128], [0, 16], [1, 16]])
            sa = AP(sinb.tensor, sinb.offset, [[sinb.ap[0][0], 128], [0, 16], [1, 16]])
            ta = AP(tmpa.tensor, tmpa.offset, [[tmpa.ap[0][0], 128], [16, 16], [1, 16]])
            tb = AP(tmpb.tensor, tmpb.offset, [[tmpb.ap[0][0], 128], [16, 16], [1, 16]])
            self.vop(lambda eng: eng.tensor_mul(ta, t1, sa),
                     reads=[rstage, r32], writes=[r32])
            self.vop(lambda eng: eng.tensor_mul(tb, t2, ca),
                     reads=[rstage, r32], writes=[r32])
            self.vop(lambda eng: eng.tensor_add(ta, ta, tb),
                     reads=[r32], writes=[r32])
            self.vop(lambda eng: eng.tensor_mul(tb, t2, sa),
                     reads=[rstage, r32], writes=[r32])
            self.vop(lambda eng: eng.tensor_copy(t2, ta),
                     reads=[r32], writes=[rstage])
            self.vop(lambda eng: eng.tensor_mul(ta, t1, ca),
                     reads=[rstage, r32], writes=[r32])
            self.vop(lambda eng: eng.tensor_sub(t1, ta, tb),
                     reads=[r32], writes=[rstage])
            pap, preg = self.ps(self.ps_rr())
            pbf = pap.bitcast(BF16)
            items = [(pbf[:, c * 128:(c + 1) * 128],
                      st[:, c * 128:(c + 1) * 128]) for c in range(8)]
            self.transpose_group(items, reads=[rstage], writes=[preg])
            self.vop(lambda eng, o=_chunks_out_h(dst, nt), a=pbf[:, 0:1024]:
                     eng.tensor_copy(o, a),
                     reads=[preg], writes=[dst_reg])

    def emit_l0f_up(self, g2cm):
        TC, rTC = self.TC
        TD = self.TD[0]
        TA, rTA = self.TA
        wbf = self.dram["l0f_wup_bf"].tensor
        for qtg in range(4):
            for i in range(4):
                qt = qtg * 4 + i
                pap, preg = self.ps(self.ps_rr())
                pbf = pap.bitcast(BF16)
                items = [(pbf[:, c * 128:(c + 1) * 128],
                          TC[:, 8192 + qt * 512 + c * 128: 8192 + qt * 512 + (c + 1) * 128])
                         for c in range(4)]
                self.transpose_group(items, reads=[rTC], writes=[preg])
                self.vop(lambda eng, o=TD[:, i * 2048:i * 2048 + 512],
                         a=pbf[:, 0:512]:
                         eng.tensor_copy(o, a),
                         reads=[preg],
                         writes=self.td_regs(i * 2048, i * 2048 + 512))
            for ds in range(2):
                psl = [self.ps_rr() for _ in range(4)]
                for dc2 in range(2):
                    slot = self.WS[dc2 % 4]
                    self.wdma(dc2, slot[0][:, 0:1024],
                              AP(wbf, ds * 512 + dc2 * 256 * 1024,
                                 [[1024, 128], [128 * 1024, 2], [1, 512]]),
                              reads=[slot[1]], writes=[slot[1]])
                    for dh in range(2):
                        dc = dc2 * 2 + dh
                        for i in range(4):
                            pap, preg = self.ps(psl[i])
                            its = [(pap,
                                    TD[:, i * 2048 + dc * 128: i * 2048 + (dc + 1) * 128],
                                    slot[0][:, dh * 512:dh * 512 + 512],
                                    dc == 0, dc == 3)]
                            self.mm_group(
                                its,
                                reads=[slot[1]] + list(self.td_regs(
                                    i * 2048 + dc * 128,
                                    i * 2048 + (dc + 1) * 128)),
                                writes=[preg])
                for i in range(4):
                    qt = qtg * 4 + i
                    pap, preg = self.ps(psl[i])
                    xs = self.X[0][:, qt * 1024 + ds * 512: qt * 1024 + ds * 512 + 512]
                    xnr = self.SCR32[0][:, 0:512]
                    mean = self.MEAN[0][:, qt:qt + 1]
                    rstd = self.RSTD[0][:, qt:qt + 1]
                    self.vop(lambda eng, o=xnr, a=xs, m=mean, r=rstd:
                             eng.tensor_scalar(o, a, m, r, ALU.subtract,
                                               ALU.mult),
                             reads=[self.X[1], self.MEAN[1], self.RSTD[1]],
                             writes=[self.SCR32[1]])
                    xfs = TA[:, qt * 1024 + ds * 512: qt * 1024 + ds * 512 + 512]
                    self.vop(lambda eng, o=xfs, a=pap, b=xnr:
                             eng.tensor_add(o, a, b),
                             reads=[preg, self.SCR32[1]], writes=[rTA])
                    self.vop(lambda eng, o=xs, a=pap, b=xs:
                             eng.tensor_add(o, a, b),
                             reads=[preg, self.X[1]], writes=[self.X[1]])
                    self.vop(lambda eng, o=xs, a=xnr, b=xs:
                             eng.tensor_add(o, a, b),
                             reads=[self.SCR32[1], self.X[1]],
                             writes=[self.X[1]])
        dump, rdump = self.SCR32[0], self.SCR32[1]
        SM, rSM = self.SMALL[0], self.SMALL[1]
        tb_ = self.SCRB[0][:, 0:1024]
        for i in range(NT):
            xfi = TA[:, i * 1024:(i + 1) * 1024]
            self.act(dump[:, 0:1024], xfi, AF.Square,
                     reads=[rTA], writes=[rdump, rSM], accum_out=SM[:, 0:1])
            self.vop(lambda eng, o=SM[:, 0:1], a=SM[:, 0:1]:
                     eng.tensor_scalar_mul(o, a, 1.0 / D),
                     reads=[rSM], writes=[rSM])
            self.act(SM[:, 1:2], SM[:, 0:1], AF.Sqrt, bias=self.constv(1e-5),
                     reads=[rSM, self.CONSTV[1]], writes=[rSM])
            self.vop(lambda eng, o=SM[:, 0:1], a=SM[:, 1:2]:
                     eng.reciprocal(o, a),
                     reads=[rSM], writes=[rSM])
            self.vop(lambda eng, o=tb_, a=xfi, s=SM[:, 0:1]:
                     eng.tensor_scalar_mul(o, a, s),
                     reads=[rTA, rSM], writes=[self.SCRB[1]])
            pap, preg = self.ps(self.ps_rr())
            pbf = pap.bitcast(BF16)
            items = [(pbf[:, c * 128:(c + 1) * 128],
                      tb_[:, c * 128:(c + 1) * 128]) for c in range(8)]
            self.transpose_group(items, reads=[self.SCRB[1]], writes=[preg])
            for c in range(8):
                self.vop(lambda eng,
                         o=self.TB[0][:, c * 2048 + i * 128: c * 2048 + (i + 1) * 128],
                         a=pbf[:, c * 128:(c + 1) * 128], g=g2cm[:, c:c + 1]:
                         eng.tensor_scalar_mul(o, a, g),
                         reads=[preg, rSM], writes=[self.TB[1]])


def _chunks_out_h(dst, i):
    a = dst[:, :]
    return AP(a.tensor, i * 128, [[a.ap[0][0], 128], [2048, 8], [1, 128]])



class FullBuilder(Builder, _AttnMixin):
    def __init__(self, parts):
        Builder.__init__(self, parts)
        self.debug = False
        self._dbg_outs = {}
        self.rVS = Region("vs")
        self.rGARR = Region("garr")
        self._brow_pos = 0
        self._garr_done = False

    def _dbg(self, tag, ap, region, shape=None):
        if not self.debug or tag in self._dbg_outs:
            return
        shape = shape or list(ap.shape)
        d = self.nc.dram_tensor("dbg_" + tag, shape, ap.dtype,
                                kind="ExternalOutput").ap()
        self._dbg_outs[tag] = d
        self.p.dma("gpsimd", d[:, :] if len(shape) == 2 else d[:],
                   ap, reads=[region], writes=[])

    def emit_garr(self):
        if self._garr_done:
            return
        self._garr_done = True
        p = self.p
        re33 = self.SMALL[0][0:33, 32:48]
        rSM = self.SMALL[1]
        p.dma("sync", self.SMALL[0][0:32, 32:48], self.dram["l0a_rel"][:, :],
              reads=[rSM], writes=[rSM])
        self.vop(lambda eng, o=self.SMALL[0][32:33, 32:48]:
                 eng.memset(o, 1.0), reads=[rSM], writes=[rSM])
        oh_t = self.dram["c_oh2"].tensor
        garr_t = self.dram["garr"].tensor
        stage = self.SCRB[0][0:16, 0:512]
        pos = 0
        while pos < NGARR:
            wdt = min(512, NGARR - pos)
            rhs32 = self.SCR32[0][0:33, 0:wdt]
            p.dma("sync", rhs32, AP(oh_t, pos, [[NGARR, 33], [1, wdt]]),
                  reads=[self.SCR32[1]], writes=[self.SCR32[1]])
            pap, preg = self.ps(self.ps_rr())
            self.mm_group([(pap[0:16, 0:wdt], re33, rhs32, True, True)],
                          reads=[rSM, self.SCR32[1]], writes=[preg])
            self.act(stage[0:16, 0:wdt], pap[0:16, 0:wdt], AF.Copy,
                     reads=[preg], writes=[self.SCRB[1]])
            p.dma("gpsimd", AP(garr_t, pos, [[NGARR, 16], [1, wdt]]),
                  stage[0:16, 0:wdt],
                  reads=[self.SCRB[1]], writes=[self.rGARR])
            pos += wdt

    def part_l0a(self):
        self.emit_garr()
        cfg = AttnCfg("l0a", "l0a_wq", "l0a_wk", "l0a_wv", "l0a_wo",
                      bq="l0a_bq", bk="l0a_bk", bv="l0a_bv", bo="l0a_bo",
                      scale=1.0, causal=True, tbl="garr")
        self.emit_full_attn(cfg)

    def part_l1a(self):
        cfg = AttnCfg("l1a", "l1a_wqkv", "l1a_wqkv", "l1a_wqkv", "l1a_wout",
                      wcols=3072, bo="l1a_bout", scale=64 ** -0.5,
                      qcol0=0, kcol0=1024, vcol0=2048)
        cfg.inner_gamma = "l1a_gamma"
        cfg.inner_beta = "l1a_beta"
        self.emit_full_attn(cfg, inner_ln=True)

    def part_l1b(self):
        cfg = AttnCfg("l1b", "l1b_wq", "l1b_wk", "l1b_wv", "l1b_wo",
                      bq="l1b_bq", bk="l1b_bk", bv="l1b_bv", bo="l1b_bo",
                      scale=64 ** -0.5, colmask=True)
        self.emit_full_attn(cfg)

    def part_l1f(self):
        self.emit_ln(self.TA[0], self.TA[1])
        wgs = self.dram["l1f_wg_s_bf"].tensor
        wus = self.dram["l1f_wu_s_bf"].tensor
        wds = self.dram["l1f_wd_s_bf"].tensor
        for n in range(2):
            self.emit_expert(self.TA[0], self.TA[1], wgs, wus, wds,
                             n * 1024 * 512, n * 512 * 1024, 512, None)
        self.emit_router_gates(self.TA[0], self.TA[1], "l1f_router", 16, True)
        wgr = self.dram["l1f_wg_r_bf"].tensor
        wur = self.dram["l1f_wu_r_bf"].tensor
        wdr = self.dram["l1f_wd_r_bf"].tensor
        for e in range(16):
            self.emit_expert(self.TA[0], self.TA[1], wgr, wur, wdr,
                             e * 1024 * 256, e * 256 * 1024, 256, e)

    def part_l0f(self):
        g1cm, g2cm = self.load_cm_pair("l0f_g1", "l0f_g2")
        self.emit_ln_rms(self.TA[0], self.TA[1], g1cm)
        self.emit_qT_proj(self.TC, "l0f_wdown", 0, None, 512, ndout=4)
        cfgv = AttnCfg("l0fv", None, None, "l0f_wv", None, wcols=512)
        self._battn = {}
        self.emit_v_to_vs(cfgv, src=self.TC, hd=32, vtot=512, nkc=4)
        self.emit_latent_qk(self.TA[0], self.TA[1], "l0f_wq")
        self.emit_latent_qk(self.TB[0], self.TB[1], "l0f_wk")
        cfgc = AttnCfg("l0fc", None, None, None, None, scale=32 ** -0.5,
                       causal=True, tbl="gcaus")
        self.emit_attn_core(cfgc, (self.TA, self.TA[1]), (self.TB, self.TB[1]),
                            self.TC[0], self.TC[1], 8192, 512, 32)
        self.emit_l0f_up(g2cm)
        self.emit_router_gates(self.TB[0], self.TB[1], "l0f_router", 8, False)
        w1 = self.dram["l0f_w1_bf"].tensor
        w3 = self.dram["l0f_w3_bf"].tensor
        w2 = self.dram["l0f_w2_bf"].tensor
        for e in range(8):
            self.emit_expert(self.TB[0], self.TB[1], w1, w3, w2,
                             e * 1024 * 2048, e * 2048 * 1024, 2048, e)

    def part_l0b(self):
        self.brow_reset()
        self._battn = {}
        self.emit_ln(self.TA[0], self.TA[1])
        self._battn["v"] = self.load_row_bf("l0b_bv", 1024, slot=0)
        self.emit_qT_proj(self.TC, "l0b_wv", 0, self._battn["v"], 1024)
        self._battn["q"] = self.load_row_bf("l0b_bq", 1024, slot=0)
        self._battn["k"] = self.load_row_bf("l0b_bk", 1024, slot=1)
        self.p.dma("sync", self.B64[0][:, :], self.dram["c_b64"][:, :],
                   reads=[self.B64[1]], writes=[self.B64[1]])
        self.p.dma("sync", self.E64[0][:, :], self.dram["c_e64"][:, :],
                   reads=[self.E64[1]], writes=[self.E64[1]])
        TD = self.TD[0]
        self.p.dma("sync", TD[0:16, 2048:4096], self.dram["c_m16"][:, :],
                   reads=list(self.td_regs(2048, 4096)),
                   writes=list(self.td_regs(2048, 4096)))
        TA, rTA = self.TA
        wq_t = self.dram["l0b_wq_bf"].tensor
        wk_t = self.dram["l0b_wk_bf"].tensor
        bq, bk = self._battn["q"], self._battn["k"]
        sc_regs = self.pr[0:4]
        for dc in range(8):
            sq_ = self.WS[(2 * dc) % 4]
            sk_ = self.WS[(2 * dc + 1) % 4]
            self.wdma(0, sq_[0][:, 0:1024],
                      AP(wq_t, dc * 128, [[1024, 128], [128 * 1024, 8], [1, 128]]),
                      reads=[sq_[1]], writes=[sq_[1]])
            self.wdma(1, sk_[0][:, 0:1024],
                      AP(wk_t, dc * 128, [[1024, 128], [128 * 1024, 8], [1, 128]]),
                      reads=[sk_[1]], writes=[sk_[1]])
            for tt4 in range(4):
                ts_ = slice(tt4 * 512, tt4 * 512 + 512)
                qi = self.ps_rr(lo=4, hi=8)
                ki = self.ps_rr(lo=4, hi=8)
                qp, qreg = self.ps(qi)
                kp, kreg = self.ps(ki)
                qits = [(qp, sq_[0][:, kc * 128:(kc + 1) * 128],
                         TA[:, kc * 2048 + tt4 * 512: kc * 2048 + tt4 * 512 + 512])
                        for kc in range(8)]
                qits.append((qp, bq[0][0:1, dc * 128:dc * 128 + 128],
                             self.ONE[0][0:1, 0:512]))
                self.mm_group(_mk_group(qits),
                              reads=[sq_[1], rTA, bq[1], self.ONE[1]],
                              writes=[qreg])
                kits = [(kp, sk_[0][:, kc * 128:(kc + 1) * 128],
                         TA[:, kc * 2048 + tt4 * 512: kc * 2048 + tt4 * 512 + 512])
                        for kc in range(8)]
                kits.append((kp, bk[0][0:1, dc * 128:dc * 128 + 128],
                             self.ONE[0][0:1, 0:512]))
                self.mm_group(_mk_group(kits),
                              reads=[sk_[1], rTA, bk[1], self.ONE[1]],
                              writes=[kreg])
                kev = self.SCR32[0][:, 0:512]
                self.act(kev, kp, AF.Copy, reads=[kreg],
                         writes=[self.SCR32[1]])
                prod = self.SCR32[0][:, 512:1024]
                self.vop(lambda eng, o=prod, a=qp, b=kev:
                         eng.tensor_mul(o, a, b),
                         reads=[qreg, self.SCR32[1]], writes=[self.SCR32[1]])
                self.mm_group([(self.PP[0:16, ts_],
                                self.B64[0][:, dc * 16:dc * 16 + 16],
                                prod, dc == 0, dc == 7)],
                              reads=[self.B64[1], self.SCR32[1]],
                              writes=[sc_regs[tt4]])
        SM, rSM = self.SMALL[0], self.SMALL[1]
        for seg in range(4):
            part = self.PP[:, :].ap[0][0]
            mneg = SM[0:16, 16:20]
            pin = AP(self.PP[:, :].tensor, seg * 512,
                     [[part, 16], [1, 4], [4, 128]])
            self.vop(lambda eng, o=mneg, a=pin:
                     eng.reduce_max(o, a, axis=AX.X, negate=True),
                     reads=[sc_regs[seg]], writes=[rSM])
            self.vop(lambda eng, o=mneg, a=mneg:
                     eng.tensor_scalar_mul(o, a, 0.125),
                     reads=[rSM], writes=[rSM])
            for rp in range(4):
                pslice = AP(self.PP[:, :].tensor, seg * 512 + rp,
                            [[part, 16], [4, 128]])
                tdp = TD[:, :].ap[0][0]
                pout = AP(TD[:, :].tensor, seg * 512 + rp, [[tdp, 16], [4, 128]])
                sums = SM[0:16, 20:21]
                self.act(pout, pslice, AF.Exp, bias=mneg[0:16, rp:rp + 1],
                         scale=self.constv(0.125)[0:16, 0:1],
                         reads=[sc_regs[seg], rSM, self.CONSTV[1]],
                         writes=[self.TDr[seg], rSM], accum_out=sums)
                rec = SM[0:16, 21:22]
                self.vop(lambda eng, o=rec, a=sums: eng.reciprocal(o, a),
                         reads=[rSM], writes=[rSM])
                self.vop(lambda eng, o=pout, a=pout, s=rec:
                         eng.tensor_scalar_mul(o, a, s),
                         reads=[rSM, self.TDr[seg]], writes=[self.TDr[seg]])
        self.vop(lambda eng, o=TD[0:16, 0:2048], a=TD[0:16, 0:2048],
                 b=TD[0:16, 2048:4096]:
                 eng.tensor_mul(o, a, b),
                 reads=list(self.td_regs(0, 4096)),
                 writes=list(self.td_regs(0, 2048)))
        TB, rTB = self.TB
        TC, rTC = self.TC
        for dc in range(8):
            for tt4 in range(4):
                pap, preg = self.ps(self.ps_rr())
                its = [(pap, self.E64[0][0:16, dc * 128:(dc + 1) * 128],
                        TD[0:16, tt4 * 512:(tt4 + 1) * 512], True, True)]
                self.mm_group(its, reads=[self.E64[1], self.TDr[tt4]],
                              writes=[preg])
                self.vop(lambda eng,
                         o=TB[:, dc * 2048 + tt4 * 512: dc * 2048 + tt4 * 512 + 512],
                         a=pap,
                         b=TC[:, dc * 2048 + tt4 * 512: dc * 2048 + tt4 * 512 + 512]:
                         eng.tensor_mul(o, a, b),
                         reads=[preg, rTC], writes=[rTB])
        self._battn["o"] = self.load_row_bf("l0b_bo", 1024, slot=1)
        self.emit_out_proj("l0b_wo", self._battn["o"], self.TB, rTB, 1024,
                           src_is_T=True)

    def load_consts(self):
        for cname, buf in (("c_ident", self.IDE),
                           ("c_J", self.JX), ("c_ones", self.ONE)):
            self.p.dma("sync", buf[0][:, :], self.dram[cname][:, :],
                       reads=[buf[1]], writes=[buf[1]])

    def build(self):
        self.load_consts()
        self.convert_weights()
        for s in range(SEQ_PER_CORE):
            self.load_x(s)
            for prt in self.parts:
                getattr(self, "part_" + prt)()
            self.store_x(s)
        self.p.finalize(self.nc, self.es)
        self.es.close()
        return self.nc



MODULE_ORDER = ["l0a", "l0f", "l0b", "l0f", "l1a", "l1f", "l1b", "l1f"]
FUSED = True


def _in_maps(parts, x, w, hc):
    need = set()
    for prt in parts:
        need.update(PART_WEIGHTS[prt])
    maps = []
    for c in range(N_CORES):
        m = {"x_in": np.ascontiguousarray(x[2 * c:2 * c + 2])}
        for n in sorted(need):
            m[n] = w[n] if n not in ROW_WEIGHTS else w[n]
        for n in CONST_NAMES:
            m[n] = hc[n]
        maps.append(m)
    return maps


def _run_parts(parts, x, w, hc, core_ids=None):
    b = FullBuilder(parts)
    nc = b.build()
    cids = core_ids or list(range(N_CORES))
    maps = _in_maps(parts, x, w, hc)[: len(cids)]
    res = run_bass_kernel_spmd(nc, maps, core_ids=cids)
    return np.concatenate([r["x_out"] for r in res.results], axis=0)


def kernel(**inputs):
    w = _prep_weights(inputs)
    hc = _host_constants()
    x = np.ascontiguousarray(np.asarray(inputs["L0_attn_a_x"], np.float32))
    if FUSED:
        x = _run_parts(MODULE_ORDER, x, w, hc)
    else:
        for prt in MODULE_ORDER:
            x = _run_parts([prt], x, w, hc)
    return x


# version 10
# speedup vs baseline: 1.0432x; 1.0432x over previous
import math
import numpy as np
import ml_dtypes
from contextlib import ExitStack

import concourse.bass as bass
import concourse.mybir as mybir
from concourse.ap import AP
from concourse.bass_utils import run_bass_kernel_spmd

F32 = mybir.dt.float32
BF16 = mybir.dt.bfloat16
AF = mybir.ActivationFunctionType
ALU = mybir.AluOpType
AX = mybir.AxisListType
BF = ml_dtypes.bfloat16

N_CORES = 8
BATCH = 16
SEQ = 2048
D = 1024
NT = SEQ // 128
NQB = SEQ // 512
SEQ_PER_CORE = BATCH // N_CORES
NEG = -1.0e38

ENGINES = ("tensor", "vector", "scalar", "gpsimd", "sync")
DMA_QUEUES = ("sync", "gpsimd", "scalar")



class _Sem:
    __slots__ = ("handle", "count", "name")

    def __init__(self, name):
        self.handle = None
        self.count = 0
        self.name = name


class Tok:
    __slots__ = ("sem", "val", "engine")

    def __init__(self, sem, val, engine):
        self.sem = sem
        self.val = val
        self.engine = engine


class Region:
    __slots__ = ("name", "writer", "readers")

    def __init__(self, name):
        self.name = name
        self.writer = None
        self.readers = {}


class Prog:
    SEM_CAP = 55000

    def __init__(self):
        self.streams = {e: [] for e in ENGINES}
        self.sems = []
        self.cur = {}
        self.dma_sem = {}
        self.clock = {}
        self.n_inst = 0
        for e in ENGINES:
            self.cur[e] = self._new_sem("c_" + e)
        for q in DMA_QUEUES:
            self.dma_sem[q] = self._new_sem("d_" + q)

    def _new_sem(self, name):
        s = _Sem(f"{name}_{len(self.sems)}")
        self.sems.append(s)
        return s

    def _wait(self, engine, tok):
        if tok is None:
            return
        key = (engine, id(tok.sem))
        if self.clock.get(key, 0) >= tok.val:
            return
        self.clock[key] = tok.val
        sem, val = tok.sem, tok.val
        self.streams[engine].append(lambda eng, s=sem, v=val: eng.wait_ge(s.handle, v))

    def _deps(self, engine, reads, writes):
        for r in reads:
            self._wait(engine, r.writer)
        for w in writes:
            self._wait(engine, w.writer)
            for t in w.readers.values():
                self._wait(engine, t)

    def _finish(self, engine, tok, reads, writes):
        for w in writes:
            w.writer = tok
            w.readers = {}
        for r in reads:
            r.readers[(tok.engine, id(tok.sem))] = tok

    def op(self, engine, fns, reads=(), writes=()):
        if not isinstance(fns, (list, tuple)):
            fns = [fns]
        self._deps(engine, reads, writes)
        sem = self.cur[engine]
        if sem.count + 1 > self.SEM_CAP:
            sem = self.cur[engine] = self._new_sem("c_" + engine)
        sem.count += 1
        val = sem.count
        self.n_inst += len(fns)
        for f in fns[:-1]:
            self.streams[engine].append(f)
        last = fns[-1]
        self.streams[engine].append(
            lambda eng, f=last, s=sem, v=val: f(eng).then_inc(s.handle, 1))
        tok = Tok(sem, val, engine)
        self._finish(engine, tok, reads, writes)
        return tok

    def dma(self, queue, out_ap, in_ap, reads=(), writes=(), **kw):
        self._deps(queue, reads, writes)
        sem = self.dma_sem[queue]
        if sem.count + 16 > self.SEM_CAP:
            sem = self.dma_sem[queue] = self._new_sem("d_" + queue)
        sem.count += 16
        val = sem.count
        self.n_inst += 1
        self.streams[queue].append(
            lambda eng, o=out_ap, i=in_ap, s=sem, v=val, kw=kw:
                eng.dma_start(out=o, in_=i, **kw).then_inc(s.handle, 16))
        tok = Tok(sem, val, queue)
        self._finish(queue, tok, reads, writes)
        return tok

    def barrier_token(self, engine):
        sem = self.cur[engine]
        return Tok(sem, sem.count, engine) if sem.count else None

    def finalize(self, nc, es):
        for s in self.sems:
            if s.name.startswith("d_") and s.count:
                self.streams["sync"].append(
                    lambda eng, s=s, v=s.count: eng.wait_ge(s.handle, v))
        for s in self.sems:
            s.handle = es.enter_context(nc.semaphore(s.name))
        block = es.enter_context(nc.Block())
        for ename in ENGINES:
            items = self.streams[ename]
            getattr(block, ename)(lambda eng, items=items: [f(eng) for f in items])



def _bucket_causal(d):
    NUM_BUCKETS, MAX_DISTANCE = 32, 128
    n = max(d, 0)
    max_exact = NUM_BUCKETS // 2
    if n < max_exact:
        return n
    val = max_exact + int(
        math.log(max(n, 1) / max_exact) / math.log(MAX_DISTANCE / max_exact)
        * (NUM_BUCKETS - max_exact))
    return min(val, NUM_BUCKETS - 1)


DMIN = -511
NGARR = 2688
TBL_W = 2560


def _host_constants():
    c = {}
    c["c_ident"] = np.eye(128).astype(BF)
    c["c_ident32"] = np.eye(128, dtype=np.float32)
    c["c_J"] = np.eye(128)[::-1].copy().astype(BF)
    c["c_ones"] = np.ones((1, 512), dtype=np.float32).astype(BF)
    j = np.arange(NGARR)
    dval = j + DMIN - 127
    c["c_gcaus"] = np.where(dval >= 0, 0.0, NEG).astype(np.float32).astype(BF)
    oh = np.zeros((33, NGARR), dtype=np.float32)
    for jj in range(NGARR):
        dd = int(dval[jj])
        if dd >= 0:
            oh[_bucket_causal(dd), jj] = 0.125
        else:
            oh[32, jj] = NEG
    c["c_oh2"] = oh
    c["c_ones16"] = np.ones((1, 16), dtype=np.float32)
    HD = 32
    inv_freq = 1.0 / (10000.0 ** (np.arange(0, HD, 2, dtype=np.float64) / HD))
    ang = np.arange(SEQ, dtype=np.float64)[:, None] * inv_freq[None, :]
    c["c_cos"] = np.cos(ang).astype(np.float32)
    c["c_sin"] = np.sin(ang).astype(np.float32)
    b64 = np.zeros((128, 128), dtype=np.float32)
    for dc in range(8):
        for p in range(128):
            b64[p, dc * 16 + (128 * dc + p) // 64] = 1.0
    c["c_b64"] = b64
    hh = np.arange(16)[:, None] % 4
    tt = np.arange(SEQ)[None, :] % 4
    c["c_m16"] = (hh == tt).astype(np.float32).astype(BF)
    c["c_e64"] = (np.arange(1024)[None, :] // 64 == np.arange(16)[:, None]
                  ).astype(np.float32).astype(BF)
    return c


def _perm_latent():
    p = []
    for h in range(16):
        p.extend(32 * h + 2 * i for i in range(16))
    for h in range(16):
        p.extend(32 * h + 2 * i + 1 for i in range(16))
    return np.array(p)


def _prep_weights(inp):
    w = {}
    f32 = np.float32

    def T(a):
        return np.ascontiguousarray(np.asarray(a, f32).T)

    def C(a):
        return np.ascontiguousarray(np.asarray(a, f32))

    def row(a):
        return np.asarray(a, f32).reshape(1, -1).copy()

    for k in ("q", "k", "v", "o"):
        w[f"l0a_w{k}"] = T(inp[f"L0_attn_a_W{k}"])
        w[f"l0a_b{k}"] = row(inp[f"L0_attn_a_b{k}"])
    w["l0a_rel"] = C(inp["L0_attn_a_rel_emb"])
    for k in ("q", "k", "v", "o"):
        w[f"l0b_w{k}"] = C(inp[f"L0_attn_b_W{k}"])
        w[f"l0b_b{k}"] = row(inp[f"L0_attn_b_b{k}"])
    w["l0f_g1"] = row(inp["L0_ffn_g1"])
    w["l0f_g2"] = row(inp["L0_ffn_g2"])
    w["l0f_wdown"] = C(inp["L0_ffn_w_down"])
    def pad_lat(a):
        a = np.asarray(a, f32)
        out = np.zeros((512, 1024), f32)
        for h in range(16):
            out[:, 64 * h: 64 * h + 16] = a[:, 32 * h + 2 * np.arange(16)]
            out[:, 64 * h + 32: 64 * h + 48] = a[:, 32 * h + 2 * np.arange(16) + 1]
        return out
    w["l0f_wq"] = pad_lat(inp["L0_ffn_wq"])
    w["l0f_wk"] = pad_lat(inp["L0_ffn_wk"])
    w["l0f_wv"] = C(inp["L0_ffn_wv"])
    w["l0f_wup"] = C(inp["L0_ffn_w_up"])
    w["l0f_router"] = C(inp["L0_ffn_router_w"])
    w["l0f_w1"] = C(inp["L0_ffn_w1"])
    w["l0f_w3"] = C(inp["L0_ffn_w3"])
    w["l0f_w2"] = C(inp["L0_ffn_w2"])
    w["l1a_gamma"] = row(inp["L1_attn_a_gamma"])
    w["l1a_beta"] = row(inp["L1_attn_a_beta"])
    w["l1a_wqkv"] = C(inp["L1_attn_a_w_qkv"])
    w["l1a_wout"] = C(inp["L1_attn_a_w_out"])
    w["l1a_bout"] = row(inp["L1_attn_a_b_out"])
    for k in ("q", "k", "v", "o"):
        w[f"l1b_w{k}"] = T(inp[f"L1_attn_b_w{k}_w"])
        w[f"l1b_b{k}"] = row(inp[f"L1_attn_b_w{k}_b"])
    w["l1f_router"] = C(inp["L1_ffn_router_w"])
    w["l1f_wg_r"] = C(inp["L1_ffn_wg_r"])
    w["l1f_wu_r"] = C(inp["L1_ffn_wu_r"])
    w["l1f_wd_r"] = C(inp["L1_ffn_wd_r"])
    w["l1f_wg_s"] = C(inp["L1_ffn_wg_s"])
    w["l1f_wu_s"] = C(inp["L1_ffn_wu_s"])
    w["l1f_wd_s"] = C(inp["L1_ffn_wd_s"])
    return w



BIG_WEIGHTS = {
    "l0a_wq": (1024, 1024), "l0a_wk": (1024, 1024), "l0a_wv": (1024, 1024),
    "l0a_wo": (1024, 1024),
    "l0b_wq": (1024, 1024), "l0b_wk": (1024, 1024), "l0b_wv": (1024, 1024),
    "l0b_wo": (1024, 1024),
    "l0f_wdown": (1024, 512), "l0f_wq": (512, 1024), "l0f_wk": (512, 1024),
    "l0f_wv": (512, 512), "l0f_wup": (512, 1024), "l0f_router": (1024, 8),
    "l0f_w1": (8, 1024, 2048), "l0f_w3": (8, 1024, 2048), "l0f_w2": (8, 2048, 1024),
    "l1a_wqkv": (1024, 3072), "l1a_wout": (1024, 1024),
    "l1b_wq": (1024, 1024), "l1b_wk": (1024, 1024), "l1b_wv": (1024, 1024),
    "l1b_wo": (1024, 1024),
    "l1f_router": (1024, 16),
    "l1f_wg_r": (16, 1024, 256), "l1f_wu_r": (16, 1024, 256),
    "l1f_wd_r": (16, 256, 1024),
    "l1f_wg_s": (2, 1024, 512), "l1f_wu_s": (2, 1024, 512),
    "l1f_wd_s": (2, 512, 1024),
}
ROW_WEIGHTS = {
    "l0a_bq": 1024, "l0a_bk": 1024, "l0a_bv": 1024, "l0a_bo": 1024,
    "l0b_bq": 1024, "l0b_bk": 1024, "l0b_bv": 1024, "l0b_bo": 1024,
    "l0f_g1": 1024, "l0f_g2": 1024,
    "l1a_gamma": 1024, "l1a_beta": 1024, "l1a_bout": 1024,
    "l1b_bq": 1024, "l1b_bk": 1024, "l1b_bv": 1024, "l1b_bo": 1024,
}
PART_WEIGHTS = {
    "l0a": ["l0a_wq", "l0a_wk", "l0a_wv", "l0a_wo", "l0a_bq", "l0a_bk",
            "l0a_bv", "l0a_bo", "l0a_rel"],
    "l0f": ["l0f_g1", "l0f_g2", "l0f_wdown", "l0f_wq", "l0f_wk", "l0f_wv",
            "l0f_wup", "l0f_router", "l0f_w1", "l0f_w3", "l0f_w2"],
    "l0b": ["l0b_wq", "l0b_wk", "l0b_wv", "l0b_wo", "l0b_bq", "l0b_bk",
            "l0b_bv", "l0b_bo"],
    "l1a": ["l1a_gamma", "l1a_beta", "l1a_wqkv", "l1a_wout", "l1a_bout"],
    "l1b": ["l1b_wq", "l1b_wk", "l1b_wv", "l1b_wo", "l1b_bq", "l1b_bk",
            "l1b_bv", "l1b_bo"],
    "l1f": ["l1f_router", "l1f_wg_r", "l1f_wu_r", "l1f_wd_r", "l1f_wg_s",
            "l1f_wu_s", "l1f_wd_s"],
}
CONST_NAMES = ["c_ident", "c_ident32", "c_J", "c_ones", "c_gcaus", "c_oh2",
               "c_ones16", "c_cos", "c_sin", "c_m16", "c_e64", "c_b64"]
CONST_DT = {"c_oh2": F32, "c_cos": F32, "c_sin": F32, "c_ones16": F32,
            "c_ident32": F32, "c_b64": F32}


class Builder:
    def __init__(self, parts):
        self.parts = list(parts)
        self.p = Prog()
        self.nc = bass.Bass("TRN2", target_bir_lowering=False)
        self.es = ExitStack()
        nc = self.nc
        self.x_in = nc.dram_tensor("x_in", [SEQ_PER_CORE, SEQ, D], F32,
                                   kind="ExternalInput").ap()
        self.x_out = nc.dram_tensor("x_out", [SEQ_PER_CORE, SEQ, D], F32,
                                    kind="ExternalOutput").ap()
        self.dram = {}
        need = set()
        for prt in self.parts:
            need.update(PART_WEIGHTS[prt])
        hc = _host_constants()
        self.need_consts = [n for n in CONST_NAMES]
        for n in self.need_consts:
            arr = hc[n]
            dt = CONST_DT.get(n, BF16)
            self.dram[n] = nc.dram_tensor(n, list(arr.shape), dt,
                                          kind="ExternalInput").ap()
        self.need_weights = sorted(need)
        for n in self.need_weights:
            if n in BIG_WEIGHTS:
                self.dram[n] = nc.dram_tensor(n, list(BIG_WEIGHTS[n]), F32,
                                              kind="ExternalInput").ap()
                self.dram[n + "_bf"] = nc.dram_tensor(
                    n + "_bf", list(BIG_WEIGHTS[n]), BF16, kind="Internal").ap()
            elif n in ROW_WEIGHTS:
                self.dram[n] = nc.dram_tensor(n, [1, ROW_WEIGHTS[n]], F32,
                                              kind="ExternalInput").ap()
            elif n == "l0a_rel":
                self.dram[n] = nc.dram_tensor(n, [32, 16], F32,
                                              kind="ExternalInput").ap()
        self.dram["vs"] = nc.dram_tensor("vs", [16, SEQ, 65], BF16,
                                         kind="Internal").ap()
        self.dram["garr"] = nc.dram_tensor("garr", [16, NGARR], BF16,
                                           kind="Internal").ap()
        es, s = self.es, self
        def sb(name, shape, dt):
            t = es.enter_context(nc.sbuf_tensor(name, shape, dt))
            return t, Region(name)
        s.X = sb("X", [128, NT * 1024], F32)
        s.TA = sb("TA", [128, NT * 1024], BF16)
        s.TB = sb("TB", [128, NT * 1024], BF16)
        s.TC = sb("TC", [128, NT * 1024], BF16)
        s.TD = sb("TD", [128, 8 * 1024], BF16)
        s.TDr = [Region(f"TD{i}") for i in range(16)]
        s.WS = [sb(f"WS{i}", [128, 1152], BF16) for i in range(4)]
        s.SCR32 = sb("SCR32", [128, 1024], F32)
        s.SCRB = sb("SCRB", [128, 2560], BF16)
        s.MROW = sb("MROW", [1, 2048], BF16)
        s.IDE = sb("IDE", [128, 128], BF16)
        s.JX = sb("JX", [128, 128], BF16)
        s.ONE = sb("ONE", [1, 512], BF16)
        s.SUMS = sb("SUMS", [128, 16], F32)
        s.MEAN = sb("MEAN", [128, 16], F32)
        s.RSTD = sb("RSTD", [128, 16], F32)
        s.MM = sb("MM", [128, 16], F32)
        s.MMB = sb("MMB", [128, 16], BF16)
        s._att_regs = [Region(f"att{i}") for i in range(5)]
        s.MSK = sb("MSK", [128, 16], F32)
        s.GTS = sb("GTS", [128, 16 * 16], F32)
        s.SMALL = sb("SMALL", [128, 64], F32)
        s.B64 = sb("B64", [128, 128], F32)
        s.E64 = sb("E64", [16, 1024], BF16)
        s.CONSTV = sb("CONSTV", [128, 16], F32)
        s._constv_map = {}
        s.BROW = sb("BROW", [1, 2048], BF16)
        s.PP = es.enter_context(nc.psum_tensor("PP", [128, 2048], F32))
        s.PQ = es.enter_context(nc.psum_tensor("PQ", [128, 2048], F32))
        s.pr = [Region(f"PP{i}") for i in range(4)] + \
               [Region(f"PQ{i}") for i in range(4)]
        self._psum_rr = 0

    def ps(self, idx):
        t = self.PP if idx < 4 else self.PQ
        off = (idx % 4) * 512
        return t[:, off:off + 512], self.pr[idx]

    def ps_rr(self, n=1, lo=0, hi=8):
        out = []
        for _ in range(n):
            out.append(lo + self._psum_rr % (hi - lo))
            self._psum_rr += 1
        return out if n > 1 else out[0]

    def td_regs(self, c0, c1):
        return self.TDr[c0 // 512: (c1 + 511) // 512]

    def wdma(self, slot_idx, out_ap, in_ap, reads=(), writes=()):
        r = list(reads)
        if getattr(self, "_conv_reg", None) is not None:
            r.append(self._conv_reg)
        return self.p.dma("sync", out_ap, in_ap, reads=r, writes=writes)

    def constv(self, val):
        if val in self._constv_map:
            return self.CONSTV[0][:, self._constv_map[val]:self._constv_map[val] + 1]
        idx = len(self._constv_map)
        assert idx < 16
        self._constv_map[val] = idx
        ap = self.CONSTV[0][:, idx:idx + 1]
        self.vop(lambda eng, o=ap, v=val: eng.memset(o, v),
                 reads=[self.CONSTV[1]], writes=[self.CONSTV[1]])
        return ap

    def mm_group(self, items, reads, writes):
        fns = [lambda eng, it=it: eng.matmul(it[0], it[1], it[2],
                                             start=it[3], stop=it[4])
               for it in items]
        return self.p.op("tensor", fns, reads=reads, writes=writes)

    def transpose_group(self, items, reads, writes):
        idet, idereg = self.IDE
        ide = idet[:, :]
        fns = [lambda eng, it=it: eng.transpose(it[0], it[1], ide)
               for it in items]
        return self.p.op("tensor", fns, reads=list(reads) + [idereg],
                         writes=writes)

    def act(self, out_ap, in_ap, func, reads, writes, bias=0.0, scale=1.0,
            accum_out=None):
        return self.p.op(
            "scalar",
            lambda eng: eng.activation(out_ap, in_ap, func, bias=bias,
                                       scale=scale, accum_out=accum_out),
            reads=reads, writes=writes)

    def vop(self, fn, reads, writes):
        return self.p.op("vector", fn, reads=reads, writes=writes)

    def convert_weights(self):
        self.conv_reg = {}
        done = set()
        order = []
        for prt in self.parts:
            if prt not in order:
                order.append(prt)
        for prt in order:
            reg = self.conv_reg[prt] = Region("conv_" + prt)
            for name in PART_WEIGHTS[prt]:
                if name not in BIG_WEIGHTS or name in done:
                    continue
                done.add(name)
                n = int(np.prod(BIG_WEIGHTS[name]))
                src = self.dram[name].tensor
                dst = self.dram[name + "_bf"].tensor
                CH = 2 * 1024 * 1024
                pos = 0
                while pos < n:
                    w = min(CH, n - pos)
                    self.p.dma("gpsimd", AP(dst, pos, [[1, w]]),
                               AP(src, pos, [[1, w]]),
                               reads=[], writes=[reg])
                    pos += w

    def load_x(self, s):
        for i in range(NT):
            self.p.dma("sync", self.X[0][:, i * 1024:(i + 1) * 1024],
                       self.x_in[s, i * 128:(i + 1) * 128, :],
                       reads=[], writes=[self.X[1]])

    def store_x(self, s):
        for i in range(NT):
            self.p.dma("gpsimd", self.x_out[s, i * 128:(i + 1) * 128, :],
                       self.X[0][:, i * 1024:(i + 1) * 1024],
                       reads=[self.X[1]], writes=[])

    def load_cm_pair(self, name_g, name_b):
        out = []
        for j, nm in enumerate((name_g, name_b)):
            dst = self.SMALL[0][:, 16 + j * 8: 16 + (j + 1) * 8]
            src = self.dram[nm].tensor
            for c in range(8):
                self.p.dma("sync", dst[:, c:c + 1],
                           AP(src, c * 128, [[1, 128], [1, 1]]),
                           reads=[self.SMALL[1]], writes=[self.SMALL[1]])
            out.append(dst)
        return out[0], out[1]

    def emit_ln(self, dst, dst_reg, want_msk=False, inner_cm=None):
        X, rX = self.X[0], self.X[1]
        if hasattr(self, "_dbg"):
            self._dbg("X0", X[:, :], rX)
        dump, rdump = self.SCR32[0], self.SCR32[1]
        xnb = self.SCRB[0][:, 0:1024]
        rxnb = self.SCRB[1]
        SM, rSM = self.SMALL[0], self.SMALL[1]
        for i in range(NT):
            xi = X[:, i * 1024:(i + 1) * 1024]
            self.vop(lambda eng, o=self.SUMS[0][:, i:i + 1], a=xi:
                     eng.reduce_sum(o, a, axis=AX.X),
                     reads=[rX], writes=[self.SUMS[1]])
            self.act(dump[:, 0:1024], xi, AF.Square,
                     reads=[rX], writes=[rdump, rSM],
                     accum_out=SM[:, 0:1])
            mean = self.MEAN[0][:, i:i + 1]
            rstd = self.RSTD[0][:, i:i + 1]
            t0, t1 = SM[:, 1:2], SM[:, 2:3]
            self.vop(lambda eng, o=mean, a=self.SUMS[0][:, i:i + 1]:
                     eng.tensor_scalar_mul(o, a, 1.0 / D),
                     reads=[self.SUMS[1]], writes=[self.MEAN[1]])
            self.vop(lambda eng, o=t0, a=SM[:, 0:1]:
                     eng.tensor_scalar_mul(o, a, 1.0 / D),
                     reads=[rSM], writes=[rSM])
            self.vop(lambda eng, o=t1, m=mean: eng.tensor_mul(o, m, m),
                     reads=[self.MEAN[1]], writes=[rSM])
            self.vop(lambda eng, o=t0, a=t0, b=t1: eng.tensor_sub(o, a, b),
                     reads=[rSM], writes=[rSM])
            self.act(t1, t0, AF.Sqrt, bias=self.constv(1e-5),
                     reads=[rSM, self.CONSTV[1]], writes=[rSM])
            self.vop(lambda eng, o=rstd, a=t1: eng.reciprocal(o, a),
                     reads=[rSM], writes=[self.RSTD[1]])
            if want_msk:
                z = SM[:, 3:4]
                self.vop(lambda eng, o=z, a=X[:, i * 1024:i * 1024 + 1],
                         m=mean, r=rstd:
                         eng.tensor_scalar(o, a, m, r, ALU.subtract, ALU.mult),
                         reads=[rX, self.MEAN[1], self.RSTD[1]], writes=[rSM])
                self.vop(lambda eng, o=self.MSK[0][:, i:i + 1], a=z:
                         eng.tensor_scalar(o, a, 0.0, NEG, ALU.is_equal,
                                           ALU.mult),
                         reads=[rSM], writes=[self.MSK[1]])
            if inner_cm is None:
                self.vop(lambda eng, o=xnb, a=xi, m=mean, r=rstd:
                         eng.tensor_scalar(o, a, m, r, ALU.subtract, ALU.mult),
                         reads=[rX, self.MEAN[1], self.RSTD[1]],
                         writes=[rxnb])
            else:
                xf = dump[:, 0:1024]
                self.vop(lambda eng, o=xf, a=xi, m=mean, r=rstd:
                         eng.tensor_scalar(o, a, m, r, ALU.subtract, ALU.mult),
                         reads=[rX, self.MEAN[1], self.RSTD[1]],
                         writes=[rdump])
                s2, q2 = SM[:, 4:5], SM[:, 5:6]
                m2, r2, t2 = SM[:, 6:7], SM[:, 7:8], SM[:, 8:9]
                self.vop(lambda eng, o=s2, a=xf:
                         eng.reduce_sum(o, a, axis=AX.X),
                         reads=[rdump], writes=[rSM])
                self.act(self.SCRB[0][:, 1024:2048], xf, AF.Square,
                         reads=[rdump], writes=[rxnb, rSM], accum_out=q2)
                self.vop(lambda eng, o=m2, a=s2:
                         eng.tensor_scalar_mul(o, a, 1.0 / D),
                         reads=[rSM], writes=[rSM])
                self.vop(lambda eng, o=q2, a=q2:
                         eng.tensor_scalar_mul(o, a, 1.0 / D),
                         reads=[rSM], writes=[rSM])
                self.vop(lambda eng, o=t2, m=m2: eng.tensor_mul(o, m, m),
                         reads=[rSM], writes=[rSM])
                self.vop(lambda eng, o=q2, a=q2, b=t2: eng.tensor_sub(o, a, b),
                         reads=[rSM], writes=[rSM])
                self.act(t2, q2, AF.Sqrt, bias=self.constv(1e-5),
                         reads=[rSM, self.CONSTV[1]], writes=[rSM])
                self.vop(lambda eng, o=r2, a=t2: eng.reciprocal(o, a),
                         reads=[rSM], writes=[rSM])
                self.vop(lambda eng, o=xnb, a=xf, m=m2, r=r2:
                         eng.tensor_scalar(o, a, m, r, ALU.subtract, ALU.mult),
                         reads=[rdump, rSM], writes=[rxnb])
            pap, preg = self.ps(self.ps_rr())
            pbf = pap.bitcast(BF16)
            items = [(pbf[:, c * 128:(c + 1) * 128],
                      xnb[:, c * 128:(c + 1) * 128]) for c in range(8)]
            self.transpose_group(items, reads=[rxnb], writes=[preg])
            if inner_cm is None:
                self.vop(lambda eng, o=_chunks_out(dst, i), a=pbf[:, 0:1024]:
                         eng.tensor_copy(o, a),
                         reads=[preg], writes=[dst_reg])
            else:
                gcm, bcm = inner_cm
                for c in range(8):
                    self.vop(lambda eng,
                             o=dst[:, c * 2048 + i * 128: c * 2048 + (i + 1) * 128],
                             a=pbf[:, c * 128:(c + 1) * 128],
                             g=gcm[:, c:c + 1], b=bcm[:, c:c + 1]:
                             eng.tensor_scalar(o, a, g, b, ALU.mult, ALU.add),
                             reads=[preg, rSM], writes=[dst_reg])


def _chunks_out(dst_handle, i):
    a = dst_handle[:, :]
    return AP(a.tensor, i * 128, [[a.ap[0][0], 128], [2048, 8], [1, 128]])




class AttnCfg:
    def __init__(self, name, wq, wk, wv, wo, wcols=1024, bq=None, bk=None,
                 bv=None, bo=None, scale=1.0, causal=False, tbl=None,
                 colmask=False, qcol0=0, kcol0=0, vcol0=0):
        self.name = name
        self.wq, self.wk, self.wv, self.wo = wq, wk, wv, wo
        self.wcols = wcols
        self.bq, self.bk, self.bv, self.bo = bq, bk, bv, bo
        self.scale = scale
        self.causal = causal
        self.tbl = tbl
        self.colmask = colmask
        self.qcol0, self.kcol0, self.vcol0 = qcol0, kcol0, vcol0


def _mk_group(items):
    out = []
    for i, (o, l, r) in enumerate(items):
        out.append((o, l, r, i == 0, i == len(items) - 1))
    return out


class _AttnMixin:
    def brow_reset(self):
        self._brow_pos = 0

    def load_row_bf(self, dram_name, n, slot=None):
        pos = self._brow_pos if slot is None else slot * 1024
        if slot is None:
            assert pos + n <= 2048, "BROW overflow"
            self._brow_pos = pos + n
        r32 = self.SCR32[0][0:1, 0:n]
        rb = self.BROW[0][0:1, pos:pos + n]
        self.p.dma("sync", r32, self.dram[dram_name][0:1, 0:n],
                   reads=[self.SCR32[1]], writes=[self.SCR32[1]])
        self.vop(lambda eng, o=rb, i=r32: eng.tensor_copy(o, i),
                 reads=[self.SCR32[1]], writes=[self.BROW[1]])
        return rb, self.BROW[1]

    def emit_qT_proj(self, dst, w_name, col0, bias, wcols, src=None,
                     ndout=8, nkc=8, dcol0=0):
        dstT, rdst = dst
        TA, rTA = src if src is not None else self.TA
        wbf = self.dram[w_name + "_bf"].tensor
        for dc in range(ndout):
            slot = self.WS[dc % 4]
            self.wdma(dc, slot[0][:, 0:nkc * 128],
                      AP(wbf, col0 + dc * 128,
                         [[wcols, 128], [128 * wcols, nkc], [1, 128]]),
                      reads=[slot[1]], writes=[slot[1]])
            for nt4 in range(4):
                pap, preg = self.ps(self.ps_rr())
                items = [(pap, slot[0][:, kc * 128:(kc + 1) * 128],
                          TA[:, kc * 2048 + nt4 * 512: kc * 2048 + nt4 * 512 + 512])
                         for kc in range(nkc)]
                reads = [slot[1], rTA]
                if bias is not None:
                    items.append((pap, bias[0][0:1, dc * 128:dc * 128 + 128],
                                  self.ONE[0][0:1, 0:512]))
                    reads += [bias[1], self.ONE[1]]
                self.mm_group(_mk_group(items), reads=reads, writes=[preg])
                self.act(dstT[:, dcol0 + dc * 2048 + nt4 * 512:
                              dcol0 + dc * 2048 + nt4 * 512 + 512],
                         pap, AF.Copy, reads=[preg], writes=[rdst])

    def emit_v_to_vs(self, cfg, src=None, hd=64, vtot=1024, nkc=8):
        TA, rTA = src if src is not None else self.TA
        wbf = self.dram[cfg.wv + "_bf"].tensor
        wcols = cfg.wcols
        bias = self._battn.get("v")
        nvs = vtot // 512
        hd1 = hd + 1
        nh_per_vs = 512 // hd
        for ntg in range(4):
            psl = [[self.ps_rr() for _ in range(4)] for _ in range(nvs)]
            for vs in range(nvs):
                for kc2 in range(nkc // 2):
                    slot = self.WS[kc2 % 4]
                    self.wdma(kc2, slot[0][:, 0:1024],
                              AP(wbf, cfg.vcol0 + vs * 512 + kc2 * 256 * wcols,
                                 [[wcols, 128], [128 * wcols, 2], [1, 512]]),
                              reads=[slot[1]], writes=[slot[1]])
                    for kh in range(2):
                        kc = kc2 * 2 + kh
                        for i in range(4):
                            nt = ntg * 4 + i
                            pap, preg = self.ps(psl[vs][i])
                            its = [(pap, TA[:, kc * 2048 + nt * 128:
                                            kc * 2048 + nt * 128 + 128],
                                    slot[0][:, kh * 512:kh * 512 + 512],
                                    kc == 0,
                                    kc == nkc - 1 and bias is None)]
                            self.mm_group(its, reads=[slot[1], rTA],
                                          writes=[preg])
                if bias is not None:
                    for i in range(4):
                        pap, preg = self.ps(psl[vs][i])
                        its = [(pap, self.ONE[0][0:1, 0:128],
                                bias[0][0:1, vs * 512:vs * 512 + 512],
                                False, True)]
                        self.mm_group(its, reads=[self.ONE[1], bias[1]],
                                      writes=[preg])
            stw = 16 * hd1
            for i in range(4):
                nt = ntg * 4 + i
                st = self.SCRB[0][:, (i % 2) * stw:(i % 2) * stw + stw]
                rst = self.SCRB[1]
                part = st.ap[0][0]
                ones_ap = AP(st.tensor, st.offset, [[part, 128], [hd1, 16], [1, 1]])
                self.vop(lambda eng, o=ones_ap: eng.memset(o, 1.0),
                         reads=[rst], writes=[rst])
                for vs in range(nvs):
                    pap, preg = self.ps(psl[vs][i])
                    out_ap = AP(st.tensor, st.offset + hd1 * nh_per_vs * vs + 1,
                                [[part, 128], [hd1, nh_per_vs], [1, hd]])
                    self.act(out_ap, pap, AF.Copy, reads=[preg], writes=[rst])
                vs_t = self.dram["vs"].tensor
                dst_ap = AP(vs_t, nt * 128 * hd1,
                            [[hd1, 128], [SEQ * hd1, 16], [1, hd1]])
                self.p.dma("sync", dst_ap, st, reads=[rst],
                           writes=[self.rVS])

    def emit_full_attn(self, cfg, inner_ln=False):
        p = self.p
        self.rO = self.TA[1]
        self.brow_reset()
        self._battn = {}
        if cfg.bq is not None:
            self._battn["q"] = self.load_row_bf(cfg.bq, 1024, slot=0)
        if cfg.bk is not None:
            self._battn["k"] = self.load_row_bf(cfg.bk, 1024, slot=1)
        if inner_ln:
            gcm, bcm = self.load_cm_pair(cfg.inner_gamma, cfg.inner_beta)
            self.emit_ln(self.TA[0], self.TA[1], inner_cm=(gcm, bcm))
        else:
            self.emit_ln(self.TA[0], self.TA[1], want_msk=cfg.colmask)
        self._dbg("xnT", self.TA[0][:, :], self.TA[1])
        self._dbg("MEAN", self.MEAN[0][:, :], self.MEAN[1])
        self._dbg("RSTD", self.RSTD[0][:, :], self.RSTD[1])
        self.emit_qT_proj(self.TB, cfg.wq, cfg.qcol0, self._battn.get("q"),
                          cfg.wcols)
        self.emit_qT_proj(self.TC, cfg.wk, cfg.kcol0, self._battn.get("k"),
                          cfg.wcols)
        self._dbg("qT", self.TB[0][:, :], self.TB[1])
        self._dbg("kT", self.TC[0][:, :], self.TC[1])
        if cfg.bv is not None:
            self._battn["v"] = self.load_row_bf(cfg.bv, 1024, slot=0)
        self.emit_v_to_vs(cfg)
        self.emit_attn_core(cfg, (self.TB, self.TB[1]), (self.TC, self.TC[1]),
                            self.TA[0], self.rO, 0, 1024, 64)
        if cfg.bo is not None:
            self._battn["o"] = self.load_row_bf(cfg.bo, 1024, slot=1)
        self.emit_out_proj(cfg.wo, self._battn.get("o"), self.TA, self.rO, 1024)

    def emit_out_proj(self, wo_name, bias, src, rsrc, src_d, src_is_T=False):
        wbf = self.dram[wo_name + "_bf"].tensor
        ndc = src_d // 128
        srcT, _ = src
        TD = self.TD[0]
        for qtg in range(4):
            if not src_is_T:
                for i in range(4):
                    qt = qtg * 4 + i
                    pap, preg = self.ps(self.ps_rr())
                    pbf = pap.bitcast(BF16)
                    items = [(pbf[:, c * 128:(c + 1) * 128],
                              srcT[:, qt * src_d + c * 128: qt * src_d + (c + 1) * 128])
                             for c in range(ndc)]
                    self.transpose_group(items, reads=[rsrc], writes=[preg])
                    self.vop(lambda eng, o=TD[:, i * 2048:i * 2048 + ndc * 128],
                             a=pbf[:, 0:ndc * 128]:
                             eng.tensor_copy(o, a),
                             reads=[preg],
                             writes=self.td_regs(i * 2048,
                                                 i * 2048 + ndc * 128))

            def lhs_tile(i, dc):
                if src_is_T:
                    qt = qtg * 4 + i
                    return srcT[:, dc * 2048 + qt * 128: dc * 2048 + qt * 128 + 128]
                return TD[:, i * 2048 + dc * 128: i * 2048 + (dc + 1) * 128]
            pass
            for ds in range(2):
                for dc2 in range((ndc + 1) // 2):
                    slot = self.WS[dc2 % 4]
                    nld = min(2, ndc - dc2 * 2)
                    self.wdma(dc2, slot[0][:, 0:512 * nld],
                              AP(wbf, ds * 512 + dc2 * 256 * 1024,
                                 [[1024, 128], [128 * 1024, nld], [1, 512]]),
                              reads=[slot[1]], writes=[slot[1]])
                    for dh in range(nld):
                        dc = dc2 * 2 + dh
                        for i in range(4):
                            pap, preg = self.ps((4 + i) if ds == 0 else i)
                            lrg = [rsrc] if src_is_T else list(self.td_regs(
                                i * 2048 + dc * 128, i * 2048 + (dc + 1) * 128))
                            its = [(pap, lhs_tile(i, dc),
                                    slot[0][:, dh * 512:dh * 512 + 512],
                                    dc == 0,
                                    dc == ndc - 1 and bias is None)]
                            self.mm_group(its, reads=[slot[1]] + lrg,
                                          writes=[preg])
                for i in range(4):
                    pap, preg = self.ps((4 + i) if ds == 0 else i)
                    if bias is not None:
                        its = [(pap, self.ONE[0][0:1, 0:128],
                                bias[0][0:1, ds * 512:ds * 512 + 512],
                                False, True)]
                        self.mm_group(its, reads=[self.ONE[1], bias[1]],
                                      writes=[preg])
                    qt = qtg * 4 + i
                    xs = self.X[0][:, qt * 1024 + ds * 512: qt * 1024 + ds * 512 + 512]
                    self.vop(lambda eng, o=xs, a=pap, b=xs:
                             eng.tensor_add(o, a, b),
                             reads=[preg, self.X[1]], writes=[self.X[1]])

    def emit_attn_core(self, cfg, qsrc, ksrc, obuf, oreg, obase, ostride, ohd):
        TB, rTB = qsrc[0][0], qsrc[1]
        TC, rTC = ksrc[0][0], ksrc[1]
        TD = self.TD[0]
        hd1 = ohd + 1
        gt0 = self.SCRB[0][:, 0:TBL_W]
        rgt = self.SCRB[1]
        sc = 1.0 if cfg.scale == 1.0 else self.constv(cfg.scale)
        kts_of = [[kt for kt in range(16)
                   if not cfg.causal or kt * 128 <= qb * 512 + 511]
                  for qb in range(4)]
        vh = [self.WS[0], self.WS[1]]
        ar = self._att_regs
        mmb = [(self.MM[0][:, 0:8], ar[0]), (self.MM[0][:, 8:16], ar[1])]
        mm2 = [(self.MMB[0][:, 0:8], ar[2]), (self.MMB[0][:, 8:16], ar[3])]
        mrow = [(self.MROW[0][0:1, :], self.MROW[1]),
                (self.SCR32[0][0:1, 0:1024].bitcast(BF16), self.SCR32[1])]

        def emit_head_prep(h):
            p = h % 2
            hb = 64 * (h % 2)
            hc = (h // 2) * 2048
            if cfg.tbl == "garr":
                self.p.dma("sync", gt0,
                           AP(self.dram["garr"].tensor, h * NGARR,
                              [[1, 128], [1, TBL_W]]),
                           reads=[rgt], writes=[rgt])
            elif cfg.tbl == "gcaus" and h == 0:
                self.p.dma("sync", gt0,
                           AP(self.dram["c_gcaus"].tensor, 0,
                              [[1, 128], [1, TBL_W]]),
                           reads=[rgt], writes=[rgt])
            self.p.dma("sync", vh[p][0][:, 0:16 * hd1],
                       AP(self.dram["vs"].tensor, h * SEQ * hd1,
                          [[hd1, 128], [hd1 * 128, 16], [1, hd1]]),
                       reads=[vh[p][1], self.rVS], writes=[vh[p][1]])
            mreg = mmb[p][1]
            for qt in range(16):
                nks = (qt // 4 + 1) if cfg.causal else 4
                half = (qt % 2) * 2
                pt = self.PP[:, half * 512:(half + 2) * 512]
                prs = self.pr[half:half + 2]
                ncols = nks * 512
                items = []
                for ks in range(min(nks, 2)):
                    items.append((pt[:, ks * 512:(ks + 1) * 512],
                                  TB[hb:hb + 64, hc + qt * 128: hc + qt * 128 + 128],
                                  TC[hb:hb + 64, hc + ks * 512: hc + ks * 512 + 512],
                                  True, True))
                self.mm_group(items, reads=[rTB, rTC], writes=prs)
                self.vop(lambda eng, o=mmb[p][0][:, qt % 8: qt % 8 + 1],
                         a=pt[:, 0:min(ncols, 1024)]:
                         eng.reduce_max(o, a, axis=AX.X, negate=True),
                         reads=prs, writes=[mreg])
                if nks > 2:
                    half2 = ((qt + 1) % 2) * 2
                    pt2 = self.PP[:, half2 * 512:(half2 + 2) * 512]
                    prs2 = self.pr[half2:half2 + 2]
                    items = []
                    for ks in range(2, nks):
                        items.append((pt2[:, (ks - 2) * 512:(ks - 1) * 512],
                                      TB[hb:hb + 64, hc + qt * 128: hc + qt * 128 + 128],
                                      TC[hb:hb + 64, hc + ks * 512: hc + ks * 512 + 512],
                                      True, True))
                    self.mm_group(items, reads=[rTB, rTC], writes=prs2)
                    t2 = self.SMALL[0][:, 11:12]
                    self.vop(lambda eng, o=t2,
                             a=pt2[:, 0:(nks - 2) * 512]:
                             eng.reduce_max(o, a, axis=AX.X, negate=True),
                             reads=prs2, writes=[self.SMALL[1]])
                    self.vop(lambda eng, o=mmb[p][0][:, qt % 8: qt % 8 + 1],
                             a=mmb[p][0][:, qt % 8: qt % 8 + 1], b=t2:
                             eng.tensor_max(o, a, b),
                             reads=[self.SMALL[1], mreg], writes=[mreg])
                if qt % 8 == 7:
                    self.vop(lambda eng, o=mm2[p][0], a=mmb[p][0]:
                             eng.tensor_copy(o, a),
                             reads=[mreg], writes=[mm2[p][1]])
                    pap, preg = self.ps(self.ps_rr(lo=4, hi=8))
                    pbf = pap.bitcast(BF16)
                    items = [(pbf[0:1, i * 128:(i + 1) * 128],
                              mm2[p][0][:, i:i + 1]) for i in range(8)]
                    self.transpose_group(items, reads=[mm2[p][1]],
                                         writes=[preg])
                    base = (qt - 7) * 128
                    self.vop(lambda eng,
                             o=mrow[p][0][0:1, base:base + 1024],
                             a=pbf[0:1, 0:1024]:
                             eng.tensor_copy(o, a),
                             reads=[preg], writes=[mrow[p][1]])

        def emit_pb(h, qb, kts):
            p = h % 2
            hb = 64 * (h % 2)
            hc = (h // 2) * 2048
            for kt in kts:
                pap, preg = self.ps(self.ps_rr(lo=4, hi=8))
                items = [(pap,
                          TC[hb:hb + 64, hc + kt * 128: hc + kt * 128 + 128],
                          TB[hb:hb + 64, hc + qb * 512: hc + qb * 512 + 512]),
                         (pap, self.ONE[0][0:1, 0:128],
                          mrow[p][0][0:1, qb * 512:(qb + 1) * 512])]
                reads = [rTB, rTC, self.ONE[1], mrow[p][1]]
                if cfg.tbl is not None:
                    c = qb * 512 - kt * 128 - DMIN
                    items.append((pap, self.JX[0][:, :], gt0[:, c:c + 512]))
                    reads += [self.JX[1], rgt]
                self.mm_group(_mk_group([(o, l, r) for (o, l, r) in items]),
                              reads=reads, writes=[preg])
                bias_ap = self.MSK[0][:, kt:kt + 1] if cfg.colmask else 0.0
                rd = [preg, self.CONSTV[1]] + \
                    ([self.MSK[1]] if cfg.colmask else [])
                self.act(TD[:, kt * 512:(kt + 1) * 512], pap, AF.Exp,
                         bias=bias_ap, scale=sc,
                         reads=rd, writes=[self.TDr[kt]])

        def emit_av(h, qb, kts):
            p = h % 2
            for i in range(4):
                qt = qb * 4 + i
                pap, preg = self.ps(i)
                items = [(pap[:, 0:hd1],
                          TD[:, kt * 512 + i * 128: kt * 512 + i * 128 + 128],
                          vh[p][0][:, kt * hd1:(kt + 1) * hd1]) for kt in kts]
                self.mm_group(_mk_group(items),
                              reads=[self.TDr[kt] for kt in kts] + [vh[p][1]],
                              writes=[preg])
                rr = self.SMALL[0][:, 9:10]
                self.vop(lambda eng, o=rr, a=pap[:, 0:1]:
                         eng.reciprocal(o, a),
                         reads=[preg], writes=[self.SMALL[1]])
                self.vop(lambda eng,
                         o=obuf[:, obase + qt * ostride + ohd * h:
                                obase + qt * ostride + ohd * h + ohd],
                         a=pap[:, 1:hd1], s=rr:
                         eng.tensor_scalar_mul(o, a, s),
                         reads=[preg, self.SMALL[1]], writes=[oreg])

        emit_head_prep(0)
        for h in range(16):
            for qb in range(4):
                emit_pb(h, qb, kts_of[qb])
                if qb == 1 and h + 1 < 16:
                    emit_head_prep(h + 1)
                emit_av(h, qb, kts_of[qb])

    def emit_router_gates(self, src, rsrc, w_name, E, plus_eps):
        wslot = self.WS[0]
        wbf = self.dram[w_name + "_bf"].tensor
        self.wdma(0, wslot[0][:, 0:8 * E],
                  AP(wbf, 0, [[E, 128], [128 * E, 8], [1, E]]),
                  reads=[wslot[1]], writes=[wslot[1]])
        SM, rSM = self.SMALL[0], self.SMALL[1]
        probs = self.SCR32[0][:, 0:16]
        rprob = self.SCR32[1]
        for nt in range(NT):
            pap, preg = self.ps(self.ps_rr())
            items = [(pap[:, 0:E], src[:, kc * 2048 + nt * 128: kc * 2048 + nt * 128 + 128],
                      wslot[0][:, kc * E:(kc + 1) * E]) for kc in range(8)]
            self.mm_group(_mk_group(items), reads=[rsrc, wslot[1]],
                          writes=[preg])
            nm = SM[:, 10:11]
            self.vop(lambda eng, o=nm, a=pap[:, 0:E]:
                     eng.reduce_max(o, a, axis=AX.X, negate=True),
                     reads=[preg], writes=[rSM])
            se_ = SM[:, 11:12]
            self.act(probs[:, 0:E], pap[:, 0:E], AF.Exp, bias=nm,
                     reads=[preg, rSM], writes=[rprob, rSM], accum_out=se_)
            rs = SM[:, 12:13]
            self.vop(lambda eng, o=rs, a=se_: eng.reciprocal(o, a),
                     reads=[rSM], writes=[rSM])
            self.vop(lambda eng, o=probs[:, 0:E], a=probs[:, 0:E], s=rs:
                     eng.tensor_scalar_mul(o, a, s),
                     reads=[rprob, rSM], writes=[rprob])
            m1 = SM[:, 13:14]
            self.vop(lambda eng, o=m1, a=probs[:, 0:E]:
                     eng.reduce_max(o, a, axis=AX.X),
                     reads=[rprob], writes=[rSM])
            ind1 = self.SCR32[0][:, 16:32]
            self.vop(lambda eng, o=ind1[:, 0:E], a=probs[:, 0:E], s=m1:
                     eng.tensor_single_scalar(o, a, s, ALU.is_ge),
                     reads=[rprob, rSM], writes=[rprob])
            pm = self.SCR32[0][:, 32:48]
            self.vop(lambda eng, o=pm[:, 0:E], a=ind1[:, 0:E], b=probs[:, 0:E]:
                     eng.scalar_tensor_tensor(o, a, -1e9, b, ALU.mult, ALU.add),
                     reads=[rprob], writes=[rprob])
            m2 = SM[:, 14:15]
            self.vop(lambda eng, o=m2, a=pm[:, 0:E]:
                     eng.reduce_max(o, a, axis=AX.X),
                     reads=[rprob], writes=[rSM])
            den = SM[:, 15:16]
            self.vop(lambda eng, o=den, a=m1, b=m2: eng.tensor_add(o, a, b),
                     reads=[rSM], writes=[rSM])
            if plus_eps:
                self.vop(lambda eng, o=den, a=den:
                         eng.tensor_scalar_add(o, a, 1e-9),
                         reads=[rSM], writes=[rSM])
            rw = SM[:, 10:11]
            self.vop(lambda eng, o=rw, a=den: eng.reciprocal(o, a),
                     reads=[rSM], writes=[rSM])
            gcol = self.GTS[0][:, nt * 16: nt * 16 + E]
            self.vop(lambda eng, o=gcol, a=probs[:, 0:E], s=m2:
                     eng.tensor_single_scalar(o, a, s, ALU.is_ge),
                     reads=[rprob, rSM], writes=[self.GTS[1]])
            self.vop(lambda eng, o=gcol, a=gcol, b=probs[:, 0:E]:
                     eng.tensor_mul(o, a, b),
                     reads=[rprob, self.GTS[1]], writes=[self.GTS[1]])
            self.vop(lambda eng, o=gcol, a=gcol, s=rw:
                     eng.tensor_scalar_mul(o, a, s),
                     reads=[rSM, self.GTS[1]], writes=[self.GTS[1]])

    def emit_expert(self, src, rsrc, wg_t, wu_t, wd_t, base_gu, base_d,
                    Fdim, gate_col, silu_on_g=True):
        TC, rTC = self.TC
        nF = Fdim // 128
        win = 4 if nF <= 8 else 2
        for w0 in range(0, 4, win):
            for fc in range(nF):
                sg = self.WS[(2 * fc) % 4]
                su = self.WS[(2 * fc + 1) % 4]
                self.wdma(0, sg[0][:, 0:1024],
                          AP(wg_t, base_gu + fc * 128,
                             [[Fdim, 128], [128 * Fdim, 8], [1, 128]]),
                          reads=[sg[1]], writes=[sg[1]])
                self.wdma(1, su[0][:, 0:1024],
                          AP(wu_t, base_gu + fc * 128,
                             [[Fdim, 128], [128 * Fdim, 8], [1, 128]]),
                          reads=[su[1]], writes=[su[1]])
                for tw in range(win):
                    tt4 = w0 + tw
                    pg, prg = self.ps(self.ps_rr())
                    pu, pru = self.ps(self.ps_rr())
                    gi = [(pg, sg[0][:, kc * 128:(kc + 1) * 128],
                           src[:, kc * 2048 + tt4 * 512: kc * 2048 + tt4 * 512 + 512])
                          for kc in range(8)]
                    self.mm_group(_mk_group(gi), reads=[sg[1], rsrc],
                                  writes=[prg])
                    ui = [(pu, su[0][:, kc * 128:(kc + 1) * 128],
                           src[:, kc * 2048 + tt4 * 512: kc * 2048 + tt4 * 512 + 512])
                          for kc in range(8)]
                    self.mm_group(_mk_group(ui), reads=[su[1], rsrc],
                                  writes=[pru])
                    stile = self.SCRB[0][:, (tw % 2) * 512:(tw % 2) * 512 + 512]
                    silu_src = pg if silu_on_g else pu
                    silu_reg = prg if silu_on_g else pru
                    other = pu if silu_on_g else pg
                    other_reg = pru if silu_on_g else prg
                    self.act(stile, silu_src, AF.Silu,
                             reads=[silu_reg], writes=[self.SCRB[1]])
                    self.vop(lambda eng,
                             o=TC[:, fc * (win * 512) + tw * 512:
                                  fc * (win * 512) + tw * 512 + 512],
                             a=other, b=stile:
                             eng.tensor_mul(o, a, b),
                             reads=[other_reg, self.SCRB[1]], writes=[rTC])
            for tb in range(win):
                tt4 = w0 + tb
                for ds in range(2):
                    psl = [self.ps_rr() for _ in range(4)]
                    for fc2 in range((nF + 1) // 2):
                        slot = self.WS[fc2 % 4]
                        nld = min(2, nF - fc2 * 2)
                        self.wdma(fc2, slot[0][:, 0:512 * nld],
                                  AP(wd_t, base_d + ds * 512 + fc2 * 256 * 1024,
                                     [[1024, 128], [128 * 1024, nld], [1, 512]]),
                                  reads=[slot[1]], writes=[slot[1]])
                        for fh in range(nld):
                            fc = fc2 * 2 + fh
                            for i in range(4):
                                pap, preg = self.ps(psl[i])
                                its = [(pap,
                                        TC[:, fc * (win * 512) + tb * 512 + i * 128:
                                           fc * (win * 512) + tb * 512 + i * 128 + 128],
                                        slot[0][:, fh * 512:fh * 512 + 512],
                                        fc == 0, fc == nF - 1)]
                                self.mm_group(its, reads=[slot[1], rTC],
                                              writes=[preg])
                    for i in range(4):
                        nt = tt4 * 4 + i
                        pap, preg = self.ps(psl[i])
                        xs = self.X[0][:, nt * 1024 + ds * 512:
                                       nt * 1024 + ds * 512 + 512]
                        if gate_col is None:
                            self.vop(lambda eng, o=xs, a=pap, b=xs:
                                     eng.tensor_add(o, a, b),
                                     reads=[preg, self.X[1]],
                                     writes=[self.X[1]])
                        else:
                            gc = self.GTS[0][:, nt * 16 + gate_col:
                                             nt * 16 + gate_col + 1]
                            self.vop(lambda eng, o=xs, a=pap, s=gc, b=xs:
                                     eng.scalar_tensor_tensor(
                                         o, a, s, b, ALU.mult, ALU.add),
                                     reads=[preg, self.X[1], self.GTS[1]],
                                     writes=[self.X[1]])

    def emit_ln_rms(self, dst, dst_reg, g1cm):
        X, rX = self.X[0], self.X[1]
        dump, rdump = self.SCR32[0], self.SCR32[1]
        xnb = self.SCRB[0][:, 0:1024]
        hb = self.SCRB[0][:, 1024:2048]
        rxnb = self.SCRB[1]
        SM, rSM = self.SMALL[0], self.SMALL[1]
        for i in range(NT):
            xi = X[:, i * 1024:(i + 1) * 1024]
            self.act(dump[:, 0:1024], xi, AF.Copy,
                     reads=[rX], writes=[rdump, rSM], accum_out=SM[:, 0:1])
            self.act(dump[:, 0:1024], xi, AF.Square,
                     reads=[rX], writes=[rdump, rSM], accum_out=SM[:, 1:2])
            mean = self.MEAN[0][:, i:i + 1]
            rstd = self.RSTD[0][:, i:i + 1]
            t0, t1 = SM[:, 2:3], SM[:, 3:4]
            self.vop(lambda eng, o=mean, a=SM[:, 0:1]:
                     eng.tensor_scalar_mul(o, a, 1.0 / D),
                     reads=[rSM], writes=[self.MEAN[1]])
            self.vop(lambda eng, o=t0, a=SM[:, 1:2]:
                     eng.tensor_scalar_mul(o, a, 1.0 / D),
                     reads=[rSM], writes=[rSM])
            self.vop(lambda eng, o=t1, m=mean: eng.tensor_mul(o, m, m),
                     reads=[self.MEAN[1]], writes=[rSM])
            self.vop(lambda eng, o=t0, a=t0, b=t1: eng.tensor_sub(o, a, b),
                     reads=[rSM], writes=[rSM])
            self.act(t1, t0, AF.Sqrt, bias=self.constv(1e-5),
                     reads=[rSM, self.CONSTV[1]], writes=[rSM])
            self.vop(lambda eng, o=rstd, a=t1: eng.reciprocal(o, a),
                     reads=[rSM], writes=[self.RSTD[1]])
            self.vop(lambda eng, o=xnb, a=xi, m=mean, r=rstd:
                     eng.tensor_scalar(o, a, m, r, ALU.subtract, ALU.mult),
                     reads=[rX, self.MEAN[1], self.RSTD[1]],
                     writes=[rxnb])
            self.act(dump[:, 0:1024], xnb, AF.Square,
                     reads=[rxnb], writes=[rdump, rSM], accum_out=SM[:, 4:5])
            self.vop(lambda eng, o=SM[:, 4:5], a=SM[:, 4:5]:
                     eng.tensor_scalar_mul(o, a, 1.0 / D),
                     reads=[rSM], writes=[rSM])
            self.act(SM[:, 5:6], SM[:, 4:5], AF.Sqrt, bias=self.constv(1e-5),
                     reads=[rSM, self.CONSTV[1]], writes=[rSM])
            self.vop(lambda eng, o=SM[:, 4:5], a=SM[:, 5:6]:
                     eng.reciprocal(o, a),
                     reads=[rSM], writes=[rSM])
            self.vop(lambda eng, o=hb, a=xnb, s=SM[:, 4:5]:
                     eng.tensor_scalar_mul(o, a, s),
                     reads=[rxnb, rSM], writes=[rxnb])
            pap, preg = self.ps(self.ps_rr())
            pbf = pap.bitcast(BF16)
            items = [(pbf[:, c * 128:(c + 1) * 128],
                      hb[:, c * 128:(c + 1) * 128]) for c in range(8)]
            self.transpose_group(items, reads=[rxnb], writes=[preg])
            for c in range(8):
                self.vop(lambda eng,
                         o=dst[:, c * 2048 + i * 128: c * 2048 + (i + 1) * 128],
                         a=pbf[:, c * 128:(c + 1) * 128], g=g1cm[:, c:c + 1]:
                         eng.tensor_scalar_mul(o, a, g),
                         reads=[preg, rSM], writes=[dst_reg])

    def emit_latent_qk(self, dst, dst_reg, w_name):
        TC, rTC = self.TC
        wbf = self.dram[w_name + "_bf"].tensor
        cos_t = self.dram["c_cos"].tensor
        sin_t = self.dram["c_sin"].tensor
        stage = self.SCRB[0][:, 0:1024]
        rstage = self.SCRB[1]
        cosb = self.SCR32[0][:, 0:256]
        sinb = self.SCR32[0][:, 256:512]
        tmpa = self.SCR32[0][:, 512:768]
        tmpb = self.SCR32[0][:, 768:1024]
        r32 = self.SCR32[1]
        for nt in range(NT):
            self.p.dma("sync", cosb,
                       AP(cos_t, nt * 128 * 16, [[16, 128], [0, 16], [1, 16]]),
                       reads=[r32], writes=[r32])
            self.p.dma("sync", sinb,
                       AP(sin_t, nt * 128 * 16, [[16, 128], [0, 16], [1, 16]]),
                       reads=[r32], writes=[r32])
            for vs in range(2):
                base = ((nt * 2 + vs) % 2) * 2
                for half in range(2):
                    slot = self.WS[base + half]
                    self.wdma(half, slot[0][:, 0:1024],
                              AP(wbf, vs * 512 + half * 2 * 128 * 1024,
                                 [[1024, 128], [128 * 1024, 2], [1, 512]]),
                              reads=[slot[1]], writes=[slot[1]])
                pap, preg = self.ps(self.ps_rr())
                its = [(pap,
                        TC[:, kc * 2048 + nt * 128: kc * 2048 + nt * 128 + 128],
                        self.WS[base + kc // 2][0][:, (kc % 2) * 512:(kc % 2) * 512 + 512])
                       for kc in range(4)]
                self.mm_group(_mk_group(its),
                              reads=[self.WS[base][1], self.WS[base + 1][1], rTC],
                              writes=[preg])
                self.act(stage[:, vs * 512:(vs + 1) * 512], pap, AF.Copy,
                         reads=[preg], writes=[rstage])
            st = stage
            part = st.ap[0][0]
            def hap(base, off):
                return AP(st.tensor, st.offset + off, [[part, 128], [64, 16], [1, 16]])
            t1 = hap(st, 0)
            t2 = hap(st, 32)
            ca = AP(cosb.tensor, cosb.offset, [[cosb.ap[0][0], 128], [0, 16], [1, 16]])
            sa = AP(sinb.tensor, sinb.offset, [[sinb.ap[0][0], 128], [0, 16], [1, 16]])
            ta = AP(tmpa.tensor, tmpa.offset, [[tmpa.ap[0][0], 128], [16, 16], [1, 16]])
            tb = AP(tmpb.tensor, tmpb.offset, [[tmpb.ap[0][0], 128], [16, 16], [1, 16]])
            self.vop(lambda eng: eng.tensor_mul(ta, t1, sa),
                     reads=[rstage, r32], writes=[r32])
            self.vop(lambda eng: eng.tensor_mul(tb, t2, ca),
                     reads=[rstage, r32], writes=[r32])
            self.vop(lambda eng: eng.tensor_add(ta, ta, tb),
                     reads=[r32], writes=[r32])
            self.vop(lambda eng: eng.tensor_mul(tb, t2, sa),
                     reads=[rstage, r32], writes=[r32])
            self.vop(lambda eng: eng.tensor_copy(t2, ta),
                     reads=[r32], writes=[rstage])
            self.vop(lambda eng: eng.tensor_mul(ta, t1, ca),
                     reads=[rstage, r32], writes=[r32])
            self.vop(lambda eng: eng.tensor_sub(t1, ta, tb),
                     reads=[r32], writes=[rstage])
            pap, preg = self.ps(self.ps_rr())
            pbf = pap.bitcast(BF16)
            items = [(pbf[:, c * 128:(c + 1) * 128],
                      st[:, c * 128:(c + 1) * 128]) for c in range(8)]
            self.transpose_group(items, reads=[rstage], writes=[preg])
            self.vop(lambda eng, o=_chunks_out_h(dst, nt), a=pbf[:, 0:1024]:
                     eng.tensor_copy(o, a),
                     reads=[preg], writes=[dst_reg])

    def emit_l0f_up(self, g2cm):
        TC, rTC = self.TC
        TD = self.TD[0]
        TA, rTA = self.TA
        wbf = self.dram["l0f_wup_bf"].tensor
        for qtg in range(4):
            for i in range(4):
                qt = qtg * 4 + i
                pap, preg = self.ps(self.ps_rr())
                pbf = pap.bitcast(BF16)
                items = [(pbf[:, c * 128:(c + 1) * 128],
                          TC[:, 8192 + qt * 512 + c * 128: 8192 + qt * 512 + (c + 1) * 128])
                         for c in range(4)]
                self.transpose_group(items, reads=[rTC], writes=[preg])
                self.vop(lambda eng, o=TD[:, i * 2048:i * 2048 + 512],
                         a=pbf[:, 0:512]:
                         eng.tensor_copy(o, a),
                         reads=[preg],
                         writes=self.td_regs(i * 2048, i * 2048 + 512))
            for ds in range(2):
                psl = [self.ps_rr() for _ in range(4)]
                for dc2 in range(2):
                    slot = self.WS[dc2 % 4]
                    self.wdma(dc2, slot[0][:, 0:1024],
                              AP(wbf, ds * 512 + dc2 * 256 * 1024,
                                 [[1024, 128], [128 * 1024, 2], [1, 512]]),
                              reads=[slot[1]], writes=[slot[1]])
                    for dh in range(2):
                        dc = dc2 * 2 + dh
                        for i in range(4):
                            pap, preg = self.ps(psl[i])
                            its = [(pap,
                                    TD[:, i * 2048 + dc * 128: i * 2048 + (dc + 1) * 128],
                                    slot[0][:, dh * 512:dh * 512 + 512],
                                    dc == 0, dc == 3)]
                            self.mm_group(
                                its,
                                reads=[slot[1]] + list(self.td_regs(
                                    i * 2048 + dc * 128,
                                    i * 2048 + (dc + 1) * 128)),
                                writes=[preg])
                for i in range(4):
                    qt = qtg * 4 + i
                    pap, preg = self.ps(psl[i])
                    xs = self.X[0][:, qt * 1024 + ds * 512: qt * 1024 + ds * 512 + 512]
                    xnr = self.SCR32[0][:, 0:512]
                    mean = self.MEAN[0][:, qt:qt + 1]
                    rstd = self.RSTD[0][:, qt:qt + 1]
                    self.vop(lambda eng, o=xnr, a=xs, m=mean, r=rstd:
                             eng.tensor_scalar(o, a, m, r, ALU.subtract,
                                               ALU.mult),
                             reads=[self.X[1], self.MEAN[1], self.RSTD[1]],
                             writes=[self.SCR32[1]])
                    xfs = TA[:, qt * 1024 + ds * 512: qt * 1024 + ds * 512 + 512]
                    self.vop(lambda eng, o=xfs, a=pap, b=xnr:
                             eng.tensor_add(o, a, b),
                             reads=[preg, self.SCR32[1]], writes=[rTA])
                    self.vop(lambda eng, o=xs, a=pap, b=xs:
                             eng.tensor_add(o, a, b),
                             reads=[preg, self.X[1]], writes=[self.X[1]])
                    self.vop(lambda eng, o=xs, a=xnr, b=xs:
                             eng.tensor_add(o, a, b),
                             reads=[self.SCR32[1], self.X[1]],
                             writes=[self.X[1]])
        dump, rdump = self.SCR32[0], self.SCR32[1]
        SM, rSM = self.SMALL[0], self.SMALL[1]
        tb_ = self.SCRB[0][:, 0:1024]
        for i in range(NT):
            xfi = TA[:, i * 1024:(i + 1) * 1024]
            self.act(dump[:, 0:1024], xfi, AF.Square,
                     reads=[rTA], writes=[rdump, rSM], accum_out=SM[:, 0:1])
            self.vop(lambda eng, o=SM[:, 0:1], a=SM[:, 0:1]:
                     eng.tensor_scalar_mul(o, a, 1.0 / D),
                     reads=[rSM], writes=[rSM])
            self.act(SM[:, 1:2], SM[:, 0:1], AF.Sqrt, bias=self.constv(1e-5),
                     reads=[rSM, self.CONSTV[1]], writes=[rSM])
            self.vop(lambda eng, o=SM[:, 0:1], a=SM[:, 1:2]:
                     eng.reciprocal(o, a),
                     reads=[rSM], writes=[rSM])
            self.vop(lambda eng, o=tb_, a=xfi, s=SM[:, 0:1]:
                     eng.tensor_scalar_mul(o, a, s),
                     reads=[rTA, rSM], writes=[self.SCRB[1]])
            pap, preg = self.ps(self.ps_rr())
            pbf = pap.bitcast(BF16)
            items = [(pbf[:, c * 128:(c + 1) * 128],
                      tb_[:, c * 128:(c + 1) * 128]) for c in range(8)]
            self.transpose_group(items, reads=[self.SCRB[1]], writes=[preg])
            for c in range(8):
                self.vop(lambda eng,
                         o=self.TB[0][:, c * 2048 + i * 128: c * 2048 + (i + 1) * 128],
                         a=pbf[:, c * 128:(c + 1) * 128], g=g2cm[:, c:c + 1]:
                         eng.tensor_scalar_mul(o, a, g),
                         reads=[preg, rSM], writes=[self.TB[1]])


def _chunks_out_h(dst, i):
    a = dst[:, :]
    return AP(a.tensor, i * 128, [[a.ap[0][0], 128], [2048, 8], [1, 128]])



class FullBuilder(Builder, _AttnMixin):
    def __init__(self, parts):
        Builder.__init__(self, parts)
        self.debug = False
        self._dbg_outs = {}
        self.rVS = Region("vs")
        self.rGARR = Region("garr")
        self._brow_pos = 0
        self._garr_done = False

    def _dbg(self, tag, ap, region, shape=None):
        if not self.debug or tag in self._dbg_outs:
            return
        shape = shape or list(ap.shape)
        d = self.nc.dram_tensor("dbg_" + tag, shape, ap.dtype,
                                kind="ExternalOutput").ap()
        self._dbg_outs[tag] = d
        self.p.dma("gpsimd", d[:, :] if len(shape) == 2 else d[:],
                   ap, reads=[region], writes=[])

    def emit_garr(self):
        if self._garr_done:
            return
        self._garr_done = True
        p = self.p
        re33 = self.SMALL[0][0:33, 32:48]
        rSM = self.SMALL[1]
        p.dma("sync", self.SMALL[0][0:32, 32:48], self.dram["l0a_rel"][:, :],
              reads=[rSM], writes=[rSM])
        self.vop(lambda eng, o=self.SMALL[0][32:33, 32:48]:
                 eng.memset(o, 1.0), reads=[rSM], writes=[rSM])
        oh_t = self.dram["c_oh2"].tensor
        garr_t = self.dram["garr"].tensor
        stage = self.SCRB[0][0:16, 0:512]
        pos = 0
        while pos < NGARR:
            wdt = min(512, NGARR - pos)
            rhs32 = self.SCR32[0][0:33, 0:wdt]
            p.dma("sync", rhs32, AP(oh_t, pos, [[NGARR, 33], [1, wdt]]),
                  reads=[self.SCR32[1]], writes=[self.SCR32[1]])
            pap, preg = self.ps(self.ps_rr())
            self.mm_group([(pap[0:16, 0:wdt], re33, rhs32, True, True)],
                          reads=[rSM, self.SCR32[1]], writes=[preg])
            self.act(stage[0:16, 0:wdt], pap[0:16, 0:wdt], AF.Copy,
                     reads=[preg], writes=[self.SCRB[1]])
            p.dma("sync", AP(garr_t, pos, [[NGARR, 16], [1, wdt]]),
                  stage[0:16, 0:wdt],
                  reads=[self.SCRB[1]], writes=[self.rGARR])
            pos += wdt

    def part_l0a(self):
        self.emit_garr()
        cfg = AttnCfg("l0a", "l0a_wq", "l0a_wk", "l0a_wv", "l0a_wo",
                      bq="l0a_bq", bk="l0a_bk", bv="l0a_bv", bo="l0a_bo",
                      scale=1.0, causal=True, tbl="garr")
        self.emit_full_attn(cfg)

    def part_l1a(self):
        cfg = AttnCfg("l1a", "l1a_wqkv", "l1a_wqkv", "l1a_wqkv", "l1a_wout",
                      wcols=3072, bo="l1a_bout", scale=64 ** -0.5,
                      qcol0=0, kcol0=1024, vcol0=2048)
        cfg.inner_gamma = "l1a_gamma"
        cfg.inner_beta = "l1a_beta"
        self.emit_full_attn(cfg, inner_ln=True)

    def part_l1b(self):
        cfg = AttnCfg("l1b", "l1b_wq", "l1b_wk", "l1b_wv", "l1b_wo",
                      bq="l1b_bq", bk="l1b_bk", bv="l1b_bv", bo="l1b_bo",
                      scale=64 ** -0.5, colmask=True)
        self.emit_full_attn(cfg)

    def part_l1f(self):
        self.emit_ln(self.TA[0], self.TA[1])
        wgs = self.dram["l1f_wg_s_bf"].tensor
        wus = self.dram["l1f_wu_s_bf"].tensor
        wds = self.dram["l1f_wd_s_bf"].tensor
        for n in range(2):
            self.emit_expert(self.TA[0], self.TA[1], wgs, wus, wds,
                             n * 1024 * 512, n * 512 * 1024, 512, None)
        self.emit_router_gates(self.TA[0], self.TA[1], "l1f_router", 16, True)
        wgr = self.dram["l1f_wg_r_bf"].tensor
        wur = self.dram["l1f_wu_r_bf"].tensor
        wdr = self.dram["l1f_wd_r_bf"].tensor
        for e in range(16):
            self.emit_expert(self.TA[0], self.TA[1], wgr, wur, wdr,
                             e * 1024 * 256, e * 256 * 1024, 256, e)

    def part_l0f(self):
        g1cm, g2cm = self.load_cm_pair("l0f_g1", "l0f_g2")
        self.emit_ln_rms(self.TA[0], self.TA[1], g1cm)
        self.emit_qT_proj(self.TC, "l0f_wdown", 0, None, 512, ndout=4)
        cfgv = AttnCfg("l0fv", None, None, "l0f_wv", None, wcols=512)
        self._battn = {}
        self.emit_v_to_vs(cfgv, src=self.TC, hd=32, vtot=512, nkc=4)
        self.emit_latent_qk(self.TA[0], self.TA[1], "l0f_wq")
        self.emit_latent_qk(self.TB[0], self.TB[1], "l0f_wk")
        cfgc = AttnCfg("l0fc", None, None, None, None, scale=32 ** -0.5,
                       causal=True, tbl="gcaus")
        self.emit_attn_core(cfgc, (self.TA, self.TA[1]), (self.TB, self.TB[1]),
                            self.TC[0], self.TC[1], 8192, 512, 32)
        self.emit_l0f_up(g2cm)
        self.emit_router_gates(self.TB[0], self.TB[1], "l0f_router", 8, False)
        w1 = self.dram["l0f_w1_bf"].tensor
        w3 = self.dram["l0f_w3_bf"].tensor
        w2 = self.dram["l0f_w2_bf"].tensor
        for e in range(8):
            self.emit_expert(self.TB[0], self.TB[1], w1, w3, w2,
                             e * 1024 * 2048, e * 2048 * 1024, 2048, e)

    def part_l0b(self):
        self.brow_reset()
        self._battn = {}
        self.emit_ln(self.TA[0], self.TA[1])
        self._battn["v"] = self.load_row_bf("l0b_bv", 1024, slot=0)
        self.emit_qT_proj(self.TC, "l0b_wv", 0, self._battn["v"], 1024)
        self._battn["q"] = self.load_row_bf("l0b_bq", 1024, slot=0)
        self._battn["k"] = self.load_row_bf("l0b_bk", 1024, slot=1)
        self.p.dma("sync", self.B64[0][:, :], self.dram["c_b64"][:, :],
                   reads=[self.B64[1]], writes=[self.B64[1]])
        self.p.dma("sync", self.E64[0][:, :], self.dram["c_e64"][:, :],
                   reads=[self.E64[1]], writes=[self.E64[1]])
        TD = self.TD[0]
        self.p.dma("sync", TD[0:16, 2048:4096], self.dram["c_m16"][:, :],
                   reads=list(self.td_regs(2048, 4096)),
                   writes=list(self.td_regs(2048, 4096)))
        TA, rTA = self.TA
        wq_t = self.dram["l0b_wq_bf"].tensor
        wk_t = self.dram["l0b_wk_bf"].tensor
        bq, bk = self._battn["q"], self._battn["k"]
        sc_regs = self.pr[0:4]
        for dc in range(8):
            sq_ = self.WS[(2 * dc) % 4]
            sk_ = self.WS[(2 * dc + 1) % 4]
            self.wdma(0, sq_[0][:, 0:1024],
                      AP(wq_t, dc * 128, [[1024, 128], [128 * 1024, 8], [1, 128]]),
                      reads=[sq_[1]], writes=[sq_[1]])
            self.wdma(1, sk_[0][:, 0:1024],
                      AP(wk_t, dc * 128, [[1024, 128], [128 * 1024, 8], [1, 128]]),
                      reads=[sk_[1]], writes=[sk_[1]])
            for tt4 in range(4):
                ts_ = slice(tt4 * 512, tt4 * 512 + 512)
                qi = self.ps_rr(lo=4, hi=8)
                ki = self.ps_rr(lo=4, hi=8)
                qp, qreg = self.ps(qi)
                kp, kreg = self.ps(ki)
                qits = [(qp, sq_[0][:, kc * 128:(kc + 1) * 128],
                         TA[:, kc * 2048 + tt4 * 512: kc * 2048 + tt4 * 512 + 512])
                        for kc in range(8)]
                qits.append((qp, bq[0][0:1, dc * 128:dc * 128 + 128],
                             self.ONE[0][0:1, 0:512]))
                self.mm_group(_mk_group(qits),
                              reads=[sq_[1], rTA, bq[1], self.ONE[1]],
                              writes=[qreg])
                kits = [(kp, sk_[0][:, kc * 128:(kc + 1) * 128],
                         TA[:, kc * 2048 + tt4 * 512: kc * 2048 + tt4 * 512 + 512])
                        for kc in range(8)]
                kits.append((kp, bk[0][0:1, dc * 128:dc * 128 + 128],
                             self.ONE[0][0:1, 0:512]))
                self.mm_group(_mk_group(kits),
                              reads=[sk_[1], rTA, bk[1], self.ONE[1]],
                              writes=[kreg])
                kev = self.SCR32[0][:, 0:512]
                self.act(kev, kp, AF.Copy, reads=[kreg],
                         writes=[self.SCR32[1]])
                prod = self.SCR32[0][:, 512:1024]
                self.vop(lambda eng, o=prod, a=qp, b=kev:
                         eng.tensor_mul(o, a, b),
                         reads=[qreg, self.SCR32[1]], writes=[self.SCR32[1]])
                self.mm_group([(self.PP[0:16, ts_],
                                self.B64[0][:, dc * 16:dc * 16 + 16],
                                prod, dc == 0, dc == 7)],
                              reads=[self.B64[1], self.SCR32[1]],
                              writes=[sc_regs[tt4]])
        SM, rSM = self.SMALL[0], self.SMALL[1]
        for seg in range(4):
            part = self.PP[:, :].ap[0][0]
            mneg = SM[0:16, 16:20]
            pin = AP(self.PP[:, :].tensor, seg * 512,
                     [[part, 16], [1, 4], [4, 128]])
            self.vop(lambda eng, o=mneg, a=pin:
                     eng.reduce_max(o, a, axis=AX.X, negate=True),
                     reads=[sc_regs[seg]], writes=[rSM])
            self.vop(lambda eng, o=mneg, a=mneg:
                     eng.tensor_scalar_mul(o, a, 0.125),
                     reads=[rSM], writes=[rSM])
            for rp in range(4):
                pslice = AP(self.PP[:, :].tensor, seg * 512 + rp,
                            [[part, 16], [4, 128]])
                tdp = TD[:, :].ap[0][0]
                pout = AP(TD[:, :].tensor, seg * 512 + rp, [[tdp, 16], [4, 128]])
                sums = SM[0:16, 20:21]
                self.act(pout, pslice, AF.Exp, bias=mneg[0:16, rp:rp + 1],
                         scale=self.constv(0.125)[0:16, 0:1],
                         reads=[sc_regs[seg], rSM, self.CONSTV[1]],
                         writes=[self.TDr[seg], rSM], accum_out=sums)
                rec = SM[0:16, 21:22]
                self.vop(lambda eng, o=rec, a=sums: eng.reciprocal(o, a),
                         reads=[rSM], writes=[rSM])
                self.vop(lambda eng, o=pout, a=pout, s=rec:
                         eng.tensor_scalar_mul(o, a, s),
                         reads=[rSM, self.TDr[seg]], writes=[self.TDr[seg]])
        self.vop(lambda eng, o=TD[0:16, 0:2048], a=TD[0:16, 0:2048],
                 b=TD[0:16, 2048:4096]:
                 eng.tensor_mul(o, a, b),
                 reads=list(self.td_regs(0, 4096)),
                 writes=list(self.td_regs(0, 2048)))
        TB, rTB = self.TB
        TC, rTC = self.TC
        for dc in range(8):
            for tt4 in range(4):
                pap, preg = self.ps(self.ps_rr())
                its = [(pap, self.E64[0][0:16, dc * 128:(dc + 1) * 128],
                        TD[0:16, tt4 * 512:(tt4 + 1) * 512], True, True)]
                self.mm_group(its, reads=[self.E64[1], self.TDr[tt4]],
                              writes=[preg])
                self.vop(lambda eng,
                         o=TB[:, dc * 2048 + tt4 * 512: dc * 2048 + tt4 * 512 + 512],
                         a=pap,
                         b=TC[:, dc * 2048 + tt4 * 512: dc * 2048 + tt4 * 512 + 512]:
                         eng.tensor_mul(o, a, b),
                         reads=[preg, rTC], writes=[rTB])
        self._battn["o"] = self.load_row_bf("l0b_bo", 1024, slot=1)
        self.emit_out_proj("l0b_wo", self._battn["o"], self.TB, rTB, 1024,
                           src_is_T=True)

    def load_consts(self):
        for cname, buf in (("c_ident", self.IDE),
                           ("c_J", self.JX), ("c_ones", self.ONE)):
            self.p.dma("sync", buf[0][:, :], self.dram[cname][:, :],
                       reads=[buf[1]], writes=[buf[1]])

    def build(self):
        self.load_consts()
        self.convert_weights()
        for s in range(SEQ_PER_CORE):
            self.load_x(s)
            for prt in self.parts:
                self._conv_reg = self.conv_reg.get(prt)
                getattr(self, "part_" + prt)()
                self._conv_reg = None
            self.store_x(s)
        self.p.finalize(self.nc, self.es)
        self.es.close()
        return self.nc



MODULE_ORDER = ["l0a", "l0f", "l0b", "l0f", "l1a", "l1f", "l1b", "l1f"]
FUSED = True


def _in_maps(parts, x, w, hc):
    need = set()
    for prt in parts:
        need.update(PART_WEIGHTS[prt])
    maps = []
    for c in range(N_CORES):
        m = {"x_in": np.ascontiguousarray(x[2 * c:2 * c + 2])}
        for n in sorted(need):
            m[n] = w[n] if n not in ROW_WEIGHTS else w[n]
        for n in CONST_NAMES:
            m[n] = hc[n]
        maps.append(m)
    return maps


def _run_parts(parts, x, w, hc, core_ids=None):
    b = FullBuilder(parts)
    nc = b.build()
    cids = core_ids or list(range(N_CORES))
    maps = _in_maps(parts, x, w, hc)[: len(cids)]
    res = run_bass_kernel_spmd(nc, maps, core_ids=cids)
    return np.concatenate([r["x_out"] for r in res.results], axis=0)


def kernel(**inputs):
    w = _prep_weights(inputs)
    hc = _host_constants()
    x = np.ascontiguousarray(np.asarray(inputs["L0_attn_a_x"], np.float32))
    if FUSED:
        x = _run_parts(MODULE_ORDER, x, w, hc)
    else:
        for prt in MODULE_ORDER:
            x = _run_parts([prt], x, w, hc)
    return x


# version 11
# speedup vs baseline: 1.0479x; 1.0045x over previous
import math
import numpy as np
import ml_dtypes
from contextlib import ExitStack

import concourse.bass as bass
import concourse.mybir as mybir
from concourse.ap import AP
from concourse.bass_utils import run_bass_kernel_spmd

F32 = mybir.dt.float32
BF16 = mybir.dt.bfloat16
AF = mybir.ActivationFunctionType
ALU = mybir.AluOpType
AX = mybir.AxisListType
BF = ml_dtypes.bfloat16

N_CORES = 8
BATCH = 16
SEQ = 2048
D = 1024
NT = SEQ // 128
NQB = SEQ // 512
SEQ_PER_CORE = BATCH // N_CORES
NEG = -1.0e38

ENGINES = ("tensor", "vector", "scalar", "gpsimd", "sync")
DMA_QUEUES = ("sync", "gpsimd", "scalar")



class _Sem:
    __slots__ = ("handle", "count", "name")

    def __init__(self, name):
        self.handle = None
        self.count = 0
        self.name = name


class Tok:
    __slots__ = ("sem", "val", "engine")

    def __init__(self, sem, val, engine):
        self.sem = sem
        self.val = val
        self.engine = engine


class Region:
    __slots__ = ("name", "writer", "readers")

    def __init__(self, name):
        self.name = name
        self.writer = None
        self.readers = {}


class Prog:
    SEM_CAP = 55000

    def __init__(self):
        self.streams = {e: [] for e in ENGINES}
        self.sems = []
        self.cur = {}
        self.dma_sem = {}
        self.clock = {}
        self.n_inst = 0
        for e in ENGINES:
            self.cur[e] = self._new_sem("c_" + e)
        for q in DMA_QUEUES:
            self.dma_sem[q] = self._new_sem("d_" + q)

    def _new_sem(self, name):
        s = _Sem(f"{name}_{len(self.sems)}")
        self.sems.append(s)
        return s

    def _wait(self, engine, tok):
        if tok is None:
            return
        key = (engine, id(tok.sem))
        if self.clock.get(key, 0) >= tok.val:
            return
        self.clock[key] = tok.val
        sem, val = tok.sem, tok.val
        self.streams[engine].append(lambda eng, s=sem, v=val: eng.wait_ge(s.handle, v))

    def _deps(self, engine, reads, writes):
        for r in reads:
            self._wait(engine, r.writer)
        for w in writes:
            self._wait(engine, w.writer)
            for t in w.readers.values():
                self._wait(engine, t)

    def _finish(self, engine, tok, reads, writes):
        for w in writes:
            w.writer = tok
            w.readers = {}
        for r in reads:
            r.readers[(tok.engine, id(tok.sem))] = tok

    def op(self, engine, fns, reads=(), writes=()):
        if not isinstance(fns, (list, tuple)):
            fns = [fns]
        self._deps(engine, reads, writes)
        sem = self.cur[engine]
        if sem.count + 1 > self.SEM_CAP:
            sem = self.cur[engine] = self._new_sem("c_" + engine)
        sem.count += 1
        val = sem.count
        self.n_inst += len(fns)
        for f in fns[:-1]:
            self.streams[engine].append(f)
        last = fns[-1]
        self.streams[engine].append(
            lambda eng, f=last, s=sem, v=val: f(eng).then_inc(s.handle, 1))
        tok = Tok(sem, val, engine)
        self._finish(engine, tok, reads, writes)
        return tok

    def dma(self, queue, out_ap, in_ap, reads=(), writes=(), **kw):
        self._deps(queue, reads, writes)
        sem = self.dma_sem[queue]
        if sem.count + 16 > self.SEM_CAP:
            sem = self.dma_sem[queue] = self._new_sem("d_" + queue)
        sem.count += 16
        val = sem.count
        self.n_inst += 1
        self.streams[queue].append(
            lambda eng, o=out_ap, i=in_ap, s=sem, v=val, kw=kw:
                eng.dma_start(out=o, in_=i, **kw).then_inc(s.handle, 16))
        tok = Tok(sem, val, queue)
        self._finish(queue, tok, reads, writes)
        return tok

    def barrier_token(self, engine):
        sem = self.cur[engine]
        return Tok(sem, sem.count, engine) if sem.count else None

    def finalize(self, nc, es):
        for s in self.sems:
            if s.name.startswith("d_") and s.count:
                self.streams["sync"].append(
                    lambda eng, s=s, v=s.count: eng.wait_ge(s.handle, v))
        for s in self.sems:
            s.handle = es.enter_context(nc.semaphore(s.name))
        block = es.enter_context(nc.Block())
        for ename in ENGINES:
            items = self.streams[ename]
            getattr(block, ename)(lambda eng, items=items: [f(eng) for f in items])



def _bucket_causal(d):
    NUM_BUCKETS, MAX_DISTANCE = 32, 128
    n = max(d, 0)
    max_exact = NUM_BUCKETS // 2
    if n < max_exact:
        return n
    val = max_exact + int(
        math.log(max(n, 1) / max_exact) / math.log(MAX_DISTANCE / max_exact)
        * (NUM_BUCKETS - max_exact))
    return min(val, NUM_BUCKETS - 1)


DMIN = -511
NGARR = 2688
TBL_W = 2560


def _host_constants():
    c = {}
    c["c_ident"] = np.eye(128).astype(BF)
    c["c_ident32"] = np.eye(128, dtype=np.float32)
    c["c_J"] = np.eye(128)[::-1].copy().astype(BF)
    c["c_ones"] = np.ones((1, 512), dtype=np.float32).astype(BF)
    j = np.arange(NGARR)
    dval = j + DMIN - 127
    c["c_gcaus"] = np.where(dval >= 0, 0.0, NEG).astype(np.float32).astype(BF)
    oh = np.zeros((33, NGARR), dtype=np.float32)
    for jj in range(NGARR):
        dd = int(dval[jj])
        if dd >= 0:
            oh[_bucket_causal(dd), jj] = 0.125
        else:
            oh[32, jj] = NEG
    c["c_oh2"] = oh
    c["c_ones16"] = np.ones((1, 16), dtype=np.float32)
    HD = 32
    inv_freq = 1.0 / (10000.0 ** (np.arange(0, HD, 2, dtype=np.float64) / HD))
    ang = np.arange(SEQ, dtype=np.float64)[:, None] * inv_freq[None, :]
    c["c_cos"] = np.cos(ang).astype(np.float32)
    c["c_sin"] = np.sin(ang).astype(np.float32)
    b64 = np.zeros((128, 128), dtype=np.float32)
    for dc in range(8):
        for p in range(128):
            b64[p, dc * 16 + (128 * dc + p) // 64] = 1.0
    c["c_b64"] = b64
    hh = np.arange(16)[:, None] % 4
    tt = np.arange(SEQ)[None, :] % 4
    c["c_m16"] = (hh == tt).astype(np.float32).astype(BF)
    c["c_e64"] = (np.arange(1024)[None, :] // 64 == np.arange(16)[:, None]
                  ).astype(np.float32).astype(BF)
    return c


def _perm_latent():
    p = []
    for h in range(16):
        p.extend(32 * h + 2 * i for i in range(16))
    for h in range(16):
        p.extend(32 * h + 2 * i + 1 for i in range(16))
    return np.array(p)


def _prep_weights(inp):
    w = {}
    f32 = np.float32

    def T(a):
        return np.ascontiguousarray(np.asarray(a, f32).T)

    def C(a):
        return np.ascontiguousarray(np.asarray(a, f32))

    def row(a):
        return np.asarray(a, f32).reshape(1, -1).copy()

    for k in ("q", "k", "v", "o"):
        w[f"l0a_w{k}"] = T(inp[f"L0_attn_a_W{k}"])
        w[f"l0a_b{k}"] = row(inp[f"L0_attn_a_b{k}"])
    w["l0a_rel"] = C(inp["L0_attn_a_rel_emb"])
    for k in ("q", "k", "v", "o"):
        w[f"l0b_w{k}"] = C(inp[f"L0_attn_b_W{k}"])
        w[f"l0b_b{k}"] = row(inp[f"L0_attn_b_b{k}"])
    w["l0f_g1"] = row(inp["L0_ffn_g1"])
    w["l0f_g2"] = row(inp["L0_ffn_g2"])
    w["l0f_wdown"] = C(inp["L0_ffn_w_down"])
    def pad_lat(a):
        a = np.asarray(a, f32)
        out = np.zeros((512, 1024), f32)
        for h in range(16):
            out[:, 64 * h: 64 * h + 16] = a[:, 32 * h + 2 * np.arange(16)]
            out[:, 64 * h + 32: 64 * h + 48] = a[:, 32 * h + 2 * np.arange(16) + 1]
        return out
    w["l0f_wq"] = pad_lat(inp["L0_ffn_wq"])
    w["l0f_wk"] = pad_lat(inp["L0_ffn_wk"])
    w["l0f_wv"] = C(inp["L0_ffn_wv"])
    w["l0f_wup"] = C(inp["L0_ffn_w_up"])
    w["l0f_router"] = C(inp["L0_ffn_router_w"])
    w["l0f_w1"] = C(inp["L0_ffn_w1"])
    w["l0f_w3"] = C(inp["L0_ffn_w3"])
    w["l0f_w2"] = C(inp["L0_ffn_w2"])
    w["l1a_gamma"] = row(inp["L1_attn_a_gamma"])
    w["l1a_beta"] = row(inp["L1_attn_a_beta"])
    w["l1a_wqkv"] = C(inp["L1_attn_a_w_qkv"])
    w["l1a_wout"] = C(inp["L1_attn_a_w_out"])
    w["l1a_bout"] = row(inp["L1_attn_a_b_out"])
    for k in ("q", "k", "v", "o"):
        w[f"l1b_w{k}"] = T(inp[f"L1_attn_b_w{k}_w"])
        w[f"l1b_b{k}"] = row(inp[f"L1_attn_b_w{k}_b"])
    w["l1f_router"] = C(inp["L1_ffn_router_w"])
    w["l1f_wg_r"] = C(inp["L1_ffn_wg_r"])
    w["l1f_wu_r"] = C(inp["L1_ffn_wu_r"])
    w["l1f_wd_r"] = C(inp["L1_ffn_wd_r"])
    w["l1f_wg_s"] = C(inp["L1_ffn_wg_s"])
    w["l1f_wu_s"] = C(inp["L1_ffn_wu_s"])
    w["l1f_wd_s"] = C(inp["L1_ffn_wd_s"])
    return w



BIG_WEIGHTS = {
    "l0a_wq": (1024, 1024), "l0a_wk": (1024, 1024), "l0a_wv": (1024, 1024),
    "l0a_wo": (1024, 1024),
    "l0b_wq": (1024, 1024), "l0b_wk": (1024, 1024), "l0b_wv": (1024, 1024),
    "l0b_wo": (1024, 1024),
    "l0f_wdown": (1024, 512), "l0f_wq": (512, 1024), "l0f_wk": (512, 1024),
    "l0f_wv": (512, 512), "l0f_wup": (512, 1024), "l0f_router": (1024, 8),
    "l0f_w1": (8, 1024, 2048), "l0f_w3": (8, 1024, 2048), "l0f_w2": (8, 2048, 1024),
    "l1a_wqkv": (1024, 3072), "l1a_wout": (1024, 1024),
    "l1b_wq": (1024, 1024), "l1b_wk": (1024, 1024), "l1b_wv": (1024, 1024),
    "l1b_wo": (1024, 1024),
    "l1f_router": (1024, 16),
    "l1f_wg_r": (16, 1024, 256), "l1f_wu_r": (16, 1024, 256),
    "l1f_wd_r": (16, 256, 1024),
    "l1f_wg_s": (2, 1024, 512), "l1f_wu_s": (2, 1024, 512),
    "l1f_wd_s": (2, 512, 1024),
}
ROW_WEIGHTS = {
    "l0a_bq": 1024, "l0a_bk": 1024, "l0a_bv": 1024, "l0a_bo": 1024,
    "l0b_bq": 1024, "l0b_bk": 1024, "l0b_bv": 1024, "l0b_bo": 1024,
    "l0f_g1": 1024, "l0f_g2": 1024,
    "l1a_gamma": 1024, "l1a_beta": 1024, "l1a_bout": 1024,
    "l1b_bq": 1024, "l1b_bk": 1024, "l1b_bv": 1024, "l1b_bo": 1024,
}
PART_WEIGHTS = {
    "l0a": ["l0a_wq", "l0a_wk", "l0a_wv", "l0a_wo", "l0a_bq", "l0a_bk",
            "l0a_bv", "l0a_bo", "l0a_rel"],
    "l0f": ["l0f_g1", "l0f_g2", "l0f_wdown", "l0f_wq", "l0f_wk", "l0f_wv",
            "l0f_wup", "l0f_router", "l0f_w1", "l0f_w3", "l0f_w2"],
    "l0b": ["l0b_wq", "l0b_wk", "l0b_wv", "l0b_wo", "l0b_bq", "l0b_bk",
            "l0b_bv", "l0b_bo"],
    "l1a": ["l1a_gamma", "l1a_beta", "l1a_wqkv", "l1a_wout", "l1a_bout"],
    "l1b": ["l1b_wq", "l1b_wk", "l1b_wv", "l1b_wo", "l1b_bq", "l1b_bk",
            "l1b_bv", "l1b_bo"],
    "l1f": ["l1f_router", "l1f_wg_r", "l1f_wu_r", "l1f_wd_r", "l1f_wg_s",
            "l1f_wu_s", "l1f_wd_s"],
}
CONST_NAMES = ["c_ident", "c_ident32", "c_J", "c_ones", "c_gcaus", "c_oh2",
               "c_ones16", "c_cos", "c_sin", "c_m16", "c_e64", "c_b64"]
CONST_DT = {"c_oh2": F32, "c_cos": F32, "c_sin": F32, "c_ones16": F32,
            "c_ident32": F32, "c_b64": F32}


class Builder:
    def __init__(self, parts):
        self.parts = list(parts)
        self.p = Prog()
        self.nc = bass.Bass("TRN2", target_bir_lowering=False)
        self.es = ExitStack()
        nc = self.nc
        self.x_in = nc.dram_tensor("x_in", [SEQ_PER_CORE, SEQ, D], F32,
                                   kind="ExternalInput").ap()
        self.x_out = nc.dram_tensor("x_out", [SEQ_PER_CORE, SEQ, D], F32,
                                    kind="ExternalOutput").ap()
        self.dram = {}
        need = set()
        for prt in self.parts:
            need.update(PART_WEIGHTS[prt])
        hc = _host_constants()
        self.need_consts = [n for n in CONST_NAMES]
        for n in self.need_consts:
            arr = hc[n]
            dt = CONST_DT.get(n, BF16)
            self.dram[n] = nc.dram_tensor(n, list(arr.shape), dt,
                                          kind="ExternalInput").ap()
        self.need_weights = sorted(need)
        for n in self.need_weights:
            if n in BIG_WEIGHTS:
                self.dram[n] = nc.dram_tensor(n, list(BIG_WEIGHTS[n]), F32,
                                              kind="ExternalInput").ap()
                self.dram[n + "_bf"] = nc.dram_tensor(
                    n + "_bf", list(BIG_WEIGHTS[n]), BF16, kind="Internal").ap()
            elif n in ROW_WEIGHTS:
                self.dram[n] = nc.dram_tensor(n, [1, ROW_WEIGHTS[n]], F32,
                                              kind="ExternalInput").ap()
            elif n == "l0a_rel":
                self.dram[n] = nc.dram_tensor(n, [32, 16], F32,
                                              kind="ExternalInput").ap()
        self.dram["vs"] = nc.dram_tensor("vs", [16, SEQ, 65], BF16,
                                         kind="Internal").ap()
        self.dram["garr"] = nc.dram_tensor("garr", [16, NGARR], BF16,
                                           kind="Internal").ap()
        es, s = self.es, self
        def sb(name, shape, dt):
            t = es.enter_context(nc.sbuf_tensor(name, shape, dt))
            return t, Region(name)
        s.X = sb("X", [128, NT * 1024], F32)
        s.TA = sb("TA", [128, NT * 1024], BF16)
        s.TB = sb("TB", [128, NT * 1024], BF16)
        s.TC = sb("TC", [128, NT * 1024], BF16)
        s.TD = sb("TD", [128, 8 * 1024], BF16)
        s.TDr = [Region(f"TD{i}") for i in range(16)]
        s.WS = [sb(f"WS{i}", [128, 1152], BF16) for i in range(4)]
        s.SCR32 = sb("SCR32", [128, 1024], F32)
        s.SCRB = sb("SCRB", [128, 2560], BF16)
        s.MROW = sb("MROW", [1, 2048], BF16)
        s.IDE = sb("IDE", [128, 128], BF16)
        s.JX = sb("JX", [128, 128], BF16)
        s.ONE = sb("ONE", [1, 512], BF16)
        s.SUMS = sb("SUMS", [128, 16], F32)
        s.MEAN = sb("MEAN", [128, 16], F32)
        s.RSTD = sb("RSTD", [128, 16], F32)
        s.MM = sb("MM", [128, 16], F32)
        s.MMB = sb("MMB", [128, 16], BF16)
        s._att_regs = [Region(f"att{i}") for i in range(5)]
        s.MSK = sb("MSK", [128, 16], F32)
        s.GTS = sb("GTS", [128, 16 * 16], F32)
        s.SMALL = sb("SMALL", [128, 64], F32)
        s.B64 = sb("B64", [128, 128], F32)
        s.E64 = sb("E64", [16, 1024], BF16)
        s.CONSTV = sb("CONSTV", [128, 16], F32)
        s._constv_map = {}
        s.BROW = sb("BROW", [1, 2048], BF16)
        s.PP = es.enter_context(nc.psum_tensor("PP", [128, 2048], F32))
        s.PQ = es.enter_context(nc.psum_tensor("PQ", [128, 2048], F32))
        s.pr = [Region(f"PP{i}") for i in range(4)] + \
               [Region(f"PQ{i}") for i in range(4)]
        self._psum_rr = 0

    def ps(self, idx):
        t = self.PP if idx < 4 else self.PQ
        off = (idx % 4) * 512
        return t[:, off:off + 512], self.pr[idx]

    def ps_rr(self, n=1, lo=0, hi=8):
        out = []
        for _ in range(n):
            out.append(lo + self._psum_rr % (hi - lo))
            self._psum_rr += 1
        return out if n > 1 else out[0]

    def td_regs(self, c0, c1):
        return self.TDr[c0 // 512: (c1 + 511) // 512]

    def wdma(self, slot_idx, out_ap, in_ap, reads=(), writes=()):
        r = list(reads)
        if getattr(self, "_conv_reg", None) is not None:
            r.append(self._conv_reg)
        return self.p.dma("sync", out_ap, in_ap, reads=r, writes=writes)

    def constv(self, val):
        if val in self._constv_map:
            return self.CONSTV[0][:, self._constv_map[val]:self._constv_map[val] + 1]
        idx = len(self._constv_map)
        assert idx < 16
        self._constv_map[val] = idx
        ap = self.CONSTV[0][:, idx:idx + 1]
        self.vop(lambda eng, o=ap, v=val: eng.memset(o, v),
                 reads=[self.CONSTV[1]], writes=[self.CONSTV[1]])
        return ap

    def mm_group(self, items, reads, writes):
        fns = [lambda eng, it=it: eng.matmul(it[0], it[1], it[2],
                                             start=it[3], stop=it[4])
               for it in items]
        return self.p.op("tensor", fns, reads=reads, writes=writes)

    def transpose_group(self, items, reads, writes):
        idet, idereg = self.IDE
        ide = idet[:, :]
        fns = [lambda eng, it=it: eng.transpose(it[0], it[1], ide)
               for it in items]
        return self.p.op("tensor", fns, reads=list(reads) + [idereg],
                         writes=writes)

    def act(self, out_ap, in_ap, func, reads, writes, bias=0.0, scale=1.0,
            accum_out=None):
        return self.p.op(
            "scalar",
            lambda eng: eng.activation(out_ap, in_ap, func, bias=bias,
                                       scale=scale, accum_out=accum_out),
            reads=reads, writes=writes)

    def vop(self, fn, reads, writes):
        return self.p.op("vector", fn, reads=reads, writes=writes)

    def convert_weights(self):
        self.conv_reg = {}
        chain = Region("conv_chain")
        done = set()
        order = []
        for prt in self.parts:
            if prt not in order:
                order.append(prt)
        for prt in order:
            reg = self.conv_reg[prt] = Region("conv_" + prt)
            for name in PART_WEIGHTS[prt]:
                if name not in BIG_WEIGHTS or name in done:
                    continue
                done.add(name)
                n = int(np.prod(BIG_WEIGHTS[name]))
                src = self.dram[name].tensor
                dst = self.dram[name + "_bf"].tensor
                CH = 2 * 1024 * 1024
                pos = 0
                while pos < n:
                    w = min(CH, n - pos)
                    self.p.dma("gpsimd", AP(dst, pos, [[1, w]]),
                               AP(src, pos, [[1, w]]),
                               reads=[], writes=[reg, chain])
                    pos += w

    def load_x(self, s):
        for i in range(NT):
            self.p.dma("sync", self.X[0][:, i * 1024:(i + 1) * 1024],
                       self.x_in[s, i * 128:(i + 1) * 128, :],
                       reads=[], writes=[self.X[1]])

    def store_x(self, s):
        for i in range(NT):
            self.p.dma("gpsimd", self.x_out[s, i * 128:(i + 1) * 128, :],
                       self.X[0][:, i * 1024:(i + 1) * 1024],
                       reads=[self.X[1]], writes=[])

    def load_cm_pair(self, name_g, name_b):
        out = []
        for j, nm in enumerate((name_g, name_b)):
            dst = self.SMALL[0][:, 16 + j * 8: 16 + (j + 1) * 8]
            src = self.dram[nm].tensor
            for c in range(8):
                self.p.dma("sync", dst[:, c:c + 1],
                           AP(src, c * 128, [[1, 128], [1, 1]]),
                           reads=[self.SMALL[1]], writes=[self.SMALL[1]])
            out.append(dst)
        return out[0], out[1]

    def emit_ln(self, dst, dst_reg, want_msk=False, inner_cm=None):
        X, rX = self.X[0], self.X[1]
        if hasattr(self, "_dbg"):
            self._dbg("X0", X[:, :], rX)
        dump, rdump = self.SCR32[0], self.SCR32[1]
        xnb = self.SCRB[0][:, 0:1024]
        rxnb = self.SCRB[1]
        SM, rSM = self.SMALL[0], self.SMALL[1]
        for i in range(NT):
            xi = X[:, i * 1024:(i + 1) * 1024]
            self.vop(lambda eng, o=self.SUMS[0][:, i:i + 1], a=xi:
                     eng.reduce_sum(o, a, axis=AX.X),
                     reads=[rX], writes=[self.SUMS[1]])
            self.act(dump[:, 0:1024], xi, AF.Square,
                     reads=[rX], writes=[rdump, rSM],
                     accum_out=SM[:, 0:1])
            mean = self.MEAN[0][:, i:i + 1]
            rstd = self.RSTD[0][:, i:i + 1]
            t0, t1 = SM[:, 1:2], SM[:, 2:3]
            self.vop(lambda eng, o=mean, a=self.SUMS[0][:, i:i + 1]:
                     eng.tensor_scalar_mul(o, a, 1.0 / D),
                     reads=[self.SUMS[1]], writes=[self.MEAN[1]])
            self.vop(lambda eng, o=t0, a=SM[:, 0:1]:
                     eng.tensor_scalar_mul(o, a, 1.0 / D),
                     reads=[rSM], writes=[rSM])
            self.vop(lambda eng, o=t1, m=mean: eng.tensor_mul(o, m, m),
                     reads=[self.MEAN[1]], writes=[rSM])
            self.vop(lambda eng, o=t0, a=t0, b=t1: eng.tensor_sub(o, a, b),
                     reads=[rSM], writes=[rSM])
            self.act(t1, t0, AF.Sqrt, bias=self.constv(1e-5),
                     reads=[rSM, self.CONSTV[1]], writes=[rSM])
            self.vop(lambda eng, o=rstd, a=t1: eng.reciprocal(o, a),
                     reads=[rSM], writes=[self.RSTD[1]])
            if want_msk:
                z = SM[:, 3:4]
                self.vop(lambda eng, o=z, a=X[:, i * 1024:i * 1024 + 1],
                         m=mean, r=rstd:
                         eng.tensor_scalar(o, a, m, r, ALU.subtract, ALU.mult),
                         reads=[rX, self.MEAN[1], self.RSTD[1]], writes=[rSM])
                self.vop(lambda eng, o=self.MSK[0][:, i:i + 1], a=z:
                         eng.tensor_scalar(o, a, 0.0, NEG, ALU.is_equal,
                                           ALU.mult),
                         reads=[rSM], writes=[self.MSK[1]])
            if inner_cm is None:
                self.vop(lambda eng, o=xnb, a=xi, m=mean, r=rstd:
                         eng.tensor_scalar(o, a, m, r, ALU.subtract, ALU.mult),
                         reads=[rX, self.MEAN[1], self.RSTD[1]],
                         writes=[rxnb])
            else:
                xf = dump[:, 0:1024]
                self.vop(lambda eng, o=xf, a=xi, m=mean, r=rstd:
                         eng.tensor_scalar(o, a, m, r, ALU.subtract, ALU.mult),
                         reads=[rX, self.MEAN[1], self.RSTD[1]],
                         writes=[rdump])
                s2, q2 = SM[:, 4:5], SM[:, 5:6]
                m2, r2, t2 = SM[:, 6:7], SM[:, 7:8], SM[:, 8:9]
                self.vop(lambda eng, o=s2, a=xf:
                         eng.reduce_sum(o, a, axis=AX.X),
                         reads=[rdump], writes=[rSM])
                self.act(self.SCRB[0][:, 1024:2048], xf, AF.Square,
                         reads=[rdump], writes=[rxnb, rSM], accum_out=q2)
                self.vop(lambda eng, o=m2, a=s2:
                         eng.tensor_scalar_mul(o, a, 1.0 / D),
                         reads=[rSM], writes=[rSM])
                self.vop(lambda eng, o=q2, a=q2:
                         eng.tensor_scalar_mul(o, a, 1.0 / D),
                         reads=[rSM], writes=[rSM])
                self.vop(lambda eng, o=t2, m=m2: eng.tensor_mul(o, m, m),
                         reads=[rSM], writes=[rSM])
                self.vop(lambda eng, o=q2, a=q2, b=t2: eng.tensor_sub(o, a, b),
                         reads=[rSM], writes=[rSM])
                self.act(t2, q2, AF.Sqrt, bias=self.constv(1e-5),
                         reads=[rSM, self.CONSTV[1]], writes=[rSM])
                self.vop(lambda eng, o=r2, a=t2: eng.reciprocal(o, a),
                         reads=[rSM], writes=[rSM])
                self.vop(lambda eng, o=xnb, a=xf, m=m2, r=r2:
                         eng.tensor_scalar(o, a, m, r, ALU.subtract, ALU.mult),
                         reads=[rdump, rSM], writes=[rxnb])
            pap, preg = self.ps(self.ps_rr())
            pbf = pap.bitcast(BF16)
            items = [(pbf[:, c * 128:(c + 1) * 128],
                      xnb[:, c * 128:(c + 1) * 128]) for c in range(8)]
            self.transpose_group(items, reads=[rxnb], writes=[preg])
            if inner_cm is None:
                self.vop(lambda eng, o=_chunks_out(dst, i), a=pbf[:, 0:1024]:
                         eng.tensor_copy(o, a),
                         reads=[preg], writes=[dst_reg])
            else:
                gcm, bcm = inner_cm
                for c in range(8):
                    self.vop(lambda eng,
                             o=dst[:, c * 2048 + i * 128: c * 2048 + (i + 1) * 128],
                             a=pbf[:, c * 128:(c + 1) * 128],
                             g=gcm[:, c:c + 1], b=bcm[:, c:c + 1]:
                             eng.tensor_scalar(o, a, g, b, ALU.mult, ALU.add),
                             reads=[preg, rSM], writes=[dst_reg])


def _chunks_out(dst_handle, i):
    a = dst_handle[:, :]
    return AP(a.tensor, i * 128, [[a.ap[0][0], 128], [2048, 8], [1, 128]])




class AttnCfg:
    def __init__(self, name, wq, wk, wv, wo, wcols=1024, bq=None, bk=None,
                 bv=None, bo=None, scale=1.0, causal=False, tbl=None,
                 colmask=False, qcol0=0, kcol0=0, vcol0=0):
        self.name = name
        self.wq, self.wk, self.wv, self.wo = wq, wk, wv, wo
        self.wcols = wcols
        self.bq, self.bk, self.bv, self.bo = bq, bk, bv, bo
        self.scale = scale
        self.causal = causal
        self.tbl = tbl
        self.colmask = colmask
        self.qcol0, self.kcol0, self.vcol0 = qcol0, kcol0, vcol0


def _mk_group(items):
    out = []
    for i, (o, l, r) in enumerate(items):
        out.append((o, l, r, i == 0, i == len(items) - 1))
    return out


class _AttnMixin:
    def brow_reset(self):
        self._brow_pos = 0

    def load_row_bf(self, dram_name, n, slot=None):
        pos = self._brow_pos if slot is None else slot * 1024
        if slot is None:
            assert pos + n <= 2048, "BROW overflow"
            self._brow_pos = pos + n
        r32 = self.SCR32[0][0:1, 0:n]
        rb = self.BROW[0][0:1, pos:pos + n]
        self.p.dma("sync", r32, self.dram[dram_name][0:1, 0:n],
                   reads=[self.SCR32[1]], writes=[self.SCR32[1]])
        self.vop(lambda eng, o=rb, i=r32: eng.tensor_copy(o, i),
                 reads=[self.SCR32[1]], writes=[self.BROW[1]])
        return rb, self.BROW[1]

    def emit_qT_proj(self, dst, w_name, col0, bias, wcols, src=None,
                     ndout=8, nkc=8, dcol0=0):
        dstT, rdst = dst
        TA, rTA = src if src is not None else self.TA
        wbf = self.dram[w_name + "_bf"].tensor
        for dc in range(ndout):
            slot = self.WS[dc % 4]
            self.wdma(dc, slot[0][:, 0:nkc * 128],
                      AP(wbf, col0 + dc * 128,
                         [[wcols, 128], [128 * wcols, nkc], [1, 128]]),
                      reads=[slot[1]], writes=[slot[1]])
            for nt4 in range(4):
                pap, preg = self.ps(self.ps_rr())
                items = [(pap, slot[0][:, kc * 128:(kc + 1) * 128],
                          TA[:, kc * 2048 + nt4 * 512: kc * 2048 + nt4 * 512 + 512])
                         for kc in range(nkc)]
                reads = [slot[1], rTA]
                if bias is not None:
                    items.append((pap, bias[0][0:1, dc * 128:dc * 128 + 128],
                                  self.ONE[0][0:1, 0:512]))
                    reads += [bias[1], self.ONE[1]]
                self.mm_group(_mk_group(items), reads=reads, writes=[preg])
                self.act(dstT[:, dcol0 + dc * 2048 + nt4 * 512:
                              dcol0 + dc * 2048 + nt4 * 512 + 512],
                         pap, AF.Copy, reads=[preg], writes=[rdst])

    def emit_v_to_vs(self, cfg, src=None, hd=64, vtot=1024, nkc=8):
        TA, rTA = src if src is not None else self.TA
        wbf = self.dram[cfg.wv + "_bf"].tensor
        wcols = cfg.wcols
        bias = self._battn.get("v")
        nvs = vtot // 512
        hd1 = hd + 1
        nh_per_vs = 512 // hd
        for ntg in range(4):
            psl = [[self.ps_rr() for _ in range(4)] for _ in range(nvs)]
            for vs in range(nvs):
                for kc2 in range(nkc // 2):
                    slot = self.WS[kc2 % 4]
                    self.wdma(kc2, slot[0][:, 0:1024],
                              AP(wbf, cfg.vcol0 + vs * 512 + kc2 * 256 * wcols,
                                 [[wcols, 128], [128 * wcols, 2], [1, 512]]),
                              reads=[slot[1]], writes=[slot[1]])
                    for kh in range(2):
                        kc = kc2 * 2 + kh
                        for i in range(4):
                            nt = ntg * 4 + i
                            pap, preg = self.ps(psl[vs][i])
                            its = [(pap, TA[:, kc * 2048 + nt * 128:
                                            kc * 2048 + nt * 128 + 128],
                                    slot[0][:, kh * 512:kh * 512 + 512],
                                    kc == 0,
                                    kc == nkc - 1 and bias is None)]
                            self.mm_group(its, reads=[slot[1], rTA],
                                          writes=[preg])
                if bias is not None:
                    for i in range(4):
                        pap, preg = self.ps(psl[vs][i])
                        its = [(pap, self.ONE[0][0:1, 0:128],
                                bias[0][0:1, vs * 512:vs * 512 + 512],
                                False, True)]
                        self.mm_group(its, reads=[self.ONE[1], bias[1]],
                                      writes=[preg])
            stw = 16 * hd1
            for i in range(4):
                nt = ntg * 4 + i
                st = self.SCRB[0][:, (i % 2) * stw:(i % 2) * stw + stw]
                rst = self.SCRB[1]
                part = st.ap[0][0]
                ones_ap = AP(st.tensor, st.offset, [[part, 128], [hd1, 16], [1, 1]])
                self.vop(lambda eng, o=ones_ap: eng.memset(o, 1.0),
                         reads=[rst], writes=[rst])
                for vs in range(nvs):
                    pap, preg = self.ps(psl[vs][i])
                    out_ap = AP(st.tensor, st.offset + hd1 * nh_per_vs * vs + 1,
                                [[part, 128], [hd1, nh_per_vs], [1, hd]])
                    self.act(out_ap, pap, AF.Copy, reads=[preg], writes=[rst])
                vs_t = self.dram["vs"].tensor
                dst_ap = AP(vs_t, nt * 128 * hd1,
                            [[hd1, 128], [SEQ * hd1, 16], [1, hd1]])
                self.p.dma("sync", dst_ap, st, reads=[rst],
                           writes=[self.rVS])

    def emit_full_attn(self, cfg, inner_ln=False):
        p = self.p
        self.rO = self.TA[1]
        self.brow_reset()
        self._battn = {}
        if cfg.bq is not None:
            self._battn["q"] = self.load_row_bf(cfg.bq, 1024, slot=0)
        if cfg.bk is not None:
            self._battn["k"] = self.load_row_bf(cfg.bk, 1024, slot=1)
        if inner_ln:
            gcm, bcm = self.load_cm_pair(cfg.inner_gamma, cfg.inner_beta)
            self.emit_ln(self.TA[0], self.TA[1], inner_cm=(gcm, bcm))
        else:
            self.emit_ln(self.TA[0], self.TA[1], want_msk=cfg.colmask)
        self._dbg("xnT", self.TA[0][:, :], self.TA[1])
        self._dbg("MEAN", self.MEAN[0][:, :], self.MEAN[1])
        self._dbg("RSTD", self.RSTD[0][:, :], self.RSTD[1])
        self.emit_qT_proj(self.TB, cfg.wq, cfg.qcol0, self._battn.get("q"),
                          cfg.wcols)
        self.emit_qT_proj(self.TC, cfg.wk, cfg.kcol0, self._battn.get("k"),
                          cfg.wcols)
        self._dbg("qT", self.TB[0][:, :], self.TB[1])
        self._dbg("kT", self.TC[0][:, :], self.TC[1])
        if cfg.bv is not None:
            self._battn["v"] = self.load_row_bf(cfg.bv, 1024, slot=0)
        self.emit_v_to_vs(cfg)
        self.emit_attn_core(cfg, (self.TB, self.TB[1]), (self.TC, self.TC[1]),
                            self.TA[0], self.rO, 0, 1024, 64)
        if cfg.bo is not None:
            self._battn["o"] = self.load_row_bf(cfg.bo, 1024, slot=1)
        self.emit_out_proj(cfg.wo, self._battn.get("o"), self.TA, self.rO, 1024)

    def emit_out_proj(self, wo_name, bias, src, rsrc, src_d, src_is_T=False):
        wbf = self.dram[wo_name + "_bf"].tensor
        ndc = src_d // 128
        srcT, _ = src
        TD = self.TD[0]
        for qtg in range(4):
            if not src_is_T:
                for i in range(4):
                    qt = qtg * 4 + i
                    pap, preg = self.ps(self.ps_rr())
                    pbf = pap.bitcast(BF16)
                    items = [(pbf[:, c * 128:(c + 1) * 128],
                              srcT[:, qt * src_d + c * 128: qt * src_d + (c + 1) * 128])
                             for c in range(ndc)]
                    self.transpose_group(items, reads=[rsrc], writes=[preg])
                    self.vop(lambda eng, o=TD[:, i * 2048:i * 2048 + ndc * 128],
                             a=pbf[:, 0:ndc * 128]:
                             eng.tensor_copy(o, a),
                             reads=[preg],
                             writes=self.td_regs(i * 2048,
                                                 i * 2048 + ndc * 128))

            def lhs_tile(i, dc):
                if src_is_T:
                    qt = qtg * 4 + i
                    return srcT[:, dc * 2048 + qt * 128: dc * 2048 + qt * 128 + 128]
                return TD[:, i * 2048 + dc * 128: i * 2048 + (dc + 1) * 128]
            pass
            for ds in range(2):
                for dc2 in range((ndc + 1) // 2):
                    slot = self.WS[dc2 % 4]
                    nld = min(2, ndc - dc2 * 2)
                    self.wdma(dc2, slot[0][:, 0:512 * nld],
                              AP(wbf, ds * 512 + dc2 * 256 * 1024,
                                 [[1024, 128], [128 * 1024, nld], [1, 512]]),
                              reads=[slot[1]], writes=[slot[1]])
                    for dh in range(nld):
                        dc = dc2 * 2 + dh
                        for i in range(4):
                            pap, preg = self.ps((4 + i) if ds == 0 else i)
                            lrg = [rsrc] if src_is_T else list(self.td_regs(
                                i * 2048 + dc * 128, i * 2048 + (dc + 1) * 128))
                            its = [(pap, lhs_tile(i, dc),
                                    slot[0][:, dh * 512:dh * 512 + 512],
                                    dc == 0,
                                    dc == ndc - 1 and bias is None)]
                            self.mm_group(its, reads=[slot[1]] + lrg,
                                          writes=[preg])
                for i in range(4):
                    pap, preg = self.ps((4 + i) if ds == 0 else i)
                    if bias is not None:
                        its = [(pap, self.ONE[0][0:1, 0:128],
                                bias[0][0:1, ds * 512:ds * 512 + 512],
                                False, True)]
                        self.mm_group(its, reads=[self.ONE[1], bias[1]],
                                      writes=[preg])
                    qt = qtg * 4 + i
                    xs = self.X[0][:, qt * 1024 + ds * 512: qt * 1024 + ds * 512 + 512]
                    self.vop(lambda eng, o=xs, a=pap, b=xs:
                             eng.tensor_add(o, a, b),
                             reads=[preg, self.X[1]], writes=[self.X[1]])

    def emit_attn_core(self, cfg, qsrc, ksrc, obuf, oreg, obase, ostride, ohd):
        TB, rTB = qsrc[0][0], qsrc[1]
        TC, rTC = ksrc[0][0], ksrc[1]
        TD = self.TD[0]
        hd1 = ohd + 1
        gt0 = self.SCRB[0][:, 0:TBL_W]
        rgt = self.SCRB[1]
        sc = 1.0 if cfg.scale == 1.0 else self.constv(cfg.scale)
        kts_of = [[kt for kt in range(16)
                   if not cfg.causal or kt * 128 <= qb * 512 + 511]
                  for qb in range(4)]
        vh = [self.WS[0], self.WS[1]]
        ar = self._att_regs
        mmb = [(self.MM[0][:, 0:8], ar[0]), (self.MM[0][:, 8:16], ar[1])]
        mm2 = [(self.MMB[0][:, 0:8], ar[2]), (self.MMB[0][:, 8:16], ar[3])]
        mrow = [(self.MROW[0][0:1, :], self.MROW[1]),
                (self.SCR32[0][0:1, 0:1024].bitcast(BF16), self.SCR32[1])]

        def emit_head_prep(h):
            p = h % 2
            hb = 64 * (h % 2)
            hc = (h // 2) * 2048
            if cfg.tbl == "garr":
                self.p.dma("sync", gt0,
                           AP(self.dram["garr"].tensor, h * NGARR,
                              [[1, 128], [1, TBL_W]]),
                           reads=[rgt], writes=[rgt])
            elif cfg.tbl == "gcaus" and h == 0:
                self.p.dma("sync", gt0,
                           AP(self.dram["c_gcaus"].tensor, 0,
                              [[1, 128], [1, TBL_W]]),
                           reads=[rgt], writes=[rgt])
            self.p.dma("sync", vh[p][0][:, 0:16 * hd1],
                       AP(self.dram["vs"].tensor, h * SEQ * hd1,
                          [[hd1, 128], [hd1 * 128, 16], [1, hd1]]),
                       reads=[vh[p][1], self.rVS], writes=[vh[p][1]])
            mreg = mmb[p][1]
            for qt in range(16):
                nks = (qt // 4 + 1) if cfg.causal else 4
                half = (qt % 2) * 2
                pt = self.PP[:, half * 512:(half + 2) * 512]
                prs = self.pr[half:half + 2]
                ncols = nks * 512
                items = []
                for ks in range(min(nks, 2)):
                    items.append((pt[:, ks * 512:(ks + 1) * 512],
                                  TB[hb:hb + 64, hc + qt * 128: hc + qt * 128 + 128],
                                  TC[hb:hb + 64, hc + ks * 512: hc + ks * 512 + 512],
                                  True, True))
                self.mm_group(items, reads=[rTB, rTC], writes=prs)
                self.vop(lambda eng, o=mmb[p][0][:, qt % 8: qt % 8 + 1],
                         a=pt[:, 0:min(ncols, 1024)]:
                         eng.reduce_max(o, a, axis=AX.X, negate=True),
                         reads=prs, writes=[mreg])
                if nks > 2:
                    half2 = ((qt + 1) % 2) * 2
                    pt2 = self.PP[:, half2 * 512:(half2 + 2) * 512]
                    prs2 = self.pr[half2:half2 + 2]
                    items = []
                    for ks in range(2, nks):
                        items.append((pt2[:, (ks - 2) * 512:(ks - 1) * 512],
                                      TB[hb:hb + 64, hc + qt * 128: hc + qt * 128 + 128],
                                      TC[hb:hb + 64, hc + ks * 512: hc + ks * 512 + 512],
                                      True, True))
                    self.mm_group(items, reads=[rTB, rTC], writes=prs2)
                    t2 = self.SMALL[0][:, 11:12]
                    self.vop(lambda eng, o=t2,
                             a=pt2[:, 0:(nks - 2) * 512]:
                             eng.reduce_max(o, a, axis=AX.X, negate=True),
                             reads=prs2, writes=[self.SMALL[1]])
                    self.vop(lambda eng, o=mmb[p][0][:, qt % 8: qt % 8 + 1],
                             a=mmb[p][0][:, qt % 8: qt % 8 + 1], b=t2:
                             eng.tensor_max(o, a, b),
                             reads=[self.SMALL[1], mreg], writes=[mreg])
                if qt % 8 == 7:
                    self.vop(lambda eng, o=mm2[p][0], a=mmb[p][0]:
                             eng.tensor_copy(o, a),
                             reads=[mreg], writes=[mm2[p][1]])
                    pap, preg = self.ps(self.ps_rr(lo=4, hi=8))
                    pbf = pap.bitcast(BF16)
                    items = [(pbf[0:1, i * 128:(i + 1) * 128],
                              mm2[p][0][:, i:i + 1]) for i in range(8)]
                    self.transpose_group(items, reads=[mm2[p][1]],
                                         writes=[preg])
                    base = (qt - 7) * 128
                    self.vop(lambda eng,
                             o=mrow[p][0][0:1, base:base + 1024],
                             a=pbf[0:1, 0:1024]:
                             eng.tensor_copy(o, a),
                             reads=[preg], writes=[mrow[p][1]])

        def emit_pb(h, qb, kts):
            p = h % 2
            hb = 64 * (h % 2)
            hc = (h // 2) * 2048
            for kt in kts:
                pap, preg = self.ps(self.ps_rr(lo=4, hi=8))
                items = [(pap,
                          TC[hb:hb + 64, hc + kt * 128: hc + kt * 128 + 128],
                          TB[hb:hb + 64, hc + qb * 512: hc + qb * 512 + 512]),
                         (pap, self.ONE[0][0:1, 0:128],
                          mrow[p][0][0:1, qb * 512:(qb + 1) * 512])]
                reads = [rTB, rTC, self.ONE[1], mrow[p][1]]
                if cfg.tbl is not None:
                    c = qb * 512 - kt * 128 - DMIN
                    items.append((pap, self.JX[0][:, :], gt0[:, c:c + 512]))
                    reads += [self.JX[1], rgt]
                self.mm_group(_mk_group([(o, l, r) for (o, l, r) in items]),
                              reads=reads, writes=[preg])
                bias_ap = self.MSK[0][:, kt:kt + 1] if cfg.colmask else 0.0
                rd = [preg, self.CONSTV[1]] + \
                    ([self.MSK[1]] if cfg.colmask else [])
                self.act(TD[:, kt * 512:(kt + 1) * 512], pap, AF.Exp,
                         bias=bias_ap, scale=sc,
                         reads=rd, writes=[self.TDr[kt]])

        def emit_av(h, qb, kts):
            p = h % 2
            for i in range(4):
                qt = qb * 4 + i
                pap, preg = self.ps(i)
                items = [(pap[:, 0:hd1],
                          TD[:, kt * 512 + i * 128: kt * 512 + i * 128 + 128],
                          vh[p][0][:, kt * hd1:(kt + 1) * hd1]) for kt in kts]
                self.mm_group(_mk_group(items),
                              reads=[self.TDr[kt] for kt in kts] + [vh[p][1]],
                              writes=[preg])
                rr = self.SMALL[0][:, 9:10]
                self.vop(lambda eng, o=rr, a=pap[:, 0:1]:
                         eng.reciprocal(o, a),
                         reads=[preg], writes=[self.SMALL[1]])
                self.vop(lambda eng,
                         o=obuf[:, obase + qt * ostride + ohd * h:
                                obase + qt * ostride + ohd * h + ohd],
                         a=pap[:, 1:hd1], s=rr:
                         eng.tensor_scalar_mul(o, a, s),
                         reads=[preg, self.SMALL[1]], writes=[oreg])

        emit_head_prep(0)
        for h in range(16):
            for qb in range(4):
                emit_pb(h, qb, kts_of[qb])
                if qb == 1 and h + 1 < 16:
                    emit_head_prep(h + 1)
                emit_av(h, qb, kts_of[qb])

    def emit_router_gates(self, src, rsrc, w_name, E, plus_eps):
        wslot = self.WS[0]
        wbf = self.dram[w_name + "_bf"].tensor
        self.wdma(0, wslot[0][:, 0:8 * E],
                  AP(wbf, 0, [[E, 128], [128 * E, 8], [1, E]]),
                  reads=[wslot[1]], writes=[wslot[1]])
        SM, rSM = self.SMALL[0], self.SMALL[1]
        probs = self.SCR32[0][:, 0:16]
        rprob = self.SCR32[1]
        for nt in range(NT):
            pap, preg = self.ps(self.ps_rr())
            items = [(pap[:, 0:E], src[:, kc * 2048 + nt * 128: kc * 2048 + nt * 128 + 128],
                      wslot[0][:, kc * E:(kc + 1) * E]) for kc in range(8)]
            self.mm_group(_mk_group(items), reads=[rsrc, wslot[1]],
                          writes=[preg])
            nm = SM[:, 10:11]
            self.vop(lambda eng, o=nm, a=pap[:, 0:E]:
                     eng.reduce_max(o, a, axis=AX.X, negate=True),
                     reads=[preg], writes=[rSM])
            se_ = SM[:, 11:12]
            self.act(probs[:, 0:E], pap[:, 0:E], AF.Exp, bias=nm,
                     reads=[preg, rSM], writes=[rprob, rSM], accum_out=se_)
            rs = SM[:, 12:13]
            self.vop(lambda eng, o=rs, a=se_: eng.reciprocal(o, a),
                     reads=[rSM], writes=[rSM])
            self.vop(lambda eng, o=probs[:, 0:E], a=probs[:, 0:E], s=rs:
                     eng.tensor_scalar_mul(o, a, s),
                     reads=[rprob, rSM], writes=[rprob])
            m1 = SM[:, 13:14]
            self.vop(lambda eng, o=m1, a=probs[:, 0:E]:
                     eng.reduce_max(o, a, axis=AX.X),
                     reads=[rprob], writes=[rSM])
            ind1 = self.SCR32[0][:, 16:32]
            self.vop(lambda eng, o=ind1[:, 0:E], a=probs[:, 0:E], s=m1:
                     eng.tensor_single_scalar(o, a, s, ALU.is_ge),
                     reads=[rprob, rSM], writes=[rprob])
            pm = self.SCR32[0][:, 32:48]
            self.vop(lambda eng, o=pm[:, 0:E], a=ind1[:, 0:E], b=probs[:, 0:E]:
                     eng.scalar_tensor_tensor(o, a, -1e9, b, ALU.mult, ALU.add),
                     reads=[rprob], writes=[rprob])
            m2 = SM[:, 14:15]
            self.vop(lambda eng, o=m2, a=pm[:, 0:E]:
                     eng.reduce_max(o, a, axis=AX.X),
                     reads=[rprob], writes=[rSM])
            den = SM[:, 15:16]
            self.vop(lambda eng, o=den, a=m1, b=m2: eng.tensor_add(o, a, b),
                     reads=[rSM], writes=[rSM])
            if plus_eps:
                self.vop(lambda eng, o=den, a=den:
                         eng.tensor_scalar_add(o, a, 1e-9),
                         reads=[rSM], writes=[rSM])
            rw = SM[:, 10:11]
            self.vop(lambda eng, o=rw, a=den: eng.reciprocal(o, a),
                     reads=[rSM], writes=[rSM])
            gcol = self.GTS[0][:, nt * 16: nt * 16 + E]
            self.vop(lambda eng, o=gcol, a=probs[:, 0:E], s=m2:
                     eng.tensor_single_scalar(o, a, s, ALU.is_ge),
                     reads=[rprob, rSM], writes=[self.GTS[1]])
            self.vop(lambda eng, o=gcol, a=gcol, b=probs[:, 0:E]:
                     eng.tensor_mul(o, a, b),
                     reads=[rprob, self.GTS[1]], writes=[self.GTS[1]])
            self.vop(lambda eng, o=gcol, a=gcol, s=rw:
                     eng.tensor_scalar_mul(o, a, s),
                     reads=[rSM, self.GTS[1]], writes=[self.GTS[1]])

    def emit_expert(self, src, rsrc, wg_t, wu_t, wd_t, base_gu, base_d,
                    Fdim, gate_col, silu_on_g=True):
        TC, rTC = self.TC
        nF = Fdim // 128
        win = 4 if nF <= 8 else 2
        for w0 in range(0, 4, win):
            for fc in range(nF):
                sg = self.WS[(2 * fc) % 4]
                su = self.WS[(2 * fc + 1) % 4]
                self.wdma(0, sg[0][:, 0:1024],
                          AP(wg_t, base_gu + fc * 128,
                             [[Fdim, 128], [128 * Fdim, 8], [1, 128]]),
                          reads=[sg[1]], writes=[sg[1]])
                self.wdma(1, su[0][:, 0:1024],
                          AP(wu_t, base_gu + fc * 128,
                             [[Fdim, 128], [128 * Fdim, 8], [1, 128]]),
                          reads=[su[1]], writes=[su[1]])
                for tw in range(win):
                    tt4 = w0 + tw
                    pg, prg = self.ps(self.ps_rr())
                    pu, pru = self.ps(self.ps_rr())
                    gi = [(pg, sg[0][:, kc * 128:(kc + 1) * 128],
                           src[:, kc * 2048 + tt4 * 512: kc * 2048 + tt4 * 512 + 512])
                          for kc in range(8)]
                    self.mm_group(_mk_group(gi), reads=[sg[1], rsrc],
                                  writes=[prg])
                    ui = [(pu, su[0][:, kc * 128:(kc + 1) * 128],
                           src[:, kc * 2048 + tt4 * 512: kc * 2048 + tt4 * 512 + 512])
                          for kc in range(8)]
                    self.mm_group(_mk_group(ui), reads=[su[1], rsrc],
                                  writes=[pru])
                    stile = self.SCRB[0][:, (tw % 2) * 512:(tw % 2) * 512 + 512]
                    silu_src = pg if silu_on_g else pu
                    silu_reg = prg if silu_on_g else pru
                    other = pu if silu_on_g else pg
                    other_reg = pru if silu_on_g else prg
                    self.act(stile, silu_src, AF.Silu,
                             reads=[silu_reg], writes=[self.SCRB[1]])
                    self.vop(lambda eng,
                             o=TC[:, fc * (win * 512) + tw * 512:
                                  fc * (win * 512) + tw * 512 + 512],
                             a=other, b=stile:
                             eng.tensor_mul(o, a, b),
                             reads=[other_reg, self.SCRB[1]], writes=[rTC])
            for tb in range(win):
                tt4 = w0 + tb
                for ds in range(2):
                    psl = [self.ps_rr() for _ in range(4)]
                    for fc2 in range((nF + 1) // 2):
                        slot = self.WS[fc2 % 4]
                        nld = min(2, nF - fc2 * 2)
                        self.wdma(fc2, slot[0][:, 0:512 * nld],
                                  AP(wd_t, base_d + ds * 512 + fc2 * 256 * 1024,
                                     [[1024, 128], [128 * 1024, nld], [1, 512]]),
                                  reads=[slot[1]], writes=[slot[1]])
                        for fh in range(nld):
                            fc = fc2 * 2 + fh
                            for i in range(4):
                                pap, preg = self.ps(psl[i])
                                its = [(pap,
                                        TC[:, fc * (win * 512) + tb * 512 + i * 128:
                                           fc * (win * 512) + tb * 512 + i * 128 + 128],
                                        slot[0][:, fh * 512:fh * 512 + 512],
                                        fc == 0, fc == nF - 1)]
                                self.mm_group(its, reads=[slot[1], rTC],
                                              writes=[preg])
                    for i in range(4):
                        nt = tt4 * 4 + i
                        pap, preg = self.ps(psl[i])
                        xs = self.X[0][:, nt * 1024 + ds * 512:
                                       nt * 1024 + ds * 512 + 512]
                        if gate_col is None:
                            self.vop(lambda eng, o=xs, a=pap, b=xs:
                                     eng.tensor_add(o, a, b),
                                     reads=[preg, self.X[1]],
                                     writes=[self.X[1]])
                        else:
                            gc = self.GTS[0][:, nt * 16 + gate_col:
                                             nt * 16 + gate_col + 1]
                            self.vop(lambda eng, o=xs, a=pap, s=gc, b=xs:
                                     eng.scalar_tensor_tensor(
                                         o, a, s, b, ALU.mult, ALU.add),
                                     reads=[preg, self.X[1], self.GTS[1]],
                                     writes=[self.X[1]])

    def emit_ln_rms(self, dst, dst_reg, g1cm):
        X, rX = self.X[0], self.X[1]
        dump, rdump = self.SCR32[0], self.SCR32[1]
        xnb = self.SCRB[0][:, 0:1024]
        hb = self.SCRB[0][:, 1024:2048]
        rxnb = self.SCRB[1]
        SM, rSM = self.SMALL[0], self.SMALL[1]
        for i in range(NT):
            xi = X[:, i * 1024:(i + 1) * 1024]
            self.act(dump[:, 0:1024], xi, AF.Copy,
                     reads=[rX], writes=[rdump, rSM], accum_out=SM[:, 0:1])
            self.act(dump[:, 0:1024], xi, AF.Square,
                     reads=[rX], writes=[rdump, rSM], accum_out=SM[:, 1:2])
            mean = self.MEAN[0][:, i:i + 1]
            rstd = self.RSTD[0][:, i:i + 1]
            t0, t1 = SM[:, 2:3], SM[:, 3:4]
            self.vop(lambda eng, o=mean, a=SM[:, 0:1]:
                     eng.tensor_scalar_mul(o, a, 1.0 / D),
                     reads=[rSM], writes=[self.MEAN[1]])
            self.vop(lambda eng, o=t0, a=SM[:, 1:2]:
                     eng.tensor_scalar_mul(o, a, 1.0 / D),
                     reads=[rSM], writes=[rSM])
            self.vop(lambda eng, o=t1, m=mean: eng.tensor_mul(o, m, m),
                     reads=[self.MEAN[1]], writes=[rSM])
            self.vop(lambda eng, o=t0, a=t0, b=t1: eng.tensor_sub(o, a, b),
                     reads=[rSM], writes=[rSM])
            self.act(t1, t0, AF.Sqrt, bias=self.constv(1e-5),
                     reads=[rSM, self.CONSTV[1]], writes=[rSM])
            self.vop(lambda eng, o=rstd, a=t1: eng.reciprocal(o, a),
                     reads=[rSM], writes=[self.RSTD[1]])
            self.vop(lambda eng, o=xnb, a=xi, m=mean, r=rstd:
                     eng.tensor_scalar(o, a, m, r, ALU.subtract, ALU.mult),
                     reads=[rX, self.MEAN[1], self.RSTD[1]],
                     writes=[rxnb])
            self.act(dump[:, 0:1024], xnb, AF.Square,
                     reads=[rxnb], writes=[rdump, rSM], accum_out=SM[:, 4:5])
            self.vop(lambda eng, o=SM[:, 4:5], a=SM[:, 4:5]:
                     eng.tensor_scalar_mul(o, a, 1.0 / D),
                     reads=[rSM], writes=[rSM])
            self.act(SM[:, 5:6], SM[:, 4:5], AF.Sqrt, bias=self.constv(1e-5),
                     reads=[rSM, self.CONSTV[1]], writes=[rSM])
            self.vop(lambda eng, o=SM[:, 4:5], a=SM[:, 5:6]:
                     eng.reciprocal(o, a),
                     reads=[rSM], writes=[rSM])
            self.vop(lambda eng, o=hb, a=xnb, s=SM[:, 4:5]:
                     eng.tensor_scalar_mul(o, a, s),
                     reads=[rxnb, rSM], writes=[rxnb])
            pap, preg = self.ps(self.ps_rr())
            pbf = pap.bitcast(BF16)
            items = [(pbf[:, c * 128:(c + 1) * 128],
                      hb[:, c * 128:(c + 1) * 128]) for c in range(8)]
            self.transpose_group(items, reads=[rxnb], writes=[preg])
            for c in range(8):
                self.vop(lambda eng,
                         o=dst[:, c * 2048 + i * 128: c * 2048 + (i + 1) * 128],
                         a=pbf[:, c * 128:(c + 1) * 128], g=g1cm[:, c:c + 1]:
                         eng.tensor_scalar_mul(o, a, g),
                         reads=[preg, rSM], writes=[dst_reg])

    def emit_latent_qk(self, dst, dst_reg, w_name):
        TC, rTC = self.TC
        wbf = self.dram[w_name + "_bf"].tensor
        cos_t = self.dram["c_cos"].tensor
        sin_t = self.dram["c_sin"].tensor
        stage = self.SCRB[0][:, 0:1024]
        rstage = self.SCRB[1]
        cosb = self.SCR32[0][:, 0:256]
        sinb = self.SCR32[0][:, 256:512]
        tmpa = self.SCR32[0][:, 512:768]
        tmpb = self.SCR32[0][:, 768:1024]
        r32 = self.SCR32[1]
        for nt in range(NT):
            self.p.dma("sync", cosb,
                       AP(cos_t, nt * 128 * 16, [[16, 128], [0, 16], [1, 16]]),
                       reads=[r32], writes=[r32])
            self.p.dma("sync", sinb,
                       AP(sin_t, nt * 128 * 16, [[16, 128], [0, 16], [1, 16]]),
                       reads=[r32], writes=[r32])
            for vs in range(2):
                base = ((nt * 2 + vs) % 2) * 2
                for half in range(2):
                    slot = self.WS[base + half]
                    self.wdma(half, slot[0][:, 0:1024],
                              AP(wbf, vs * 512 + half * 2 * 128 * 1024,
                                 [[1024, 128], [128 * 1024, 2], [1, 512]]),
                              reads=[slot[1]], writes=[slot[1]])
                pap, preg = self.ps(self.ps_rr())
                its = [(pap,
                        TC[:, kc * 2048 + nt * 128: kc * 2048 + nt * 128 + 128],
                        self.WS[base + kc // 2][0][:, (kc % 2) * 512:(kc % 2) * 512 + 512])
                       for kc in range(4)]
                self.mm_group(_mk_group(its),
                              reads=[self.WS[base][1], self.WS[base + 1][1], rTC],
                              writes=[preg])
                self.act(stage[:, vs * 512:(vs + 1) * 512], pap, AF.Copy,
                         reads=[preg], writes=[rstage])
            st = stage
            part = st.ap[0][0]
            def hap(base, off):
                return AP(st.tensor, st.offset + off, [[part, 128], [64, 16], [1, 16]])
            t1 = hap(st, 0)
            t2 = hap(st, 32)
            ca = AP(cosb.tensor, cosb.offset, [[cosb.ap[0][0], 128], [0, 16], [1, 16]])
            sa = AP(sinb.tensor, sinb.offset, [[sinb.ap[0][0], 128], [0, 16], [1, 16]])
            ta = AP(tmpa.tensor, tmpa.offset, [[tmpa.ap[0][0], 128], [16, 16], [1, 16]])
            tb = AP(tmpb.tensor, tmpb.offset, [[tmpb.ap[0][0], 128], [16, 16], [1, 16]])
            self.vop(lambda eng: eng.tensor_mul(ta, t1, sa),
                     reads=[rstage, r32], writes=[r32])
            self.vop(lambda eng: eng.tensor_mul(tb, t2, ca),
                     reads=[rstage, r32], writes=[r32])
            self.vop(lambda eng: eng.tensor_add(ta, ta, tb),
                     reads=[r32], writes=[r32])
            self.vop(lambda eng: eng.tensor_mul(tb, t2, sa),
                     reads=[rstage, r32], writes=[r32])
            self.vop(lambda eng: eng.tensor_copy(t2, ta),
                     reads=[r32], writes=[rstage])
            self.vop(lambda eng: eng.tensor_mul(ta, t1, ca),
                     reads=[rstage, r32], writes=[r32])
            self.vop(lambda eng: eng.tensor_sub(t1, ta, tb),
                     reads=[r32], writes=[rstage])
            pap, preg = self.ps(self.ps_rr())
            pbf = pap.bitcast(BF16)
            items = [(pbf[:, c * 128:(c + 1) * 128],
                      st[:, c * 128:(c + 1) * 128]) for c in range(8)]
            self.transpose_group(items, reads=[rstage], writes=[preg])
            self.vop(lambda eng, o=_chunks_out_h(dst, nt), a=pbf[:, 0:1024]:
                     eng.tensor_copy(o, a),
                     reads=[preg], writes=[dst_reg])

    def emit_l0f_up(self, g2cm):
        TC, rTC = self.TC
        TD = self.TD[0]
        TA, rTA = self.TA
        wbf = self.dram["l0f_wup_bf"].tensor
        for qtg in range(4):
            for i in range(4):
                qt = qtg * 4 + i
                pap, preg = self.ps(self.ps_rr())
                pbf = pap.bitcast(BF16)
                items = [(pbf[:, c * 128:(c + 1) * 128],
                          TC[:, 8192 + qt * 512 + c * 128: 8192 + qt * 512 + (c + 1) * 128])
                         for c in range(4)]
                self.transpose_group(items, reads=[rTC], writes=[preg])
                self.vop(lambda eng, o=TD[:, i * 2048:i * 2048 + 512],
                         a=pbf[:, 0:512]:
                         eng.tensor_copy(o, a),
                         reads=[preg],
                         writes=self.td_regs(i * 2048, i * 2048 + 512))
            for ds in range(2):
                psl = [self.ps_rr() for _ in range(4)]
                for dc2 in range(2):
                    slot = self.WS[dc2 % 4]
                    self.wdma(dc2, slot[0][:, 0:1024],
                              AP(wbf, ds * 512 + dc2 * 256 * 1024,
                                 [[1024, 128], [128 * 1024, 2], [1, 512]]),
                              reads=[slot[1]], writes=[slot[1]])
                    for dh in range(2):
                        dc = dc2 * 2 + dh
                        for i in range(4):
                            pap, preg = self.ps(psl[i])
                            its = [(pap,
                                    TD[:, i * 2048 + dc * 128: i * 2048 + (dc + 1) * 128],
                                    slot[0][:, dh * 512:dh * 512 + 512],
                                    dc == 0, dc == 3)]
                            self.mm_group(
                                its,
                                reads=[slot[1]] + list(self.td_regs(
                                    i * 2048 + dc * 128,
                                    i * 2048 + (dc + 1) * 128)),
                                writes=[preg])
                for i in range(4):
                    qt = qtg * 4 + i
                    pap, preg = self.ps(psl[i])
                    xs = self.X[0][:, qt * 1024 + ds * 512: qt * 1024 + ds * 512 + 512]
                    xnr = self.SCR32[0][:, 0:512]
                    mean = self.MEAN[0][:, qt:qt + 1]
                    rstd = self.RSTD[0][:, qt:qt + 1]
                    self.vop(lambda eng, o=xnr, a=xs, m=mean, r=rstd:
                             eng.tensor_scalar(o, a, m, r, ALU.subtract,
                                               ALU.mult),
                             reads=[self.X[1], self.MEAN[1], self.RSTD[1]],
                             writes=[self.SCR32[1]])
                    xfs = TA[:, qt * 1024 + ds * 512: qt * 1024 + ds * 512 + 512]
                    self.vop(lambda eng, o=xfs, a=pap, b=xnr:
                             eng.tensor_add(o, a, b),
                             reads=[preg, self.SCR32[1]], writes=[rTA])
                    self.vop(lambda eng, o=xs, a=pap, b=xs:
                             eng.tensor_add(o, a, b),
                             reads=[preg, self.X[1]], writes=[self.X[1]])
                    self.vop(lambda eng, o=xs, a=xnr, b=xs:
                             eng.tensor_add(o, a, b),
                             reads=[self.SCR32[1], self.X[1]],
                             writes=[self.X[1]])
        dump, rdump = self.SCR32[0], self.SCR32[1]
        SM, rSM = self.SMALL[0], self.SMALL[1]
        tb_ = self.SCRB[0][:, 0:1024]
        for i in range(NT):
            xfi = TA[:, i * 1024:(i + 1) * 1024]
            self.act(dump[:, 0:1024], xfi, AF.Square,
                     reads=[rTA], writes=[rdump, rSM], accum_out=SM[:, 0:1])
            self.vop(lambda eng, o=SM[:, 0:1], a=SM[:, 0:1]:
                     eng.tensor_scalar_mul(o, a, 1.0 / D),
                     reads=[rSM], writes=[rSM])
            self.act(SM[:, 1:2], SM[:, 0:1], AF.Sqrt, bias=self.constv(1e-5),
                     reads=[rSM, self.CONSTV[1]], writes=[rSM])
            self.vop(lambda eng, o=SM[:, 0:1], a=SM[:, 1:2]:
                     eng.reciprocal(o, a),
                     reads=[rSM], writes=[rSM])
            self.vop(lambda eng, o=tb_, a=xfi, s=SM[:, 0:1]:
                     eng.tensor_scalar_mul(o, a, s),
                     reads=[rTA, rSM], writes=[self.SCRB[1]])
            pap, preg = self.ps(self.ps_rr())
            pbf = pap.bitcast(BF16)
            items = [(pbf[:, c * 128:(c + 1) * 128],
                      tb_[:, c * 128:(c + 1) * 128]) for c in range(8)]
            self.transpose_group(items, reads=[self.SCRB[1]], writes=[preg])
            for c in range(8):
                self.vop(lambda eng,
                         o=self.TB[0][:, c * 2048 + i * 128: c * 2048 + (i + 1) * 128],
                         a=pbf[:, c * 128:(c + 1) * 128], g=g2cm[:, c:c + 1]:
                         eng.tensor_scalar_mul(o, a, g),
                         reads=[preg, rSM], writes=[self.TB[1]])


def _chunks_out_h(dst, i):
    a = dst[:, :]
    return AP(a.tensor, i * 128, [[a.ap[0][0], 128], [2048, 8], [1, 128]])



class FullBuilder(Builder, _AttnMixin):
    def __init__(self, parts):
        Builder.__init__(self, parts)
        self.debug = False
        self._dbg_outs = {}
        self.rVS = Region("vs")
        self.rGARR = Region("garr")
        self._brow_pos = 0
        self._garr_done = False

    def _dbg(self, tag, ap, region, shape=None):
        if not self.debug or tag in self._dbg_outs:
            return
        shape = shape or list(ap.shape)
        d = self.nc.dram_tensor("dbg_" + tag, shape, ap.dtype,
                                kind="ExternalOutput").ap()
        self._dbg_outs[tag] = d
        self.p.dma("gpsimd", d[:, :] if len(shape) == 2 else d[:],
                   ap, reads=[region], writes=[])

    def emit_garr(self):
        if self._garr_done:
            return
        self._garr_done = True
        p = self.p
        re33 = self.SMALL[0][0:33, 32:48]
        rSM = self.SMALL[1]
        p.dma("sync", self.SMALL[0][0:32, 32:48], self.dram["l0a_rel"][:, :],
              reads=[rSM], writes=[rSM])
        self.vop(lambda eng, o=self.SMALL[0][32:33, 32:48]:
                 eng.memset(o, 1.0), reads=[rSM], writes=[rSM])
        oh_t = self.dram["c_oh2"].tensor
        garr_t = self.dram["garr"].tensor
        stage = self.SCRB[0][0:16, 0:512]
        pos = 0
        while pos < NGARR:
            wdt = min(512, NGARR - pos)
            rhs32 = self.SCR32[0][0:33, 0:wdt]
            p.dma("sync", rhs32, AP(oh_t, pos, [[NGARR, 33], [1, wdt]]),
                  reads=[self.SCR32[1]], writes=[self.SCR32[1]])
            pap, preg = self.ps(self.ps_rr())
            self.mm_group([(pap[0:16, 0:wdt], re33, rhs32, True, True)],
                          reads=[rSM, self.SCR32[1]], writes=[preg])
            self.act(stage[0:16, 0:wdt], pap[0:16, 0:wdt], AF.Copy,
                     reads=[preg], writes=[self.SCRB[1]])
            p.dma("sync", AP(garr_t, pos, [[NGARR, 16], [1, wdt]]),
                  stage[0:16, 0:wdt],
                  reads=[self.SCRB[1]], writes=[self.rGARR])
            pos += wdt

    def part_l0a(self):
        self.emit_garr()
        cfg = AttnCfg("l0a", "l0a_wq", "l0a_wk", "l0a_wv", "l0a_wo",
                      bq="l0a_bq", bk="l0a_bk", bv="l0a_bv", bo="l0a_bo",
                      scale=1.0, causal=True, tbl="garr")
        self.emit_full_attn(cfg)

    def part_l1a(self):
        cfg = AttnCfg("l1a", "l1a_wqkv", "l1a_wqkv", "l1a_wqkv", "l1a_wout",
                      wcols=3072, bo="l1a_bout", scale=64 ** -0.5,
                      qcol0=0, kcol0=1024, vcol0=2048)
        cfg.inner_gamma = "l1a_gamma"
        cfg.inner_beta = "l1a_beta"
        self.emit_full_attn(cfg, inner_ln=True)

    def part_l1b(self):
        cfg = AttnCfg("l1b", "l1b_wq", "l1b_wk", "l1b_wv", "l1b_wo",
                      bq="l1b_bq", bk="l1b_bk", bv="l1b_bv", bo="l1b_bo",
                      scale=64 ** -0.5, colmask=True)
        self.emit_full_attn(cfg)

    def part_l1f(self):
        self.emit_ln(self.TA[0], self.TA[1])
        wgs = self.dram["l1f_wg_s_bf"].tensor
        wus = self.dram["l1f_wu_s_bf"].tensor
        wds = self.dram["l1f_wd_s_bf"].tensor
        for n in range(2):
            self.emit_expert(self.TA[0], self.TA[1], wgs, wus, wds,
                             n * 1024 * 512, n * 512 * 1024, 512, None)
        self.emit_router_gates(self.TA[0], self.TA[1], "l1f_router", 16, True)
        wgr = self.dram["l1f_wg_r_bf"].tensor
        wur = self.dram["l1f_wu_r_bf"].tensor
        wdr = self.dram["l1f_wd_r_bf"].tensor
        for e in range(16):
            self.emit_expert(self.TA[0], self.TA[1], wgr, wur, wdr,
                             e * 1024 * 256, e * 256 * 1024, 256, e)

    def part_l0f(self):
        g1cm, g2cm = self.load_cm_pair("l0f_g1", "l0f_g2")
        self.emit_ln_rms(self.TA[0], self.TA[1], g1cm)
        self.emit_qT_proj(self.TC, "l0f_wdown", 0, None, 512, ndout=4)
        cfgv = AttnCfg("l0fv", None, None, "l0f_wv", None, wcols=512)
        self._battn = {}
        self.emit_v_to_vs(cfgv, src=self.TC, hd=32, vtot=512, nkc=4)
        self.emit_latent_qk(self.TA[0], self.TA[1], "l0f_wq")
        self.emit_latent_qk(self.TB[0], self.TB[1], "l0f_wk")
        cfgc = AttnCfg("l0fc", None, None, None, None, scale=32 ** -0.5,
                       causal=True, tbl="gcaus")
        self.emit_attn_core(cfgc, (self.TA, self.TA[1]), (self.TB, self.TB[1]),
                            self.TC[0], self.TC[1], 8192, 512, 32)
        self.emit_l0f_up(g2cm)
        self.emit_router_gates(self.TB[0], self.TB[1], "l0f_router", 8, False)
        w1 = self.dram["l0f_w1_bf"].tensor
        w3 = self.dram["l0f_w3_bf"].tensor
        w2 = self.dram["l0f_w2_bf"].tensor
        for e in range(8):
            self.emit_expert(self.TB[0], self.TB[1], w1, w3, w2,
                             e * 1024 * 2048, e * 2048 * 1024, 2048, e)

    def part_l0b(self):
        self.brow_reset()
        self._battn = {}
        self.emit_ln(self.TA[0], self.TA[1])
        self._battn["v"] = self.load_row_bf("l0b_bv", 1024, slot=0)
        self.emit_qT_proj(self.TC, "l0b_wv", 0, self._battn["v"], 1024)
        self._battn["q"] = self.load_row_bf("l0b_bq", 1024, slot=0)
        self._battn["k"] = self.load_row_bf("l0b_bk", 1024, slot=1)
        self.p.dma("sync", self.B64[0][:, :], self.dram["c_b64"][:, :],
                   reads=[self.B64[1]], writes=[self.B64[1]])
        self.p.dma("sync", self.E64[0][:, :], self.dram["c_e64"][:, :],
                   reads=[self.E64[1]], writes=[self.E64[1]])
        TD = self.TD[0]
        self.p.dma("sync", TD[0:16, 2048:4096], self.dram["c_m16"][:, :],
                   reads=list(self.td_regs(2048, 4096)),
                   writes=list(self.td_regs(2048, 4096)))
        TA, rTA = self.TA
        wq_t = self.dram["l0b_wq_bf"].tensor
        wk_t = self.dram["l0b_wk_bf"].tensor
        bq, bk = self._battn["q"], self._battn["k"]
        sc_regs = self.pr[0:4]
        for dc in range(8):
            sq_ = self.WS[(2 * dc) % 4]
            sk_ = self.WS[(2 * dc + 1) % 4]
            self.wdma(0, sq_[0][:, 0:1024],
                      AP(wq_t, dc * 128, [[1024, 128], [128 * 1024, 8], [1, 128]]),
                      reads=[sq_[1]], writes=[sq_[1]])
            self.wdma(1, sk_[0][:, 0:1024],
                      AP(wk_t, dc * 128, [[1024, 128], [128 * 1024, 8], [1, 128]]),
                      reads=[sk_[1]], writes=[sk_[1]])
            for tt4 in range(4):
                ts_ = slice(tt4 * 512, tt4 * 512 + 512)
                qi = self.ps_rr(lo=4, hi=8)
                ki = self.ps_rr(lo=4, hi=8)
                qp, qreg = self.ps(qi)
                kp, kreg = self.ps(ki)
                qits = [(qp, sq_[0][:, kc * 128:(kc + 1) * 128],
                         TA[:, kc * 2048 + tt4 * 512: kc * 2048 + tt4 * 512 + 512])
                        for kc in range(8)]
                qits.append((qp, bq[0][0:1, dc * 128:dc * 128 + 128],
                             self.ONE[0][0:1, 0:512]))
                self.mm_group(_mk_group(qits),
                              reads=[sq_[1], rTA, bq[1], self.ONE[1]],
                              writes=[qreg])
                kits = [(kp, sk_[0][:, kc * 128:(kc + 1) * 128],
                         TA[:, kc * 2048 + tt4 * 512: kc * 2048 + tt4 * 512 + 512])
                        for kc in range(8)]
                kits.append((kp, bk[0][0:1, dc * 128:dc * 128 + 128],
                             self.ONE[0][0:1, 0:512]))
                self.mm_group(_mk_group(kits),
                              reads=[sk_[1], rTA, bk[1], self.ONE[1]],
                              writes=[kreg])
                kev = self.SCR32[0][:, 0:512]
                self.act(kev, kp, AF.Copy, reads=[kreg],
                         writes=[self.SCR32[1]])
                prod = self.SCR32[0][:, 512:1024]
                self.vop(lambda eng, o=prod, a=qp, b=kev:
                         eng.tensor_mul(o, a, b),
                         reads=[qreg, self.SCR32[1]], writes=[self.SCR32[1]])
                self.mm_group([(self.PP[0:16, ts_],
                                self.B64[0][:, dc * 16:dc * 16 + 16],
                                prod, dc == 0, dc == 7)],
                              reads=[self.B64[1], self.SCR32[1]],
                              writes=[sc_regs[tt4]])
        SM, rSM = self.SMALL[0], self.SMALL[1]
        for seg in range(4):
            part = self.PP[:, :].ap[0][0]
            mneg = SM[0:16, 16:20]
            pin = AP(self.PP[:, :].tensor, seg * 512,
                     [[part, 16], [1, 4], [4, 128]])
            self.vop(lambda eng, o=mneg, a=pin:
                     eng.reduce_max(o, a, axis=AX.X, negate=True),
                     reads=[sc_regs[seg]], writes=[rSM])
            self.vop(lambda eng, o=mneg, a=mneg:
                     eng.tensor_scalar_mul(o, a, 0.125),
                     reads=[rSM], writes=[rSM])
            for rp in range(4):
                pslice = AP(self.PP[:, :].tensor, seg * 512 + rp,
                            [[part, 16], [4, 128]])
                tdp = TD[:, :].ap[0][0]
                pout = AP(TD[:, :].tensor, seg * 512 + rp, [[tdp, 16], [4, 128]])
                sums = SM[0:16, 20:21]
                self.act(pout, pslice, AF.Exp, bias=mneg[0:16, rp:rp + 1],
                         scale=self.constv(0.125)[0:16, 0:1],
                         reads=[sc_regs[seg], rSM, self.CONSTV[1]],
                         writes=[self.TDr[seg], rSM], accum_out=sums)
                rec = SM[0:16, 21:22]
                self.vop(lambda eng, o=rec, a=sums: eng.reciprocal(o, a),
                         reads=[rSM], writes=[rSM])
                self.vop(lambda eng, o=pout, a=pout, s=rec:
                         eng.tensor_scalar_mul(o, a, s),
                         reads=[rSM, self.TDr[seg]], writes=[self.TDr[seg]])
        self.vop(lambda eng, o=TD[0:16, 0:2048], a=TD[0:16, 0:2048],
                 b=TD[0:16, 2048:4096]:
                 eng.tensor_mul(o, a, b),
                 reads=list(self.td_regs(0, 4096)),
                 writes=list(self.td_regs(0, 2048)))
        TB, rTB = self.TB
        TC, rTC = self.TC
        for dc in range(8):
            for tt4 in range(4):
                pap, preg = self.ps(self.ps_rr())
                its = [(pap, self.E64[0][0:16, dc * 128:(dc + 1) * 128],
                        TD[0:16, tt4 * 512:(tt4 + 1) * 512], True, True)]
                self.mm_group(its, reads=[self.E64[1], self.TDr[tt4]],
                              writes=[preg])
                self.vop(lambda eng,
                         o=TB[:, dc * 2048 + tt4 * 512: dc * 2048 + tt4 * 512 + 512],
                         a=pap,
                         b=TC[:, dc * 2048 + tt4 * 512: dc * 2048 + tt4 * 512 + 512]:
                         eng.tensor_mul(o, a, b),
                         reads=[preg, rTC], writes=[rTB])
        self._battn["o"] = self.load_row_bf("l0b_bo", 1024, slot=1)
        self.emit_out_proj("l0b_wo", self._battn["o"], self.TB, rTB, 1024,
                           src_is_T=True)

    def load_consts(self):
        for cname, buf in (("c_ident", self.IDE),
                           ("c_J", self.JX), ("c_ones", self.ONE)):
            self.p.dma("sync", buf[0][:, :], self.dram[cname][:, :],
                       reads=[buf[1]], writes=[buf[1]])

    def build(self):
        self.load_consts()
        self.convert_weights()
        for s in range(SEQ_PER_CORE):
            self.load_x(s)
            for prt in self.parts:
                self._conv_reg = self.conv_reg.get(prt)
                getattr(self, "part_" + prt)()
                self._conv_reg = None
            self.store_x(s)
        self.p.finalize(self.nc, self.es)
        self.es.close()
        return self.nc



MODULE_ORDER = ["l0a", "l0f", "l0b", "l0f", "l1a", "l1f", "l1b", "l1f"]
FUSED = True


def _in_maps(parts, x, w, hc):
    need = set()
    for prt in parts:
        need.update(PART_WEIGHTS[prt])
    maps = []
    for c in range(N_CORES):
        m = {"x_in": np.ascontiguousarray(x[2 * c:2 * c + 2])}
        for n in sorted(need):
            m[n] = w[n] if n not in ROW_WEIGHTS else w[n]
        for n in CONST_NAMES:
            m[n] = hc[n]
        maps.append(m)
    return maps


def _run_parts(parts, x, w, hc, core_ids=None):
    b = FullBuilder(parts)
    nc = b.build()
    cids = core_ids or list(range(N_CORES))
    maps = _in_maps(parts, x, w, hc)[: len(cids)]
    res = run_bass_kernel_spmd(nc, maps, core_ids=cids)
    return np.concatenate([r["x_out"] for r in res.results], axis=0)


def kernel(**inputs):
    w = _prep_weights(inputs)
    hc = _host_constants()
    x = np.ascontiguousarray(np.asarray(inputs["L0_attn_a_x"], np.float32))
    if FUSED:
        x = _run_parts(MODULE_ORDER, x, w, hc)
    else:
        for prt in MODULE_ORDER:
            x = _run_parts([prt], x, w, hc)
    return x
